# version 02
# baseline (speedup 1.0000x reference)
from concourse.bass_utils import run_bass_kernel_spmd

"""Megakernel builder: GLA mixture-of-memories attention + expert-choice MoE FFN.

One raw-Bass SPMD program (identical across cores). Core c's in_map carries batch
row c (attention) and expert c's FFN weights. Cross-core via AllGather. NOTES.md
has the full design. v1: conservative (serializing) semaphore discipline.
"""
import numpy as np
import concourse.bass as bass
import concourse.mybir as mybir
from concourse.library_config import mlp as MLP_LIB
from concourse.library_overlay import lower_extended_insts

F32 = mybir.dt.float32
BF16 = mybir.dt.bfloat16
I16 = mybir.dt.int16
I32 = mybir.dt.int32
U8 = mybir.dt.uint8
AF = mybir.ActivationFunctionType
OP = mybir.AluOpType
AX = mybir.AxisListType

S, D = 2048, 1024
NMEM, CAP, H = 4, 1024, 4
DK, DV, HDK, HDV, LOW = 512, 1024, 128, 256, 16
NE, FF, KEXP, NTOK = 8, 4096, 2048, 16384
NT = S // 128
GATE_NORM, EPS = 16.0, 1e-5
QSCALE = float(HDK) ** -0.5
TBF = 512
NTB = KEXP // TBF
NBIS = 44

CC = dict(zero=0, one=1, eps=2, lnqs=3, inv16=4, neg16=5, negone=6, half=7)
NCC = 8


def host_constants():
    import ml_dtypes
    cc = np.zeros((128, NCC), np.float32)
    cc[:, CC["one"]] = 1.0
    cc[:, CC["eps"]] = EPS
    cc[:, CC["lnqs"]] = np.log(QSCALE)
    cc[:, CC["inv16"]] = 1.0 / GATE_NORM
    cc[:, CC["neg16"]] = -1.0 / GATE_NORM
    cc[:, CC["negone"]] = -1.0
    cc[:, CC["half"]] = 0.5
    p = np.arange(128)
    return {
        "cc": cc,
        "ident_f": np.eye(128, dtype=np.float32),
        "ident_b": np.eye(128, dtype=ml_dtypes.bfloat16),
        "triu_inc": np.triu(np.ones((128, 128), np.float32)).astype(ml_dtypes.bfloat16),
        "upper_strict": np.triu(np.ones((128, 128), np.float32), 1),
        "ones_mat": np.ones((128, 128), np.float32),
        "iota4_bc": np.tile(np.arange(4, dtype=np.float32), (128, 1)),
        "iota_tg": (128.0 * p[:, None] + np.arange(128)[None, :]).astype(np.float32),
        "iota_cm": (p[:, None] + 128.0 * np.arange(128)[None, :]).astype(np.float32),
        "r128s": (128.0 * (p // 16) + (p % 16)).astype(np.float32).reshape(128, 1),
        "iota_rows": ((np.arange(128)[None, :] // 16) * 2048.0
                      + (np.arange(128)[None, :] % 16) * 128.0
                      + p[:, None] * 1.0).astype(np.float32),
        "ones_w": np.ones((128, S), ml_dtypes.bfloat16),
    }


def host_inputs(inputs):
    consts = host_constants()
    x = np.asarray(inputs["L0_attn_x"], np.float32)
    bgk = np.asarray(inputs["L0_attn_b_gk"], np.float32)
    bgk_t = np.zeros((128, NMEM * H), np.float32)
    for m in range(NMEM):
        for h in range(H):
            bgk_t[:, m * H + h] = -bgk[m, h * 128:(h + 1) * 128]
    bgks = np.asarray(inputs["L0_attn_b_gks"], np.float32)
    bgks_t = np.zeros((128, H), np.float32)
    for h in range(H):
        bgks_t[:, h] = -bgks[h * 128:(h + 1) * 128]
    normw = np.asarray(inputs["L0_attn_norm_w"], np.float32)
    b1 = np.asarray(inputs["L0_ffn_b1"], np.float32)
    b2 = np.asarray(inputs["L0_ffn_b2"], np.float32)
    W1 = np.asarray(inputs["L0_ffn_W1"], np.float32)
    W2 = np.asarray(inputs["L0_ffn_W2"], np.float32)
    base = {
        "w_gate": np.asarray(inputs["L0_attn_w_gate"], np.float32),
        "w_q": np.asarray(inputs["L0_attn_w_q"], np.float32),
        "w_k": np.asarray(inputs["L0_attn_w_k"], np.float32),
        "w_v": np.asarray(inputs["L0_attn_w_v"], np.float32),
        "w_gk1": np.asarray(inputs["L0_attn_w_gk1"], np.float32),
        "w_gk2": np.asarray(inputs["L0_attn_w_gk2"], np.float32),
        "b_gk_t": bgk_t,
        "w_ks": np.asarray(inputs["L0_attn_w_ks"], np.float32),
        "w_vs": np.asarray(inputs["L0_attn_w_vs"], np.float32),
        "w_gk1s": np.asarray(inputs["L0_attn_w_gk1s"], np.float32),
        "w_gk2s": np.asarray(inputs["L0_attn_w_gk2s"], np.float32),
        "b_gks_t": bgks_t,
        "w_g": np.asarray(inputs["L0_attn_w_g"], np.float32),
        "w_o": np.asarray(inputs["L0_attn_w_o"], np.float32),
        "normw_bc": np.tile(np.tile(normw, H)[None, :], (128, 1)).astype(np.float32),
        "ffn_Ws": np.asarray(inputs["L0_ffn_Ws"], np.float32),
        "ffn_bs_bc": np.tile(np.asarray(inputs["L0_ffn_bs"], np.float32)[None, :], (128, 1)),
    }
    base.update(consts)
    in_maps = []
    for c in range(8):
        m = dict(base)
        m["x"] = np.ascontiguousarray(x[c])
        m["ffn_W1"] = np.ascontiguousarray(W1[c])
        m["ffn_b1c"] = np.ascontiguousarray(b1[c].reshape(FF // 128, 128).T)
        m["ffn_W2"] = np.ascontiguousarray(W2[c])
        m["ffn_b2_bc"] = np.tile(b2[c][None, :], (128, 1)).astype(np.float32)
        m["coreid"] = np.full((1, 1), float(c), np.float32)
        in_maps.append(m)
    return in_maps


class MK:
    def __init__(self, upto=99, debug=(), gla_order=(4, 0, 1, 2, 3), p12mode="full"):
        self.p12mode = p12mode
        self.gla_order = list(gla_order)
        self.upto = upto
        self.debug = set(debug)
        self.dbg_t = {}
        self.nc = bass.Bass("TRN2", target_bir_lowering=False, debug=False,
                            detect_race_conditions=True)
        self._sems = {}
        self.cnt = {}

    def sem(self, name):
        if name not in self._sems:
            self._sems[name] = self.nc.alloc_semaphore(name)
            self.cnt[name] = 0
        return self._sems[name]

    def inc(self, instr, name, k=1):
        h = self.sem(name)
        instr.then_inc(h, k)
        self.cnt[name] += k
        return self.cnt[name]

    def incw(self, eng, instr, name, k=1):
        n = self.inc(instr, name, k)
        eng.wait_ge(self.sem(name), n)
        return n

    def w(self, eng, name, val=None):
        h = self.sem(name)
        v = self.cnt[name] if val is None else val
        if v > 0:
            eng.wait_ge(h, v)

    def din(self, name, shape, dtype=F32):
        return self.nc.dram_tensor(name, list(shape), dtype, kind="ExternalInput").ap()

    def dout(self, name, shape, dtype=F32):
        return self.nc.dram_tensor(name, list(shape), dtype, kind="ExternalOutput").ap()

    def dint(self, name, shape, dtype=F32):
        kind = "ExternalOutput" if "omout" in self.debug else "Internal"
        return self.nc.dram_tensor(name, list(shape), dtype, kind=kind).ap()

    def sb(self, name, shape, dtype=F32):
        return self.nc.alloc_sbuf_tensor(name, list(shape), dtype).ap()

    def sbctx(self, stack, name, shape, dtype=F32):
        t = stack.enter_context(self.nc.sbuf_tensor(name, list(shape), dtype))
        return t.ap() if hasattr(t, "ap") else t[:]

    def dbg(self, name):
        return name in self.debug

    def dbg_out(self, name, shape, dtype=F32):
        self.dbg_t[name] = self.dout("dbg_" + name, shape, dtype)
        return self.dbg_t[name]

    def cc1(self, key):
        return self.t_cc[:, CC[key]:CC[key] + 1]

    def dump_tile(self, tile_ap, name, shape, dtype=F32):
        if not self.dbg(name):
            return
        dst = self.dbg_out(name, shape, dtype)
        blk = self.block

        @blk.sync
        def _(q):
            self.w(q, "v")
            self.w(q, "s")
            self.inc(q.dma_start(out=dst[:, :] if len(shape) == 2 else dst[:, :, :],
                                 in_=tile_ap), "d_dbg", 16)

    def declare(self):
        self.x_in = self.din("x", (S, D))
        self.w_gate_in = self.din("w_gate", (D, NMEM))
        self.w_q_in = self.din("w_q", (D, DK))
        self.w_k_in = self.din("w_k", (NMEM, D, DK))
        self.w_v_in = self.din("w_v", (NMEM, D, DV))
        self.w_gk1_in = self.din("w_gk1", (NMEM, D, LOW))
        self.w_gk2_in = self.din("w_gk2", (NMEM, LOW, DK))
        self.bgk_in = self.din("b_gk_t", (128, NMEM * H))
        self.w_ks_in = self.din("w_ks", (D, DK))
        self.w_vs_in = self.din("w_vs", (D, DV))
        self.w_gk1s_in = self.din("w_gk1s", (D, LOW))
        self.w_gk2s_in = self.din("w_gk2s", (LOW, DK))
        self.bgks_in = self.din("b_gks_t", (128, H))
        self.w_g_in = self.din("w_g", (D, DV))
        self.w_o_in = self.din("w_o", (DV, D))
        self.normw_in = self.din("normw_bc", (128, DV))
        self.Ws_in = self.din("ffn_Ws", (D, NE))
        self.bs_in = self.din("ffn_bs_bc", (128, NE))
        self.W1_in = self.din("ffn_W1", (D, FF))
        self.b1_in = self.din("ffn_b1c", (128, FF // 128))
        self.W2_in = self.din("ffn_W2", (FF, D))
        self.b2_in = self.din("ffn_b2_bc", (128, D))
        self.cid_in = self.din("coreid", (1, 1))
        self.cc_in = self.din("cc", (128, NCC))
        self.identf_in = self.din("ident_f", (128, 128))
        self.identb_in = self.din("ident_b", (128, 128), BF16)
        self.triu_in = self.din("triu_inc", (128, 128), BF16)
        self.upst_in = self.din("upper_strict", (128, 128))
        self.onesm_in = self.din("ones_mat", (128, 128))
        self.iota4_in = self.din("iota4_bc", (128, 4))
        self.iotatg_in = self.din("iota_tg", (128, 128))
        self.iotacm_in = self.din("iota_cm", (128, 128))
        self.r128s_in = self.din("r128s", (128, 1))
        self.iotarows_in = self.din("iota_rows", (128, 128))
        self.onesw_in = self.din("ones_w", (128, S), BF16)

        self.out = self.dout("out", (S, D))

        self.tx_hbm = self.dint("tx_hbm", (NMEM * CAP + 8, D), BF16)
        self.om_hbm = self.dint("om_hbm", (NMEM * CAP + 8, D), BF16)
        self.sh_hbm = self.dint("sh_hbm", (S, D), BF16)
        self.w1bf_hbm = self.dint("w1bf_hbm", (FF // 128 * D, 128), BF16)
        self.yn_loc = self.dint("yn_loc", (S, D), BF16)
        self.yn_all = self.dint("yn_all", (NTOK, D), BF16)
        self.pT_loc = self.dint("pT_loc", (NE, S))
        self.pT_all = self.dint("pT_all", (NE * NE, S))
        self.tbl_hbm = self.dint("tbl_hbm", (KEXP + 1, 64))
        self.tbl2_hbm = (self.dout if "dbg12" in self.debug else self.dint)("tbl2_hbm", (20480, 64))
        self.obuf_loc = self.dint("obuf_loc", (KEXP, D))
        self.obuf_all = self.dint("obuf_all", (NTOK, D))
        self.ids_loc = self.dint("ids_loc", (128, KEXP // 128))
        self.ids_all = self.dint("ids_all", (8 * 128, KEXP // 128))

        self.t_cc = self.sb("t_cc", (128, NCC))
        self.t_identf = self.sb("t_identf", (128, 128))
        self.t_identb = self.sb("t_identb", (128, 128), BF16)
        self.t_triu = self.sb("t_triu", (128, 128), BF16)
        self.t_upst = self.sb("t_upst", (128, 128))
        self.t_ones = self.sb("t_ones", (128, 128))
        self.t_iota4 = self.sb("t_iota4", (128, 4))
        self.t_iotatg = self.sb("t_iotatg", (128, 128))
        self.t_iotacm = self.sb("t_iotacm", (128, 128))
        self.t_r128s = self.sb("t_r128s", (128, 1))
        self.t_iotarows = self.sb("t_iotarows", (128, 128))
        self.t_onesw = self.sb("t_onesw", (128, S), BF16)
        self.t_normw = self.sb("t_normw", (128, DV))
        self.t_bsbc = self.sb("t_bsbc", (128, NE))
        self.t_b2bc = self.sb("t_b2bc", (128, D))
        self.t_b1c = self.sb("t_b1c", (128, FF // 128))
        self.t_bgk = self.sb("t_bgk", (128, NMEM * H))
        self.t_bgks = self.sb("t_bgks", (128, H))
        self.t_cid = self.sb("t_cid", (128, 1))
        self.t_cid1 = self.sb("t_cid1", (1, 1))
        self.t_w1s = self.sb("t_w1s", (128, 2048), BF16)
        self.rw = self.sb("rw", (128, NT, 2))
        self.disp_idx = self.sb("disp_idx", (128, 2 * NT))
        self.wrap0 = self.sb("wrap0", (128, S // 16), I16)
        self.wrap1 = self.sb("wrap1", (128, S // 16), I16)
        self.wrapC = self.sb("wrapC", (128, KEXP // 16), I16)
        self.wq = self.sb("wq", (128, 8, DK), BF16)
        self.wgate = self.sb("wgate", (128, 8, NMEM), BF16)
        self.wWs = self.sb("wWs", (128, 8, NE), BF16)
        self.zf32 = self.sb("zf32", (128, 1024))
        self.ln_a = self.sb("ln_a", (128, 8))
        self.snk = self.sb("snk", (128, 1024))
        self.snk_v = self.sb("snk_v", (128, 1024))
        self.wrA = self.sb("wrA", (128, 128))
        self.sc1 = self.sb("sc1", (128, 1024))
        self.sc3 = self.sb("sc3", (128, 128))
        self.sc4 = self.sb("sc4", (128, 128))
        self.sc5 = self.sb("sc5", (128, 128))
        self.sc6 = self.sb("sc6", (128, 128))
        self.scu8 = self.sb("scu8", (128, 128), U8)
        self.sccol = self.sb("sccol", (128, 16))
        self.sci32 = self.sb("sci32", (1, 8), I32)
        self.tid = self.sb("tid", (128, 16))
        self.cnt8 = self.sb("cnt8", (128, 8))
        self.rwd = self.sb("rwd", (128, 2 * NT))
        self.tval = self.sb("tval", (128, 16))

        self.nc.const_aps.register(F32, 0.0, self.cc1("zero"))
        self.nc.const_aps.register(F32, 1.0, self.cc1("one"))

        self.pA = self.sb_ps("pA", F32)
        self.pO = self.sb_ps("pO", F32)
        self.pD = self.sb_ps("pD", F32)
        self.pT1 = self.sb_ps("pT1", BF16)
        self.pT2 = self.sb_ps("pT2", BF16)
        self.pW = [self.sb_ps("pW0", F32), self.sb_ps("pW1", F32)]
        self.pM = self.sb_ps("pM", F32)

    def sb_ps(self, name, dtype):
        n = 512 if dtype == F32 else 1024
        return self.nc.alloc_psum_tensor(name, [128, n], dtype).ap()

    def phase_barrier(self):
        blk = self.block

        @blk.gpsimd
        def _(g):
            for name in list(self._sems):
                if name.startswith("d_"):
                    self.w(g, name)
            self.w(g, "t")
            self.w(g, "v")
            self.w(g, "s")
            self.inc(g.nop(), "g")

        for mk_sec in (blk.vector, blk.scalar, blk.tensor, blk.sync):
            @mk_sec
            def _(e):
                self.w(e, "g")

    def wrap_build(self, n, nat_ap, out_tile, replicate=2):
        m = n // 128
        blk = self.block

        @blk.tensor
        def _(t):
            self.w(t, "v")
            self.w(t, "s")
            self.inc(t.transpose(self.pM[0:m, 0:128], nat_ap, self.t_identf), "t")

        @blk.vector
        def _(v):
            self.w(v, "t")
            self.inc(v.tensor_copy(self.wrA[0:m, 0:128], self.pM[0:m, 0:128]), "v")

        for g8 in range(8):
            @blk.tensor
            def _(t, g8=g8):
                self.w(t, "v")
                self.inc(t.transpose(self.pM[0:16, 0:m], self.wrA[0:m, 16 * g8:16 * g8 + 16],
                                     self.t_identf[0:m, 0:m]), "t")

            @blk.vector
            def _(v, g8=g8):
                self.w(v, "t")
                dst = out_tile[0:16, 0:8 * m].rearrange("r (c g) -> r c g", g=8)[:, :, g8]
                self.inc(v.tensor_copy(dst, self.pM[0:16, 0:m]), "v")

        @blk.gpsimd
        def _(g):
            self.w(g, "v")
            for r in range(1, replicate):
                self.inc(g.dma_start(out=out_tile[16 * r:16 * r + 16, 0:8 * m],
                                     in_=out_tile[0:16, 0:8 * m]), "d_wr", 16)
            self.w(g, "d_wr")

    def ln_tile(self, s, v, src_f32, dst_bf16):
        raise RuntimeError("inline in phases")

    def p1_ln(self, stack):
        blk = self.block
        self.xn = self.sbctx(stack, "xn", (128, NT, D), BF16)
        xb = [self.sbctx(stack, f"p1x{i}", (128, D)) for i in range(2)]

        @blk.sync
        def _(q):
            self.w(q, "d_c")
            for j in range(2):
                self.inc(q.dma_start(out=xb[j][:],
                                     in_=self.x_in.rearrange("(c p) d -> p c d", p=128)[:, j, :]),
                         f"d_x{j % 2}", 16)

        for j in range(NT):
            xt = xb[j % 2]

            @blk.vector
            def _(v, j=j, xt=xt):
                self.w(v, f"d_x{j % 2}")
                self.incw(v, v.tensor_scalar(out=self.snk_v[:, 0:D], in0=xt[:], scalar1=1.0,
                                         scalar2=0.0, op0=OP.mult, op1=OP.add,
                                         accum_out=self.ln_a[:, 0:1]), "v")

            @blk.scalar
            def _(s, j=j, xt=xt):
                self.w(s, f"d_x{j % 2}")
                self.inc(s.activation(out=self.snk[:, 0:D], in_=xt[:], func=AF.Square,
                                      bias=self.cc1("zero"), scale=self.cc1("one"),
                                      accum_out=self.ln_a[:, 1:2]), "s")

            @blk.vector
            def _(v, j=j):
                self.w(v, "s")
                self.w(v, "v")
                la = self.ln_a
                self.incw(v, v.tensor_scalar(out=la[:, 2:3], in0=la[:, 0:1], scalar1=1.0 / D,
                                         scalar2=None, op0=OP.mult), "v")
                self.incw(v, v.tensor_scalar(out=la[:, 3:4], in0=la[:, 1:2], scalar1=1.0 / D,
                                         scalar2=None, op0=OP.mult), "v")
                self.incw(v, v.tensor_tensor(out=la[:, 4:5], in0=la[:, 2:3], in1=la[:, 2:3],
                                         op=OP.mult), "v")
                self.incw(v, v.tensor_tensor(out=la[:, 5:6], in0=la[:, 3:4], in1=la[:, 4:5],
                                         op=OP.subtract), "v")

            @blk.scalar
            def _(s, j=j):
                self.w(s, "v")
                self.inc(s.activation(out=self.ln_a[:, 6:7], in_=self.ln_a[:, 5:6],
                                      func=AF.Ln, bias=self.cc1("eps"),
                                      scale=self.cc1("one")), "s")
                self.w(s, "s")
                self.inc(s.activation(out=self.ln_a[:, 6:7], in_=self.ln_a[:, 6:7],
                                      func=AF.Exp, bias=self.cc1("zero"),
                                      scale=self.cc1("half")), "s")

            @blk.vector
            def _(v, j=j, xt=xt):
                self.w(v, "s")
                self.incw(v, v.reciprocal(self.ln_a[:, 7:8], self.ln_a[:, 6:7]), "v")
                self.w(v, "v")
                self.incw(v, v.tensor_scalar(out=self.xn[:, j, :], in0=xt[:],
                                         scalar1=self.ln_a[:, 2:3], scalar2=self.ln_a[:, 7:8],
                                         op0=OP.subtract, op1=OP.mult), "v")

            if j + 2 < NT:
                @blk.sync
                def _(q, j=j, xt=xt):
                    self.w(q, "v")
                    self.inc(q.dma_start(
                        out=xt[:],
                        in_=self.x_in.rearrange("(c p) d -> p c d", p=128)[:, j + 2, :]),
                        f"d_x{j % 2}", 16)

            @blk.tensor
            def _(t, j=j):
                self.w(t, "v")
                for dc in range(8):
                    self.inc(t.transpose(self.pT1[:, dc * 128:(dc + 1) * 128],
                                         self.xn[:, j, dc * 128:(dc + 1) * 128],
                                         self.t_identb), "t")

            @blk.vector
            def _(v, j=j):
                self.w(v, "t")
                for dc in range(8):
                    self.incw(v, v.tensor_copy(self.xnT[:, dc, j * 128:(j + 1) * 128],
                                           self.pT1[:, dc * 128:(dc + 1) * 128]), "v")

        if self.dbg("xnT"):
            dstT = self.dbg_out("xnT", (128, 8, S))

            @blk.gpsimd
            def _(g):
                self.w(g, "v")
                self.inc(g.dma_start(out=dstT[:, :, :], in_=self.xnT[:, :, :]), "d_dbgg", 16)

        if self.dbg("xn"):
            dst = self.dbg_out("xn", (128, NT, D))

            @blk.gpsimd
            def _(g):
                self.w(g, "v")
                self.inc(g.dma_start(out=dst[:, :, :], in_=self.xn[:, :, :]), "d_dbgg", 16)

    def p2_gate(self, stack):
        blk = self.block
        gtile = self.sbctx(stack, "gtile", (128, 4))
        gdbg = self.sbctx(stack, "gdbg", (128, NT * 4)) if self.dbg("gate") else None
        ftile = self.sbctx(stack, "ftile", (128, NT, 4))
        stile = self.sbctx(stack, "stile", (128, NT, 4))
        maskT = self.sbctx(stack, "maskT", (4, S))
        slotT = self.sbctx(stack, "slotT", (4, S))
        col = self.sccol

        for j in range(NT):
            @blk.tensor
            def _(t, j=j):
                self.w(t, "v")
                self.w(t, "d_w", 32)
                for dc in range(8):
                    self.inc(t.matmul(self.pW[0][:, 0:4], self.xnT[:, dc, j * 128:(j + 1) * 128],
                                      self.wgate[:, dc, :], start=(dc == 0), stop=(dc == 7)), "t")

            @blk.vector
            def _(v, j=j):
                self.w(v, "t")
                G = gtile
                self.incw(v, v.tensor_copy(G[:], self.pW[0][:, 0:4]), "v")
                if gdbg is not None:
                    self.incw(v, v.tensor_copy(gdbg[:, 4 * j:4 * j + 4], self.pW[0][:, 0:4]), "v")
                self.incw(v, v.tensor_reduce(out=col[:, 0:1], in_=G[:], axis=AX.X, op=OP.max), "v")
                self.w(v, "v")
                e1 = self.sc3[:, 0:4]
                self.incw(v, v.tensor_scalar(out=e1, in0=G[:], scalar1=col[:, 0:1],
                                         scalar2=None, op0=OP.is_equal), "v")
                c1 = self.sc4[:, 0:4]
                self.incw(v, v.tensor_tensor_scan(out=c1, data0=self.t_onesw[:, 0:4], data1=e1,
                                              initial=0.0, op0=OP.mult, op1=OP.add), "v")
                t1 = self.sc5[:, 0:4]
                self.incw(v, v.tensor_scalar(out=t1, in0=c1, scalar1=1.0, scalar2=None,
                                         op0=OP.is_equal), "v")
                f = ftile[:, j, :]
                self.incw(v, v.tensor_tensor(out=f, in0=e1, in1=t1, op=OP.mult), "v")
                Gm = self.sc6[:, 0:4]
                self.incw(v, v.scalar_tensor_tensor(out=Gm, in0=f, scalar=-1e30, in1=G[:],
                                                op0=OP.mult, op1=OP.add), "v")
                self.incw(v, v.tensor_reduce(out=col[:, 1:2], in_=Gm, axis=AX.X, op=OP.max), "v")
                self.w(v, "v")
                self.incw(v, v.tensor_scalar(out=e1, in0=Gm, scalar1=col[:, 1:2], scalar2=None,
                                         op0=OP.is_equal), "v")
                self.incw(v, v.tensor_tensor_scan(out=c1, data0=self.t_onesw[:, 0:4], data1=e1,
                                              initial=0.0, op0=OP.mult, op1=OP.add), "v")
                self.incw(v, v.tensor_scalar(out=t1, in0=c1, scalar1=1.0, scalar2=None,
                                         op0=OP.is_equal), "v")
                sflag = stile[:, j, :]
                self.incw(v, v.tensor_tensor(out=sflag, in0=e1, in1=t1, op=OP.mult), "v")
                self.incw(v, v.tensor_tensor(out=col[:, 2:3], in0=col[:, 0:1], in1=col[:, 1:2],
                                         op=OP.subtract), "v")
                self.incw(v, v.tensor_tensor(out=self.sc3[:, 0:4], in0=f, in1=sflag, op=OP.add), "v")

            @blk.vector
            def _(v, j=j):
                self.incw(v, v.tensor_copy(self.rwd[:, j:j + 1], col[:, 2:3]), "v")

            @blk.tensor
            def _(t, j=j):
                self.w(t, "v")
                self.inc(t.transpose(self.pM[0:4, 0:128], self.sc3[:, 0:4], self.t_identf), "t")

            @blk.vector
            def _(v, j=j):
                self.w(v, "t")
                self.incw(v, v.tensor_copy(maskT[:, j * 128:(j + 1) * 128], self.pM[0:4, 0:128]), "v")

        @blk.scalar
        def _(s):
            self.w(s, "v")
            self.inc(s.activation(out=self.rwd[:, NT:2 * NT], in_=self.rwd[:, 0:NT],
                                  func=AF.Exp, bias=self.cc1("zero"),
                                  scale=self.cc1("negone")), "s")

        @blk.vector
        def _(v):
            self.w(v, "s")
            self.incw(v, v.tensor_scalar(out=self.rwd[:, NT:2 * NT], in0=self.rwd[:, NT:2 * NT],
                                     scalar1=1.0, scalar2=None, op0=OP.add), "v")
            self.incw(v, v.reciprocal(self.rwd[:, NT:2 * NT], self.rwd[:, NT:2 * NT]), "v")
            for j in range(NT):
                self.incw(v, v.tensor_copy(self.rw[:, j, 0:1], self.rwd[:, NT + j:NT + j + 1]), "v")
                self.incw(v, v.tensor_scalar(out=self.rw[:, j, 1:2], in0=self.rw[:, j, 0:1],
                                         scalar1=-1.0, scalar2=1.0, op0=OP.mult, op1=OP.add), "v")

        @blk.vector
        def _(v):
            self.incw(v, v.tensor_tensor_scan(out=slotT[:], data0=self.t_onesw[0:4, :],
                                          data1=maskT[:], initial=0.0,
                                          op0=OP.mult, op1=OP.add), "v")
            self.incw(v, v.tensor_scalar(out=col[0:4, 8:9], in0=slotT[:, S - 1:S], scalar1=-1.0,
                                     scalar2=float(CAP - 1), op0=OP.mult, op1=OP.add), "v")
            self.w(v, "v")
            self.incw(v, v.tensor_scalar(out=slotT[:], in0=slotT[:], scalar1=col[0:4, 8:9],
                                     scalar2=None, op0=OP.add), "v")

        for j in range(NT):
            @blk.tensor
            def _(t, j=j):
                self.w(t, "v")
                self.inc(t.transpose(self.pM[0:128, 0:4], slotT[:, j * 128:(j + 1) * 128],
                                     self.t_identf[0:4, 0:4]), "t")

            @blk.vector
            def _(v, j=j):
                self.w(v, "t")
                st4 = self.sc4[:, 0:4]
                self.incw(v, v.tensor_copy(st4, self.pM[0:128, 0:4]), "v")
                for c, ftl in ((0, ftile), (1, stile)):
                    fj = ftl[:, j, :]
                    self.incw(v, v.tensor_tensor(out=self.sc5[:, 0:4], in0=fj, in1=self.t_iota4,
                                             op=OP.mult), "v")
                    self.incw(v, v.tensor_reduce(out=col[:, 3:4], in_=self.sc5[:, 0:4],
                                             axis=AX.X, op=OP.add), "v")
                    self.incw(v, v.tensor_tensor(out=self.sc5[:, 0:4], in0=fj, in1=st4,
                                             op=OP.mult), "v")
                    self.incw(v, v.tensor_reduce(out=col[:, 4:5], in_=self.sc5[:, 0:4],
                                             axis=AX.X, op=OP.add), "v")
                    self.incw(v, v.tensor_scalar(out=col[:, 5:6], in0=col[:, 4:5], scalar1=0.0,
                                             scalar2=None, op0=OP.is_ge), "v")
                    self.incw(v, v.scalar_tensor_tensor(out=col[:, 6:7], in0=col[:, 3:4],
                                                    scalar=float(CAP), in1=col[:, 4:5],
                                                    op0=OP.mult, op1=OP.add), "v")
                    self.incw(v, v.tensor_scalar(out=col[:, 6:7], in0=col[:, 6:7], scalar1=-4096.0,
                                             scalar2=None, op0=OP.add), "v")
                    self.incw(v, v.tensor_tensor(out=col[:, 6:7], in0=col[:, 6:7], in1=col[:, 5:6],
                                             op=OP.mult), "v")
                    self.incw(v, v.tensor_scalar(out=self.disp_idx[:, c * NT + j:c * NT + j + 1],
                                             in0=col[:, 6:7], scalar1=4096.0,
                                             scalar2=None, op0=OP.add), "v")

        if self.dbg("fs"):
            dstf = self.dbg_out("fs", (128, NT, 4))
            dsts = self.dbg_out("ss", (128, NT, 4))
            dstm = self.dbg_out("maskT", (4, S))
            dstsl = self.dbg_out("slotT", (4, S))

            @blk.sync
            def _(q):
                self.w(q, "v")
                self.inc(q.dma_start(out=dstf[:, :, :], in_=ftile[:, :, :]), "d_dbg", 16)
                self.inc(q.dma_start(out=dsts[:, :, :], in_=stile[:, :, :]), "d_dbg", 16)
                self.inc(q.dma_start(out=dstm[:, :], in_=maskT[:, :]), "d_dbg", 16)
                self.inc(q.dma_start(out=dstsl[:, :], in_=slotT[:, :]), "d_dbg", 16)

        if gdbg is not None:
            dstg = self.dbg_out("gate", (128, NT * 4))

            @blk.sync
            def _(q):
                self.w(q, "v")
                self.inc(q.dma_start(out=dstg[:, :], in_=gdbg[:, :]), "d_dbg", 16)

        self.wrap_build(S, self.disp_idx[:, 0:NT], self.wrap0)
        self.wrap_build(S, self.disp_idx[:, NT:2 * NT], self.wrap1)

        if self.dbg("disp"):
            dst = self.dbg_out("disp", (128, 2 * NT))

            @blk.sync
            def _(q):
                self.w(q, "v")
                self.inc(q.dma_start(out=dst[:, :], in_=self.disp_idx[:, :]), "d_dbg", 16)
        if self.dbg("rw"):
            dst = self.dbg_out("rw", (128, NT, 2))

            @blk.sync
            def _(q):
                self.w(q, "v")
                self.inc(q.dma_start(out=dst[:, :, :], in_=self.rw[:, :, :]), "d_dbg", 16)

        @blk.gpsimd
        def _(g):
            self.w(g, "v")
            self.w(g, "d_z")
            for wt in (self.wrap0, self.wrap1):
                self.inc(g.dma_scatter_add(out_ap=self.tx_hbm[:, :], in_ap=self.xn[:, :, :],
                                           idxs_ap=wt[:, :], num_idxs=S, num_idxs_reg=S,
                                           elem_size=D, single_packet=False), "d_g", 16)
                self.w(g, "d_g")

    def gla_group(self, stack, gi, tiles):
        blk = self.block
        routed = gi < 4
        cap = CAP if routed else S
        nmh = cap // 1024
        (txT, vsb, u2m, qfh, kfh, lamt, emt, lpt, tmpT, cs_c, eCf, Ssb,
         k2c4, amt4, k2t4, wkb, wvb, wg1b, wg2b) = tiles

        @blk.gpsimd
        def _(g):
            self.w(g, "t")
            self.w(g, "v")
            if routed:
                self.inc(g.dma_start(out=wkb[:],
                                     in_=self.w_k_in[gi].rearrange("(c p) o -> p c o", p=128)),
                         "d_wm", 16)
                self.inc(g.dma_start(out=wvb[:],
                                     in_=self.w_v_in[gi].rearrange("(c p) o -> p c o", p=128)),
                         "d_wm", 16)
                self.inc(g.dma_start(out=wg1b[:],
                                     in_=self.w_gk1_in[gi].rearrange("(c p) o -> p c o", p=128)),
                         "d_wm", 16)
                self.inc(g.dma_start(out=wg2b[0:16, :], in_=self.w_gk2_in[gi][:, :]),
                         "d_wm", 16)
            else:
                self.inc(g.dma_start(out=wkb[:],
                                     in_=self.w_ks_in.rearrange("(c p) o -> p c o", p=128)),
                         "d_wm", 16)
                self.inc(g.dma_start(out=wvb[:],
                                     in_=self.w_vs_in.rearrange("(c p) o -> p c o", p=128)),
                         "d_wm", 16)
                self.inc(g.dma_start(out=wg1b[:],
                                     in_=self.w_gk1s_in.rearrange("(c p) o -> p c o", p=128)),
                         "d_wm", 16)
                self.inc(g.dma_start(out=wg2b[0:16, :], in_=self.w_gk2s_in[:, :]),
                         "d_wm", 16)

        wm_tok = self.cnt["d_wm"]
        bcol = (lambda h: self.t_bgk[:, gi * H + h:gi * H + h + 1]) if routed else                (lambda h: self.t_bgks[:, h:h + 1])

        if routed:
            @blk.sync
            def _(q):
                self.w(q, "d_g")
                self.w(q, "v")
                self.w(q, "d_om")
                self.w(q, "d_sh")
                self.inc(q.dma_start(
                    out=u2m[:, :, :],
                    in_=self.tx_hbm[gi * CAP:(gi + 1) * CAP, :].rearrange(
                        "(c p) d -> p c d", p=128)), "d_tx", 16)

            for tb in range(8):
                @blk.tensor
                def _(t, tb=tb):
                    self.w(t, "d_tx")
                    self.w(t, "v")
                    for dc in range(8):
                        self.inc(t.transpose(self.pT1[:, dc * 128:(dc + 1) * 128],
                                             u2m[:, tb, dc * 128:(dc + 1) * 128],
                                             self.t_identb), "t")

                @blk.vector
                def _(v, tb=tb):
                    self.w(v, "t")
                    for dc in range(8):
                        self.inc(v.tensor_copy(txT[:, dc, tb * 128:(tb + 1) * 128],
                                               self.pT1[:, dc * 128:(dc + 1) * 128]), "v")

        @blk.vector
        def _(v):
            for h in range(H):
                self.inc(v.memset(Ssb[h][:], 0.0), "v")

        for mh in range(nmh):
            def srcT(dc, sl):
                if routed:
                    return txT[:, dc, sl]
                base = mh * 1024
                return self.xnT[:, dc, base + sl.start: base + sl.stop]

            for tb in range(8):
                @blk.tensor
                def _(t, tb=tb, mh=mh):
                    self.w(t, "v")
                    self.w(t, "s")
                    self.w(t, "d_wm", wm_tok)
                    for dvh in range(2):
                        pb = self.pW[dvh]
                        for dc in range(8):
                            self.inc(t.matmul(pb[:, 0:512],
                                              srcT(dc, slice(tb * 128, (tb + 1) * 128)),
                                              wvb[:, dc, dvh * 512:(dvh + 1) * 512],
                                              start=(dc == 0), stop=(dc == 7)), "t")

                @blk.vector
                def _(v, tb=tb):
                    self.w(v, "t")
                    self.inc(v.tensor_copy(vsb[:, tb, 0:512], self.pW[0][:, 0:512]), "v")

                @blk.scalar
                def _(s, tb=tb):
                    self.w(s, "t")
                    self.inc(s.activation(out=vsb[:, tb, 512:1024], in_=self.pW[1][:, 0:512],
                                          func=AF.Copy, bias=0.0, scale=1.0), "s")

            for h in range(H):
                for hf in range(2):
                    @blk.tensor
                    def _(t, h=h, hf=hf, mh=mh):
                        self.w(t, "v")
                        self.w(t, "s")
                        for dc in range(8):
                            self.inc(t.matmul(self.pW[0][0:16, 0:512],
                                              wg1b[:, dc, :],
                                              srcT(dc, slice(hf * 512, (hf + 1) * 512)),
                                              start=(dc == 0), stop=(dc == 7)), "t")

                    @blk.scalar
                    def _(s, h=h, hf=hf):
                        self.w(s, "t")
                        self.inc(s.activation(out=tmpT[0:16, hf * 512:(hf + 1) * 512],
                                              in_=self.pW[0][0:16, 0:512], func=AF.Copy, bias=0.0, scale=1.0), "s")

                    @blk.tensor
                    def _(t, h=h, hf=hf):
                        self.w(t, "s")
                        self.inc(t.matmul(self.pW[1][:, 0:512],
                                          wg2b[0:16, h * 128:(h + 1) * 128],
                                          tmpT[0:16, hf * 512:(hf + 1) * 512],
                                          start=True, stop=True), "t")

                    @blk.scalar
                    def _(s, h=h, hf=hf):
                        self.w(s, "t")
                        self.inc(s.activation(out=self.sc1[:, 0:512], in_=self.pW[1][:, 0:512],
                                              func=AF.Exp, bias=bcol(h),
                                              scale=self.cc1("negone")), "s")
                        self.w(s, "s")
                        self.inc(s.activation(out=lpt[:, hf * 512:(hf + 1) * 512],
                                              in_=self.sc1[:, 0:512], func=AF.Ln,
                                              bias=self.cc1("one"), scale=self.cc1("one")), "s")

                @blk.vector
                def _(v, h=h):
                    self.w(v, "s")
                    for j in range(8):
                        self.inc(v.tensor_tensor_scan(
                            out=cs_c[:, j * 128:(j + 1) * 128],
                            data0=self.t_onesw[:, 0:128],
                            data1=lpt[:, j * 128:(j + 1) * 128],
                            initial=0.0, op0=OP.mult, op1=OP.add), "v")

                @blk.scalar
                def _(s, h=h):
                    self.w(s, "v")
                    self.inc(s.activation(out=lamt[:], in_=cs_c[:, 0:1024], func=AF.Exp,
                                          bias=self.cc1("lnqs"), scale=self.cc1("neg16")), "s")
                    self.inc(s.activation(out=emt[:], in_=cs_c[:, 0:1024], func=AF.Exp,
                                          bias=self.cc1("zero"), scale=self.cc1("inv16")), "s")
                    for j in range(8):
                        self.inc(s.activation(out=eCf[:, h * 8 + j:h * 8 + j + 1],
                                              in_=cs_c[:, j * 128 + 127:j * 128 + 128],
                                              func=AF.Exp, bias=self.cc1("zero"),
                                              scale=self.cc1("neg16")), "s")

                for hf in range(2):
                    @blk.tensor
                    def _(t, h=h, hf=hf, mh=mh):
                        self.w(t, "s")
                        self.w(t, "v")
                        for dc in range(8):
                            self.inc(t.matmul(self.pW[0][:, 0:512],
                                              self.wq[:, dc, h * 128:(h + 1) * 128],
                                              srcT(dc, slice(hf * 512, (hf + 1) * 512)),
                                              start=(dc == 0), stop=(dc == 7)), "t")
                        for dc in range(8):
                            self.inc(t.matmul(self.pW[1][:, 0:512],
                                              wkb[:, dc, h * 128:(h + 1) * 128],
                                              srcT(dc, slice(hf * 512, (hf + 1) * 512)),
                                              start=(dc == 0), stop=(dc == 7)), "t")

                    @blk.vector
                    def _(v, h=h, hf=hf):
                        self.w(v, "t")
                        sl = slice(hf * 512, (hf + 1) * 512)
                        self.inc(v.tensor_tensor(out=qfh[h][:, sl], in0=self.pW[0][:, 0:512],
                                                 in1=lamt[:, sl], op=OP.mult), "v")
                        self.inc(v.tensor_tensor(out=kfh[h][:, sl], in0=self.pW[1][:, 0:512],
                                                 in1=emt[:, sl], op=OP.mult), "v")

            for jl in range(8):
                jb = slice(jl * 128, (jl + 1) * 128)
                jg = mh * 8 + jl

                @blk.tensor
                def _(t, jl=jl, jb=jb):
                    self.w(t, "v")
                    for h in range(H):
                        self.inc(t.matmul(self.pA[:, h * 128:(h + 1) * 128],
                                          kfh[h][:, jb], qfh[h][:, jb],
                                          start=True, stop=True), "t")

                @blk.vector
                def _(v, jl=jl, jb=jb):
                    self.w(v, "t")
                    for h in range(H):
                        self.inc(v.tensor_tensor(out=amt4[:, h * 128:(h + 1) * 128],
                                                 in0=self.pA[:, h * 128:(h + 1) * 128],
                                                 in1=self.t_triu, op=OP.mult), "v")
                        self.inc(v.tensor_scalar(out=k2c4[:, h * 128:(h + 1) * 128],
                                                 in0=kfh[h][:, jb],
                                                 scalar1=eCf[:, h * 8 + jl:h * 8 + jl + 1],
                                                 scalar2=None, op0=OP.mult), "v")

                @blk.tensor
                def _(t, jl=jl, jb=jb, jg=jg, mh=mh):
                    self.w(t, "v")
                    for h in range(H):
                        self.inc(t.transpose(self.pT1[:, h * 128:(h + 1) * 128],
                                             k2c4[:, h * 128:(h + 1) * 128], self.t_identb), "t")
                    for h in range(H):
                        pOb = self.pO if h < 2 else self.pD
                        po = pOb[:, (h % 2) * 256:(h % 2) * 256 + 256]
                        first = (jg == 0)
                        self.inc(t.matmul(po, amt4[:, h * 128:(h + 1) * 128],
                                          vsb[:, jl, h * 256:(h + 1) * 256],
                                          start=True, stop=first), "t")
                        if not first:
                            self.inc(t.matmul(po, qfh[h][:, jb], Ssb[h][:],
                                              start=False, stop=True), "t")

                @blk.vector
                def _(v, jl=jl):
                    self.w(v, "t")
                    for h in range(H):
                        self.inc(v.tensor_copy(k2t4[:, h * 128:(h + 1) * 128],
                                               self.pT1[:, h * 128:(h + 1) * 128]), "v")

                @blk.tensor
                def _(t, jl=jl):
                    self.w(t, "v")
                    for h in range(H):
                        pDb = self.pW[0] if h < 2 else self.pW[1]
                        self.inc(t.matmul(pDb[:, (h % 2) * 256:(h % 2) * 256 + 256],
                                          k2t4[:, h * 128:(h + 1) * 128],
                                          vsb[:, jl, h * 256:(h + 1) * 256],
                                          start=True, stop=True), "t")

                @blk.vector
                def _(v, jl=jl, jg=jg):
                    self.w(v, "t")
                    if jl == 0:
                        self.w(v, "d_om")
                        self.w(v, "d_sh")
                    for h in range(H):
                        pOb = self.pO if h < 2 else self.pD
                        po = pOb[:, (h % 2) * 256:(h % 2) * 256 + 256]
                        self.inc(v.tensor_copy(u2m[:, jl, h * 256:(h + 1) * 256], po), "v")
                    for h in range(H):
                        pDb = self.pW[0] if h < 2 else self.pW[1]
                        self.inc(v.scalar_tensor_tensor(
                            out=Ssb[h][:], in0=Ssb[h][:],
                            scalar=eCf[:, h * 8 + jl:h * 8 + jl + 1],
                            in1=pDb[:, (h % 2) * 256:(h % 2) * 256 + 256],
                            op0=OP.mult, op1=OP.add), "v")

            @blk.sync
            def _(q, mh=mh):
                self.w(q, "v")
                if routed:
                    dst = self.om_hbm[gi * CAP:(gi + 1) * CAP, :]
                    self.inc(q.dma_start(out=dst.rearrange("(c p) d -> p c d", p=128),
                                         in_=u2m[:, :, :]), "d_om", 16)
                else:
                    dst = self.sh_hbm[mh * 1024:(mh + 1) * 1024, :]
                    self.inc(q.dma_start(out=dst.rearrange("(c p) d -> p c d", p=128),
                                         in_=u2m[:, :, :]), "d_sh", 16)

    def p_gla(self, stack):
        blk = self.block
        txT = self.sbctx(stack, "txT", (128, 8, CAP), BF16)
        vsb = self.sbctx(stack, "vsb", (128, 8, DV), BF16)
        u2m = self.sbctx(stack, "u2m", (128, 8, D), BF16)
        qfh = [self.sbctx(stack, f"qfh{h}", (128, 1024), BF16) for h in range(H)]
        kfh = [self.sbctx(stack, f"kfh{h}", (128, 1024), BF16) for h in range(H)]
        emt = self.sbctx(stack, "emt", (128, 1024), BF16)
        lpt = self.sbctx(stack, "lpt", (128, 1024), BF16)
        lamt = lpt
        tmpT = self.sbctx(stack, "tmpT", (16, 1024), BF16)
        cs_c = self.sbctx(stack, "cs_c", (128, 1024))
        eCf = self.sbctx(stack, "eCf", (128, H * 8))
        Ssb = [self.sbctx(stack, f"Ssb{h}", (128, HDV), BF16) for h in range(H)]
        k2c4 = self.sbctx(stack, "k2c4", (128, 512), BF16)
        amt4 = self.sbctx(stack, "amt4", (128, 512), BF16)
        k2t4 = self.sbctx(stack, "k2t4", (128, 512), BF16)
        wkb = self.sbctx(stack, "wkb", (128, 8, DK), BF16)
        wvb = self.sbctx(stack, "wvb", (128, 8, DV), BF16)
        wg1b = self.sbctx(stack, "wg1b", (128, 8, LOW), BF16)
        wg2b = self.sbctx(stack, "wg2b", (16, DK), BF16)
        tiles = (txT, vsb, u2m, qfh, kfh, lamt, emt, lpt, tmpT, cs_c, eCf, Ssb,
                 k2c4, amt4, k2t4, wkb, wvb, wg1b, wg2b)

        @blk.vector
        def _(v):
            self.w(v, "d_g")

        @blk.scalar
        def _(s):
            self.w(s, "d_g")

        for gi in self.gla_order:
            self.gla_group(stack, gi, tiles)

        if self.dbg("om"):
            pass

    def p6_reco(self, stack):
        blk = self.block
        self.sc2 = self.sbctx(stack, "sc2", (128, 1024))
        gat0 = self.sbctx(stack, "gat0", (128, 8, DV), BF16)
        gat1 = self.sbctx(stack, "gat1", (128, 8, DV), BF16)
        shb = self.sbctx(stack, "shb", (128, 8, DV), BF16)
        wgt = self.sbctx(stack, "wgt", (128, 8, DV), BF16)
        sg = self.sbctx(stack, "sg", (128, DV))

        @blk.gpsimd
        def _(g):
            self.w(g, "d_wm")
            self.w(g, "t")
            self.w(g, "v")
            self.w(g, "s")
            self.inc(g.dma_start(out=wgt[:],
                                 in_=self.w_g_in.rearrange("(c p) o -> p c o", p=128)),
                     "d_wg", 16)

        wg_tok = self.cnt["d_wg"]

        for hf in range(2):
            @blk.gpsimd
            def _(g, hf=hf):
                self.w(g, "d_om")
                self.w(g, "d_z")
                self.w(g, "s")
                self.w(g, "v")
                for c, (wt, gat) in enumerate(((self.wrap0, gat0), (self.wrap1, gat1))):
                    self.inc(g.dma_gather(out_ap=gat[:, :, :], in_ap=self.om_hbm[:, :],
                                          idxs_ap=wt[:, 64 * hf:64 * hf + 64],
                                          num_idxs=1024, num_idxs_reg=1024,
                                          elem_size=D, single_packet=False), "d_g", 16)
                self.w(g, "d_g")

            @blk.sync
            def _(q, hf=hf):
                self.w(q, "d_sh")
                self.w(q, "v")
                self.inc(q.dma_start(
                    out=shb[:, :, :],
                    in_=self.sh_hbm[hf * 1024:(hf + 1) * 1024, :].rearrange(
                        "(c p) d -> p c d", p=128)), "d_tx", 16)

            for jg in range(8):
                j = hf * 8 + jg

                @blk.scalar
                def _(s, j=j, jg=jg):
                    self.w(s, "d_g")
                    self.w(s, "v")
                    self.inc(s.activation(out=self.sc1[:, 0:DV], in_=gat0[:, jg, :],
                                          func=AF.Copy, bias=0.0,
                                          scale=self.rw[:, j, 0:1]), "s")

                @blk.vector
                def _(v, j=j, jg=jg):
                    self.w(v, "s")
                    self.w(v, "d_tx")
                    self.incw(v, v.scalar_tensor_tensor(out=self.sc2[:, 0:DV], in0=gat1[:, jg, :],
                                                    scalar=self.rw[:, j, 1:2],
                                                    in1=self.sc1[:, 0:DV],
                                                    op0=OP.mult, op1=OP.add), "v")
                    self.incw(v, v.tensor_tensor(out=self.sc1[:, 0:DV], in0=self.sc2[:, 0:DV],
                                             in1=shb[:, jg, :], op=OP.add), "v")

                @blk.scalar
                def _(s, j=j):
                    self.w(s, "v")
                    for h in range(H):
                        self.inc(s.activation(out=self.snk[:, h * HDV:(h + 1) * HDV],
                                              in_=self.sc1[:, h * HDV:(h + 1) * HDV],
                                              func=AF.Square, bias=self.cc1("zero"),
                                              scale=self.cc1("one"),
                                              accum_out=self.sccol[:, 8 + h:9 + h]), "s")

                @blk.vector
                def _(v, j=j):
                    self.w(v, "s")
                    self.incw(v, v.tensor_scalar(out=self.sccol[:, 12:16],
                                             in0=self.sccol[:, 8:12], scalar1=1.0 / HDV,
                                             scalar2=EPS, op0=OP.mult, op1=OP.add), "v")

                @blk.scalar
                def _(s, j=j):
                    self.w(s, "v")
                    self.inc(s.activation(out=self.sccol[:, 8:12], in_=self.sccol[:, 12:16],
                                          func=AF.Ln, bias=self.cc1("zero"),
                                          scale=self.cc1("one")), "s")
                    self.w(s, "s")
                    self.inc(s.activation(out=self.sccol[:, 8:12], in_=self.sccol[:, 8:12],
                                          func=AF.Exp, bias=self.cc1("zero"),
                                          scale=self.cc1("half")), "s")

                @blk.vector
                def _(v, j=j):
                    self.w(v, "s")
                    self.incw(v, v.reciprocal(self.sccol[:, 12:16], self.sccol[:, 8:12]), "v")
                    self.w(v, "v")
                    for h in range(H):
                        sl = slice(h * HDV, (h + 1) * HDV)
                        self.incw(v, v.scalar_tensor_tensor(out=self.sc2[:, sl],
                                                        in0=self.sc1[:, sl],
                                                        scalar=self.sccol[:, 12 + h:13 + h],
                                                        in1=self.t_normw[:, sl],
                                                        op0=OP.mult, op1=OP.mult), "v")

                @blk.tensor
                def _(t, j=j):
                    self.w(t, "v")
                    self.w(t, "d_wg", wg_tok)
                    for dvh in range(2):
                        for dc in range(8):
                            self.inc(t.matmul(self.pW[dvh][:, 0:512],
                                              self.xnT[:, dc, j * 128:(j + 1) * 128],
                                              wgt[:, dc, dvh * 512:(dvh + 1) * 512],
                                              start=(dc == 0), stop=(dc == 7)), "t")

                @blk.scalar
                def _(s, j=j):
                    self.w(s, "t")
                    for dvh in range(2):
                        self.inc(s.activation(out=sg[:, dvh * 512:(dvh + 1) * 512],
                                              in_=self.pW[dvh][:, 0:512], func=AF.Exp,
                                              bias=self.cc1("zero"),
                                              scale=self.cc1("negone")), "s")
                        self.inc(s.activation(out=self.snk[:, dvh * 512:(dvh + 1) * 512],
                                              in_=self.pW[dvh][:, 0:512], func=AF.Copy,
                                              bias=0.0, scale=1.0), "s")

                @blk.vector
                def _(v, j=j):
                    self.w(v, "s")
                    self.incw(v, v.tensor_scalar(out=sg[:], in0=sg[:], scalar1=1.0,
                                             scalar2=None, op0=OP.add), "v")
                    self.incw(v, v.reciprocal(sg[:], sg[:]), "v")
                    self.incw(v, v.tensor_tensor(out=sg[:], in0=sg[:], in1=self.snk[:, 0:DV],
                                             op=OP.mult), "v")

                @blk.vector
                def _(v, j=j):
                    self.w(v, "s")
                    self.incw(v, v.tensor_tensor(out=self.og[:, j, :], in0=self.sc2[:, 0:DV],
                                             in1=sg[:], op=OP.mult), "v")

    def p7_out(self, stack):
        blk = self.block
        wot = self.sbctx(stack, "wot", (128, 8, D), BF16)
        ynT = self.sbctx(stack, "ynT", (128, 8, S), BF16)
        ogT = self.sbctx(stack, "ogT", (128, 8, 128), BF16)
        xb = [self.sbctx(stack, f"p7x{i}", (128, D)) for i in range(2)]
        yb = [self.sbctx(stack, f"p7y{i}", (128, D)) for i in range(2)]
        ynb = [self.sbctx(stack, f"p7n{i}", (128, D), BF16) for i in range(2)]
        pTl = self.sbctx(stack, "pTl", (NE, S))
        zr = self.sbctx(stack, "zr", (128, NE))
        er = self.sbctx(stack, "er", (128, NE))

        @blk.gpsimd
        def _(g):
            self.w(g, "d_wg")
            self.w(g, "t")
            self.w(g, "v")
            self.w(g, "s")
            self.inc(g.dma_start(out=wot[:],
                                 in_=self.w_o_in.rearrange("(c p) o -> p c o", p=128)),
                     "d_wo", 16)
            self.inc(g.dma_start(out=self.wWs[:],
                                 in_=self.Ws_in.rearrange("(c p) o -> p c o", p=128)),
                     "d_wo", 16)

        wo_tok = self.cnt["d_wo"]

        @blk.sync
        def _(q):
            self.w(q, "t")
            self.w(q, "v")
            self.w(q, "s")
            for j in range(2):
                self.inc(q.dma_start(out=xb[j][:],
                                     in_=self.x_in.rearrange("(c p) d -> p c d", p=128)[:, j, :]),
                         f"d_x{j % 2}", 16)

        for j in range(NT):
            xt = xb[j % 2]
            yt = yb[j % 2]
            ynt = ynb[j % 2]

            @blk.tensor
            def _(t, j=j):
                self.w(t, "v")
                self.w(t, "d_wo", wo_tok)
                for dc in range(8):
                    self.inc(t.transpose(self.pT1[:, dc * 128:(dc + 1) * 128],
                                         self.og[:, j, dc * 128:(dc + 1) * 128],
                                         self.t_identb), "t")

            @blk.vector
            def _(v, j=j):
                self.w(v, "t")
                for dc in range(8):
                    self.incw(v, v.tensor_copy(ogT[:, dc, :],
                                           self.pT1[:, dc * 128:(dc + 1) * 128]), "v")

            @blk.tensor
            def _(t, j=j):
                self.w(t, "v")
                for half in range(2):
                    for dc in range(8):
                        self.inc(t.matmul(self.pW[half][:, 0:512],
                                          ogT[:, dc, :],
                                          wot[:, dc, half * 512:(half + 1) * 512],
                                          start=(dc == 0), stop=(dc == 7)), "t")

            @blk.vector
            def _(v, j=j, xt=xt, yt=yt):
                self.w(v, "t")
                self.w(v, f"d_x{j % 2}")
                self.w(v, f"d_y{j % 2}")
                for half in range(2):
                    self.incw(v, v.tensor_tensor(out=yt[:, half * 512:(half + 1) * 512],
                                             in0=self.pW[half][:, 0:512],
                                             in1=xt[:, half * 512:(half + 1) * 512],
                                             op=OP.add), "v")
                self.incw(v, v.tensor_scalar(out=self.snk_v[:, 0:D], in0=yt[:], scalar1=1.0,
                                         scalar2=0.0, op0=OP.mult, op1=OP.add,
                                         accum_out=self.ln_a[:, 0:1]), "v")

            @blk.sync
            def _(q, j=j, yt=yt, xt=xt):
                self.w(q, "v")
                self.inc(q.dma_start(
                    out=self.out.rearrange("(c p) d -> p c d", p=128)[:, j, :],
                    in_=yt[:]), f"d_y{j % 2}", 16)
                if j + 2 < NT:
                    self.inc(q.dma_start(
                        out=xt[:],
                        in_=self.x_in.rearrange("(c p) d -> p c d", p=128)[:, j + 2, :]),
                        f"d_x{j % 2}", 16)

            @blk.scalar
            def _(s, j=j, yt=yt):
                self.w(s, "v")
                self.inc(s.activation(out=self.snk[:, 0:D], in_=yt[:], func=AF.Square,
                                      bias=self.cc1("zero"), scale=self.cc1("one"),
                                      accum_out=self.ln_a[:, 1:2]), "s")

            @blk.vector
            def _(v, j=j):
                self.w(v, "s")
                self.w(v, "v")
                la = self.ln_a
                self.incw(v, v.tensor_scalar(out=la[:, 2:3], in0=la[:, 0:1], scalar1=1.0 / D,
                                         scalar2=None, op0=OP.mult), "v")
                self.incw(v, v.tensor_scalar(out=la[:, 3:4], in0=la[:, 1:2], scalar1=1.0 / D,
                                         scalar2=None, op0=OP.mult), "v")
                self.incw(v, v.tensor_tensor(out=la[:, 4:5], in0=la[:, 2:3], in1=la[:, 2:3],
                                         op=OP.mult), "v")
                self.incw(v, v.tensor_tensor(out=la[:, 5:6], in0=la[:, 3:4], in1=la[:, 4:5],
                                         op=OP.subtract), "v")

            @blk.scalar
            def _(s, j=j):
                self.w(s, "v")
                self.inc(s.activation(out=self.ln_a[:, 6:7], in_=self.ln_a[:, 5:6],
                                      func=AF.Ln, bias=self.cc1("eps"),
                                      scale=self.cc1("one")), "s")
                self.w(s, "s")
                self.inc(s.activation(out=self.ln_a[:, 6:7], in_=self.ln_a[:, 6:7],
                                      func=AF.Exp, bias=self.cc1("zero"),
                                      scale=self.cc1("half")), "s")

            @blk.vector
            def _(v, j=j, yt=yt, ynt=ynt):
                self.w(v, "s")
                self.w(v, f"d_yn{j % 2}")
                self.incw(v, v.reciprocal(self.ln_a[:, 7:8], self.ln_a[:, 6:7]), "v")
                self.w(v, "v")
                self.incw(v, v.tensor_scalar(out=ynt[:], in0=yt[:],
                                         scalar1=self.ln_a[:, 2:3], scalar2=self.ln_a[:, 7:8],
                                         op0=OP.subtract, op1=OP.mult), "v")

            @blk.sync
            def _(q, j=j, ynt=ynt):
                self.w(q, "v")
                self.inc(q.dma_start(
                    out=self.yn_loc.rearrange("(c p) d -> p c d", p=128)[:, j, :],
                    in_=ynt[:]), f"d_yn{j % 2}", 16)

            @blk.tensor
            def _(t, j=j, ynt=ynt):
                self.w(t, "v")
                for dc in range(8):
                    self.inc(t.transpose(self.pT2[:, dc * 128:(dc + 1) * 128],
                                         ynt[:, dc * 128:(dc + 1) * 128],
                                         self.t_identb), "t")

            @blk.vector
            def _(v, j=j):
                self.w(v, "t")
                for dc in range(8):
                    self.incw(v, v.tensor_copy(ynT[:, dc, j * 128:(j + 1) * 128],
                                           self.pT2[:, dc * 128:(dc + 1) * 128]), "v")

            @blk.tensor
            def _(t, j=j):
                self.w(t, "v")
                for dc in range(8):
                    self.inc(t.matmul(self.pW[0][:, 0:8],
                                      ynT[:, dc, j * 128:(j + 1) * 128],
                                      self.wWs[:, dc, :], start=(dc == 0), stop=(dc == 7)), "t")

            @blk.vector
            def _(v, j=j):
                self.w(v, "t")
                self.incw(v, v.tensor_tensor(out=zr[:], in0=self.pW[0][:, 0:8],
                                         in1=self.t_bsbc, op=OP.add), "v")
                self.incw(v, v.tensor_reduce(out=self.sccol[:, 0:1], in_=zr[:], axis=AX.X,
                                         op=OP.max), "v")
                self.incw(v, v.tensor_scalar(out=self.sccol[:, 1:2], in0=self.sccol[:, 0:1],
                                         scalar1=-1.0, scalar2=None, op0=OP.mult), "v")

            @blk.scalar
            def _(s, j=j):
                self.w(s, "v")
                self.inc(s.activation(out=er[:], in_=zr[:], func=AF.Exp,
                                      bias=self.sccol[:, 1:2], scale=self.cc1("one"),
                                      accum_out=self.sccol[:, 2:3]), "s")

            @blk.vector
            def _(v, j=j):
                self.w(v, "s")
                self.incw(v, v.reciprocal(self.sccol[:, 3:4], self.sccol[:, 2:3]), "v")
                self.w(v, "v")
                self.incw(v, v.tensor_scalar(out=er[:], in0=er[:], scalar1=self.sccol[:, 3:4],
                                         scalar2=None, op0=OP.mult), "v")

            @blk.tensor
            def _(t, j=j):
                self.w(t, "v")
                self.inc(t.transpose(self.pM[0:8, 0:128], er[:], self.t_identf), "t")

            @blk.vector
            def _(v, j=j):
                self.w(v, "t")
                self.incw(v, v.tensor_copy(pTl[:, j * 128:(j + 1) * 128], self.pM[0:8, 0:128]), "v")

        @blk.sync
        def _(q):
            self.w(q, "v")
            self.inc(q.dma_start(out=self.pT_loc[:, :], in_=pTl[:]), "d_pt", 16)

        if self.dbg("y"):
            pass

    def p8_collect(self):
        blk = self.block

        @blk.gpsimd
        def _(g):
            self.w(g, "d_yn0")
            self.w(g, "d_yn1")
            self.w(g, "d_pt")
            self.inc(g.collective_compute(
                "AllGather", OP.bypass, replica_groups=[list(range(8))],
                ins=[self.yn_loc[:, :]], outs=[self.yn_all[:, :]]), "coll")
            self.inc(g.collective_compute(
                "AllGather", OP.bypass, replica_groups=[list(range(8))],
                ins=[self.pT_loc[:, :]], outs=[self.pT_all[:, :]]), "coll")

    def p9_route(self, stack):
        blk = self.block
        pc = self.sbctx(stack, "pc", (128, 1, 128))
        self.wrapB = self.sbctx(stack, "wrapB", (128, NTOK // 16), I16)

        @blk.vector
        def _(v):
            self.incw(v, v.memset(self.wrapB[:], 0), "v")
        lo = self.sccol[:, 4:5]
        hi = self.sccol[:, 5:6]
        cntc = self.sccol[:, 6:7]
        gec = self.sccol[:, 7:8]
        idx8 = self.sbctx(stack, "idx8", (128, 8), I16)
        gtf = self.sbctx(stack, "gtf", (128, 128))
        tie = self.sbctx(stack, "tie", (128, 128))
        csr = self.sbctx(stack, "csr", (128, 128))
        pos = self.sbctx(stack, "pos", (128, 128))
        pay = self.sbctx(stack, "pay", (128, 128, 2))
        tblr = self.sbctx(stack, "tblr", (128, 16, 64))

        idxn = self.sbctx(stack, "idxn", (128, 1))

        @blk.vector
        def _(v):
            self.incw(v, v.memset(idx8[:], 0), "v")

        @blk.vector
        def _(v):
            self.w(v, "coll")
            self.incw(v, v.tensor_scalar(out=self.sccol[:, 0:1], in0=self.t_cid, scalar1=16.0,
                                     scalar2=None, op0=OP.mult), "v")
            self.incw(v, v.tensor_tensor(out=idxn[:], in0=self.t_r128s,
                                     in1=self.sccol[:, 0:1], op=OP.add), "v")

        self.wrap_build(128, idxn[:, :], idx8)

        @blk.gpsimd
        def _(g):
            self.w(g, "v")
            self.inc(g.dma_gather(
                out_ap=pc[:, :, :],
                in_ap=self.pT_all.rearrange("e (c j) -> (e c) j", j=128),
                idxs_ap=idx8[:, :], num_idxs=128, num_idxs_reg=128,
                elem_size=128), "d_g", 16)

        @blk.vector
        def _(v):
            self.w(v, "d_g")
            self.incw(v, v.memset(lo, 0.0), "v")
            self.incw(v, v.memset(hi, 1.0), "v")

        for it in range(NBIS):
            @blk.vector
            def _(v, it=it):
                self.w(v, "t")
                self.incw(v, v.tensor_tensor(out=self.sccol[:, 0:1], in0=lo, in1=hi, op=OP.add), "v")
                self.incw(v, v.tensor_scalar(out=self.sccol[:, 0:1], in0=self.sccol[:, 0:1],
                                         scalar1=0.5, scalar2=None, op0=OP.mult), "v")
                self.w(v, "v")
                self.incw(v, v.tensor_scalar(out=self.scu8[:, 0:128], in0=pc[:, 0, :],
                                         scalar1=self.sccol[:, 0:1], scalar2=0.0,
                                         op0=OP.is_gt, op1=OP.add,
                                         accum_out=cntc), "v")

            @blk.tensor
            def _(t, it=it):
                self.w(t, "v")
                self.inc(t.matmul(self.pM[:, 0:1], self.t_ones, cntc,
                                  start=True, stop=True), "t")

            @blk.vector
            def _(v, it=it):
                self.w(v, "t")
                self.incw(v, v.tensor_copy(cntc, self.pM[:, 0:1]), "v")
                self.incw(v, v.tensor_scalar(out=gec, in0=cntc, scalar1=float(KEXP),
                                         scalar2=None, op0=OP.is_ge), "v")
                self.incw(v, v.tensor_tensor(out=self.sccol[:, 2:3], in0=self.sccol[:, 0:1],
                                         in1=lo, op=OP.subtract), "v")
                self.w(v, "v")
                self.incw(v, v.scalar_tensor_tensor(out=lo, in0=self.sccol[:, 2:3],
                                                scalar=gec, in1=lo,
                                                op0=OP.mult, op1=OP.add), "v")
                self.incw(v, v.tensor_scalar(out=self.sccol[:, 3:4], in0=gec, scalar1=-1.0,
                                         scalar2=1.0, op0=OP.mult, op1=OP.add), "v")
                self.incw(v, v.tensor_tensor(out=self.sccol[:, 2:3], in0=self.sccol[:, 0:1],
                                         in1=hi, op=OP.subtract), "v")
                self.w(v, "v")
                self.incw(v, v.scalar_tensor_tensor(out=hi, in0=self.sccol[:, 2:3],
                                                scalar=self.sccol[:, 3:4], in1=hi,
                                                op0=OP.mult, op1=OP.add), "v")

        @blk.vector
        def _(v):
            self.w(v, "v")
            self.incw(v, v.tensor_scalar(out=gtf[:], in0=pc[:, 0, :], scalar1=hi,
                                     scalar2=0.0, op0=OP.is_gt, op1=OP.add,
                                     accum_out=cntc), "v")

        @blk.tensor
        def _(t):
            self.w(t, "v")
            self.inc(t.matmul(self.pM[:, 0:1], self.t_ones, cntc, start=True, stop=True), "t")

        @blk.vector
        def _(v):
            self.w(v, "t")
            self.incw(v, v.tensor_copy(cntc, self.pM[:, 0:1]), "v")
            self.incw(v, v.tensor_scalar(out=self.sccol[:, 2:3], in0=cntc, scalar1=-1.0,
                                     scalar2=float(KEXP), op0=OP.mult, op1=OP.add), "v")
            self.w(v, "v")
            self.incw(v, v.tensor_scalar(out=tie[:], in0=pc[:, 0, :], scalar1=lo,
                                     scalar2=None, op0=OP.is_gt), "v")
            self.incw(v, v.tensor_tensor(out=tie[:], in0=tie[:], in1=gtf[:], op=OP.subtract), "v")
            self.incw(v, v.tensor_tensor_scan(out=csr[:], data0=self.t_onesw[:, 0:128],
                                          data1=tie[:], initial=0.0,
                                          op0=OP.mult, op1=OP.add), "v")
            self.incw(v, v.tensor_copy(self.sccol[:, 0:1], csr[:, 127:128]), "v")

        @blk.tensor
        def _(t):
            self.w(t, "v")
            self.inc(t.matmul(self.pM[:, 0:1], self.t_upst, self.sccol[:, 0:1],
                              start=True, stop=True), "t")

        @blk.vector
        def _(v):
            self.w(v, "t")
            self.incw(v, v.tensor_copy(self.sccol[:, 1:2], self.pM[:, 0:1]), "v")
            self.w(v, "v")
            self.incw(v, v.tensor_scalar(out=csr[:], in0=csr[:], scalar1=self.sccol[:, 1:2],
                                     scalar2=None, op0=OP.add), "v")
            self.incw(v, v.tensor_scalar(out=csr[:], in0=csr[:], scalar1=self.sccol[:, 2:3],
                                     scalar2=None, op0=OP.is_le), "v")
            self.incw(v, v.tensor_tensor(out=tie[:], in0=tie[:], in1=csr[:], op=OP.mult), "v")
            self.incw(v, v.tensor_tensor(out=gtf[:], in0=gtf[:], in1=tie[:], op=OP.add), "v")
            self.incw(v, v.tensor_tensor_scan(out=csr[:], data0=self.t_onesw[:, 0:128],
                                          data1=gtf[:], initial=0.0,
                                          op0=OP.mult, op1=OP.add), "v")
            self.incw(v, v.tensor_copy(self.sccol[:, 0:1], csr[:, 127:128]), "v")

        @blk.tensor
        def _(t):
            self.w(t, "v")
            self.inc(t.matmul(self.pM[:, 0:1], self.t_upst, self.sccol[:, 0:1],
                              start=True, stop=True), "t")

        @blk.vector
        def _(v):
            self.w(v, "t")
            self.incw(v, v.tensor_copy(self.sccol[:, 1:2], self.pM[:, 0:1]), "v")
            self.w(v, "v")
            self.incw(v, v.tensor_scalar(out=pos[:], in0=csr[:], scalar1=self.sccol[:, 1:2],
                                     scalar2=-1.0, op0=OP.add, op1=OP.add), "v")
            self.incw(v, v.tensor_copy(pay[:, :, 0], self.t_iotatg), "v")
            self.incw(v, v.tensor_copy(pay[:, :, 1], pc[:, 0, :]), "v")
            self.incw(v, v.tensor_scalar(out=csr[:], in0=pos[:], scalar1=-2048.0,
                                     scalar2=None, op0=OP.add), "v")
            self.incw(v, v.tensor_tensor(out=csr[:], in0=csr[:], in1=gtf[:], op=OP.mult), "v")
            self.incw(v, v.tensor_scalar(out=csr[:], in0=csr[:], scalar1=2048.0,
                                     scalar2=None, op0=OP.add), "v")

        self.wrap_build(NTOK, csr[:, :], self.wrapB)

        @blk.gpsimd
        def _(g):
            self.w(g, "v")
            self.w(g, "d_z")
            for ck in range(4):
                self.inc(g.dma_scatter_add(out_ap=self.tbl_hbm[:, 0:2],
                                           in_ap=pay[:, 32 * ck:32 * ck + 32, :],
                                           idxs_ap=self.wrapB[:, 256 * ck:256 * ck + 256],
                                           num_idxs=4096,
                                           num_idxs_reg=4096, elem_size=2, elem_step=64,
                                           single_packet=False), "d_g", 16)
                self.w(g, "d_g")
            self.inc(g.dma_start(
                out=tblr[:, :, :],
                in_=self.tbl_hbm[0:KEXP, :].rearrange("(c p) k -> p c k", p=128)),
                "d_g", 16)

        @blk.vector
        def _(v):
            self.w(v, "d_g")
            self.incw(v, v.tensor_copy(self.tid[:, :], tblr[:, :, 0]), "v")
            self.incw(v, v.tensor_copy(self.tval[:, :], tblr[:, :, 1]), "v")

        self.wrap_build(KEXP, self.tid[:, :], self.wrapC)

        @blk.sync
        def _(q):
            self.w(q, "v")
            self.inc(q.dma_start(out=self.ids_loc[:, :], in_=self.tid[:, :]), "d_ids", 16)

        if self.dbg("pc"):
            dst = self.dbg_out("pc", (128, 128))

            @blk.sync
            def _(q):
                self.w(q, "v")
                self.inc(q.dma_start(out=dst[:, :], in_=pc[:, 0, :]), "d_dbg", 16)
        if self.dbg("sel"):
            dst = self.dbg_out("sel", (128, 128))

            @blk.sync
            def _(q):
                self.w(q, "v")
                self.inc(q.dma_start(out=dst[:, :], in_=gtf[:, :]), "d_dbg", 16)
        if self.dbg("omout") or self.dbg("dbg9"):
            dst = self.dbg_out("tid", (128, 16))
            dst2 = self.dbg_out("tval", (128, 16))

            @blk.sync
            def _(q):
                self.w(q, "v")
                self.inc(q.dma_start(out=dst[:, :], in_=self.tid[:, :]), "d_dbg", 16)
                self.inc(q.dma_start(out=dst2[:, :], in_=self.tval[:, :]), "d_dbg", 16)

    def p10_ffn(self, stack):
        blk = self.block
        w2t = self.sbctx(stack, "w2t", (128, 32, D), BF16)
        hT = self.sbctx(stack, "hT", (128, 32, TBF), BF16)
        xg = self.sbctx(stack, "xg", (128, 4, D), BF16)
        xinT = self.sbctx(stack, "xinT", (128, 8, TBF), BF16)
        w1t = [self.sbctx(stack, f"w1t{i}", (128, 8, 128), BF16) for i in range(2)]
        ob = [self.sbctx(stack, f"ob{i}", (128, D)) for i in range(2)]

        @blk.gpsimd
        def _(g):
            self.w(g, "d_wo")
            self.w(g, "t")
            self.w(g, "v")
            self.w(g, "s")
            self.inc(g.dma_start(out=w2t[:],
                                 in_=self.W2_in.rearrange("(c p) o -> p c o", p=128)),
                     "d_w2", 16)

        w2_tok = self.cnt["d_w2"]

        for tb in range(NTB):
            @blk.gpsimd
            def _(g, tb=tb):
                self.w(g, "v")
                self.inc(g.dma_gather(out_ap=xg[:, :, :], in_ap=self.yn_all[:, :],
                                      idxs_ap=self.wrapC[:, 32 * tb:32 * tb + 32],
                                      num_idxs=TBF, num_idxs_reg=TBF, elem_size=D),
                         "d_g", 16)

            for sb4 in range(4):
                @blk.tensor
                def _(t, tb=tb, sb4=sb4):
                    self.w(t, "d_g")
                    self.w(t, "v")
                    for dc in range(8):
                        self.inc(t.transpose(self.pT1[:, dc * 128:(dc + 1) * 128],
                                             xg[:, sb4, dc * 128:(dc + 1) * 128],
                                             self.t_identb), "t")

                @blk.vector
                def _(v, tb=tb, sb4=sb4):
                    self.w(v, "t")
                    for dc in range(8):
                        self.incw(v, v.tensor_copy(xinT[:, dc, sb4 * 128:(sb4 + 1) * 128],
                                               self.pT1[:, dc * 128:(dc + 1) * 128]), "v")

            @blk.gpsimd
            def _(g, tb=tb):
                self.w(g, "v")
                self.w(g, "t")
                self.inc(g.dma_start(
                    out=w1t[0][:, :, :],
                    in_=self.w1bf_hbm[0:D, :].rearrange("(c p) k -> p c k", p=128)),
                    "d_w1_0", 16)

            for fb in range(32):
                @blk.gpsimd
                def _(g, tb=tb, fb=fb):
                    if fb + 1 < 32:
                        self.w(g, "t")
                        self.inc(g.dma_start(
                            out=w1t[(fb + 1) % 2][:, :, :],
                            in_=self.w1bf_hbm[(fb + 1) * D:(fb + 2) * D, :].rearrange(
                                "(c p) k -> p c k", p=128)), f"d_w1_{(fb + 1) % 2}", 16)

                @blk.tensor
                def _(t, tb=tb, fb=fb):
                    self.w(t, f"d_w1_{fb % 2}")
                    self.w(t, "s")
                    for dc in range(8):
                        self.inc(t.matmul(self.pW[fb % 2][:, 0:TBF],
                                          w1t[fb % 2][:, dc, :],
                                          xinT[:, dc, :],
                                          start=(dc == 0), stop=(dc == 7)), "t")

                @blk.scalar
                def _(s, tb=tb, fb=fb):
                    self.w(s, "t")
                    self.inc(s.activation(out=hT[:, fb, :], in_=self.pW[fb % 2][:, 0:TBF],
                                          func=AF.Gelu_apprx_tanh,
                                          bias=self.t_b1c[:, fb:fb + 1],
                                          scale=self.cc1("one")), "s")

            for obi in range(TBF // 128):
                qi = tb * 4 + obi

                @blk.tensor
                def _(t, tb=tb, obi=obi):
                    self.w(t, "s")
                    self.w(t, "d_w2", w2_tok)
                    self.w(t, "v")
                    for half in range(2):
                        for fb in range(32):
                            self.inc(t.matmul(self.pW[half][:, 0:512],
                                              hT[:, fb, obi * 128:(obi + 1) * 128],
                                              w2t[:, fb, half * 512:(half + 1) * 512],
                                              start=(fb == 0), stop=(fb == 31)), "t")

                @blk.vector
                def _(v, tb=tb, obi=obi, qi=qi):
                    self.w(v, "t")
                    self.w(v, f"d_ob{qi % 2}")
                    for half in range(2):
                        sl = slice(half * 512, (half + 1) * 512)
                        self.incw(v, v.tensor_tensor(out=self.sc1[:, sl],
                                                 in0=self.pW[half][:, 0:512],
                                                 in1=self.t_b2bc[:, sl], op=OP.add), "v")
                    self.incw(v, v.tensor_scalar(out=ob[qi % 2][:], in0=self.sc1[:, 0:D],
                                             scalar1=self.tval[:, qi:qi + 1],
                                             scalar2=None, op0=OP.mult), "v")

                @blk.sync
                def _(q, qi=qi):
                    self.w(q, "v")
                    self.inc(q.dma_start(
                        out=self.obuf_loc.rearrange("(c p) d -> p c d", p=128)[:, qi, :],
                        in_=ob[qi % 2][:]), f"d_ob{qi % 2}", 16)
                    if self.dbg("obdbg"):
                        if not hasattr(self, "_obdbg"):
                            self._obdbg = self.dbg_out("obuf", (KEXP, D))
                        self.inc(q.dma_start(
                            out=self._obdbg.rearrange("(c p) d -> p c d", p=128)[:, qi, :],
                            in_=ob[qi % 2][:]), f"d_ob{qi % 2}", 16)

    def p11_collect(self):
        blk = self.block

        @blk.gpsimd
        def _(g):
            self.w(g, "d_ob0")
            self.w(g, "d_ob1")
            self.w(g, "d_ids")
            self.inc(g.collective_compute(
                "AllGather", OP.bypass, replica_groups=[list(range(8))],
                ins=[self.obuf_loc[:, :]], outs=[self.obuf_all[:, :]]), "coll")
            self.inc(g.collective_compute(
                "AllGather", OP.bypass, replica_groups=[list(range(8))],
                ins=[self.ids_loc[:, :]], outs=[self.ids_all[:, :]]), "coll")

    def p12_combine(self, stack):
        blk = self.block
        idsb = self.sbctx(stack, "idsb", (128, 8, 16))
        self.wrapB = self.sbctx(stack, "wrapB2", (128, NTOK // 16), I16)

        @blk.vector
        def _(v):
            self.incw(v, v.memset(self.wrapB[:], 0), "v")
        lid = self.sbctx(stack, "lid", (128, 128))
        mine = self.sbctx(stack, "mine", (128, 128))
        pos = self.sbctx(stack, "pos2", (128, 128))
        pay = self.sbctx(stack, "pay2", (128, 128, 2))
        tblr = self.sbctx(stack, "tblr2", (128, 16, 64))
        gidx = self.sbctx(stack, "gidx", (128, 16))
        lidx = self.sbctx(stack, "lidx", (128, 16))
        wrapD = self.sbctx(stack, "wrapD", (128, 128), I16)
        wrapE = self.sbctx(stack, "wrapE", (128, 128), I16)
        cbuf = self.sbctx(stack, "cbuf", (128, 16, D))
        TRASH = 8 * 2048 + 8

        @blk.sync
        def _(q):
            self.w(q, "coll")
            self.inc(q.dma_start(out=idsb[:, :, :],
                                 in_=self.ids_all.rearrange("(e p) c -> p e c", p=128)),
                     "d_tx", 16)

        @blk.vector
        def _(v):
            self.incw(v, v.memset(cbuf[:, :, :], 0.0), "v")
            self.incw(v, v.memset(wrapD[:], 0), "v")
            self.incw(v, v.memset(wrapE[:], 0), "v")
            self.w(v, "d_tx")
            ids128 = idsb.rearrange("p e c -> p (e c)")
            self.incw(v, v.tensor_scalar(out=self.sccol[:, 0:1], in0=self.t_cid, scalar1=2048.0,
                                     scalar2=None, op0=OP.mult), "v")
            self.incw(v, v.tensor_scalar(out=lid[:], in0=ids128, scalar1=self.sccol[:, 0:1],
                                     scalar2=None, op0=OP.subtract), "v")
            self.incw(v, v.tensor_scalar(out=mine[:], in0=lid[:], scalar1=0.0,
                                     scalar2=None, op0=OP.is_ge), "v")
            self.incw(v, v.tensor_scalar(out=pos[:], in0=lid[:], scalar1=2048.0,
                                     scalar2=None, op0=OP.is_lt), "v")
            self.incw(v, v.tensor_tensor(out=mine[:], in0=mine[:], in1=pos[:], op=OP.mult), "v")
            self.incw(v, v.tensor_copy(pay[:, :, 0], self.t_iotarows), "v")
            self.incw(v, v.tensor_copy(pay[:, :, 1], lid[:]), "v")

        for e in range(NE):
            sl = slice(e * 16, (e + 1) * 16)

            @blk.vector
            def _(v, e=e, sl=sl):
                self.incw(v, v.tensor_tensor_scan(out=pos[:, sl], data0=self.t_onesw[:, 0:16],
                                              data1=mine[:, sl], initial=0.0,
                                              op0=OP.mult, op1=OP.add), "v")
                self.incw(v, v.tensor_copy(self.sccol[:, 6:7], pos[:, sl][:, 15:16]), "v")

            @blk.tensor
            def _(t, e=e):
                self.w(t, "v")
                self.inc(t.matmul(self.pM[:, 0:1], self.t_upst, self.sccol[:, 6:7],
                                  start=True, stop=True), "t")
                self.inc(t.matmul(self.pM[:, 1:2], self.t_ones, self.sccol[:, 6:7],
                                  start=True, stop=True), "t")

            @blk.vector
            def _(v, e=e, sl=sl):
                self.w(v, "t")
                self.incw(v, v.tensor_copy(self.sccol[:, 7:8], self.pM[:, 0:1]), "v")
                self.incw(v, v.tensor_copy(self.sccol[:, 8:9], self.pM[:, 1:2]), "v")
                self.incw(v, v.tensor_copy(self.cnt8[:, e:e + 1], self.sccol[:, 8:9]), "v")
                self.incw(v, v.tensor_scalar(out=self.sccol[:, 9:10], in0=self.sccol[:, 8:9],
                                         scalar1=1.0, scalar2=None, op0=OP.add), "v")
                self.incw(v, v.tensor_copy(self.sci32[0:1, e:e + 1], self.sccol[0:1, 9:10]), "v")
                self.incw(v, v.tensor_scalar(out=pos[:, sl], in0=pos[:, sl],
                                         scalar1=self.sccol[:, 7:8],
                                         scalar2=float(e * 2048 - TRASH),
                                         op0=OP.add, op1=OP.add), "v")
                self.incw(v, v.tensor_tensor(out=pos[:, sl], in0=pos[:, sl], in1=mine[:, sl],
                                         op=OP.mult), "v")
                self.incw(v, v.tensor_scalar(out=pos[:, sl], in0=pos[:, sl],
                                         scalar1=float(TRASH), scalar2=None, op0=OP.add), "v")

        self.wrap_build(NTOK, pos[:, :], self.wrapB)

        @blk.gpsimd
        def _(g):
            self.w(g, "v")
            self.w(g, "d_z")
            for ck in range(4):
                self.inc(g.dma_scatter_add(out_ap=self.tbl2_hbm[:, 0:2],
                                           in_ap=pay[:, 32 * ck:32 * ck + 32, :],
                                           idxs_ap=self.wrapB[:, 256 * ck:256 * ck + 256],
                                           num_idxs=4096,
                                           num_idxs_reg=4096, elem_size=2, elem_step=64,
                                           single_packet=False), "d_g", 16)
                self.w(g, "d_g")

        for e in range(NE):
            @blk.sync
            def _(q, e=e):
                self.w(q, "d_g")
                self.w(q, "d_comb")
                self.w(q, "v")
                self.inc(q.dma_start(
                    out=tblr[:, :, :],
                    in_=self.tbl2_hbm[e * 2048:(e + 1) * 2048, :].rearrange(
                        "(c p) k -> p c k", p=128)), "d_tx", 16)

            @blk.vector
            def _(v, e=e):
                self.w(v, "d_tx")
                msk = self.sc3
                d0 = self.sc4
                self.incw(v, v.tensor_scalar(out=msk[:, 0:16], in0=self.t_iotacm[:, 0:16],
                                         scalar1=self.cnt8[:, e:e + 1], scalar2=None,
                                         op0=OP.is_le), "v")
                self.incw(v, v.tensor_scalar(out=msk[:, 16:32], in0=self.t_iotacm[:, 0:16],
                                         scalar1=1.0, scalar2=None, op0=OP.is_ge), "v")
                self.incw(v, v.tensor_tensor(out=msk[:, 0:16], in0=msk[:, 0:16],
                                         in1=msk[:, 16:32], op=OP.mult), "v")
                self.incw(v, v.tensor_scalar(out=d0[:, 0:16], in0=self.t_iotacm[:, 0:16],
                                         scalar1=0.0, scalar2=None, op0=OP.is_equal), "v")
                for (dst, colk) in ((gidx, 0), (lidx, 1)):
                    self.incw(v, v.tensor_scalar(out=dst[:], in0=tblr[:, :, colk], scalar1=1.0,
                                             scalar2=None, op0=OP.add), "v")
                    self.incw(v, v.tensor_tensor(out=dst[:], in0=dst[:], in1=msk[:, 0:16],
                                             op=OP.mult), "v")
                    self.incw(v, v.tensor_tensor(out=dst[:], in0=dst[:], in1=d0[:, 0:16],
                                             op=OP.add), "v")
                    self.incw(v, v.tensor_scalar(out=dst[:], in0=dst[:], scalar1=-1.0,
                                             scalar2=None, op0=OP.add), "v")

            self.wrap_build(2048, gidx[:, :], wrapD)
            self.wrap_build(2048, lidx[:, :], wrapE)

            @blk.gpsimd
            def _(g, e=e):
                self.w(g, "v")
                self.w(g, "d_y0")
                self.w(g, "d_y1")
                with g.register(f"creg{e}") as reg:
                    g.load(reg, self.sci32[0:1, e:e + 1])
                    self.inc(g.dma_gather(out_ap=cbuf[:, :, :], in_ap=self.obuf_all[:, :],
                                          idxs_ap=wrapD[:, :], num_idxs=2048,
                                          num_idxs_reg=reg, elem_size=D,
                                          single_packet=False), "d_g", 16)
                    self.w(g, "d_g")
                    self.inc(g.nop(), "g")

            @blk.vector
            def _(v, e=e):
                self.w(v, "g")
                self.incw(v, v.memset(cbuf[0:1, 0, 0:D], 0.0), "v")

            @blk.gpsimd
            def _(g, e=e):
                self.w(g, "v")
                with g.register(f"creg2_{e}") as reg:
                    g.load(reg, self.sci32[0:1, e:e + 1])
                    self.inc(g.dma_scatter_add(out_ap=self.out[:, :], in_ap=cbuf[:, :, :],
                                               idxs_ap=wrapE[:, :], num_idxs=2048,
                                               num_idxs_reg=reg, elem_size=D,
                                               single_packet=False), "d_comb", 16)
                    self.w(g, "d_comb")

    def finale(self):
        blk = self.block

        @blk.gpsimd
        def _(g):
            for name in list(self._sems):
                if name.startswith("d_") or name in ("v", "s", "t", "g", "coll"):
                    self.w(g, name)

    def build(self):
        from contextlib import ExitStack
        self.declare()
        with ExitStack() as top:
            self.block = top.enter_context(self.nc.Block())
            for nm in ("d_y0", "d_y1", "d_yn0", "d_yn1", "d_ob0", "d_ob1",
                       "d_om", "d_sh", "d_g", "d_wm", "coll"):
                self.sem(nm)
            self.sem("t")
            self.sem("v")
            self.sem("s")
            self.sem("g")
            self.p0_setup()
            if self.upto >= 10:
                self.p0b_precast_w1()
            if self.upto >= 1:
                with ExitStack() as stA:
                    self.xnT = self.sbctx(stA, "xnT", (128, 8, S), BF16)
                    self.p1_ln_outer(stA)
                    if self.upto >= 6:
                        with ExitStack() as st67:
                            self.og = self.sbctx(st67, "og", (128, NT, DV), BF16)
                            with ExitStack() as st:
                                self.p6_reco(st)
                            if self.upto >= 7:
                                with ExitStack() as st:
                                    self.p7_out(st)
            if self.upto >= 8:
                self.p8_collect()
            if self.upto >= 9:
                self.phase_barrier()
                with ExitStack() as st:
                    self.p9_route(st)
            if self.upto >= 10:
                self.phase_barrier()
                with ExitStack() as st:
                    self.p10_ffn(st)
            if self.upto >= 11:
                self.p11_collect()
            if self.upto >= 12:
                self.phase_barrier()
                with ExitStack() as st:
                    self.p12_combine(st)
            self.finale()
        lower_extended_insts(self.nc)
        return self.nc

    def p1_ln_outer(self, st_outer):
        from contextlib import ExitStack
        with ExitStack() as st:
            self.p1_ln(st)
            if self.upto >= 2:
                self.p2_gate(st)
        if self.upto >= 4:
            with ExitStack() as st:
                self.p_gla(st)

    def p0_setup(self):
        blk = self.block

        @blk.sync
        def _(q):
            for dst, src in [
                (self.t_cc, self.cc_in), (self.t_identf, self.identf_in),
                (self.t_identb, self.identb_in), (self.t_triu, self.triu_in),
                (self.t_upst, self.upst_in), (self.t_ones, self.onesm_in),
                (self.t_iota4, self.iota4_in), (self.t_iotatg, self.iotatg_in),
                (self.t_iotacm, self.iotacm_in), (self.t_r128s, self.r128s_in),
                (self.t_iotarows, self.iotarows_in), (self.t_onesw, self.onesw_in),
                (self.t_normw, self.normw_in), (self.t_bsbc, self.bs_in),
                (self.t_b2bc, self.b2_in), (self.t_b1c, self.b1_in),
                (self.t_bgk, self.bgk_in), (self.t_bgks, self.bgks_in),
                (self.t_cid1, self.cid_in),
            ]:
                self.inc(q.dma_start(out=dst[:], in_=src[:, :]), "d_c", 16)

        @blk.vector
        def _(v):
            self.w(v, "d_c")
            self.inc(v.memset(self.zf32[:], 0.0), "v")
            for wt in (self.wrap0, self.wrap1, self.wrapC):
                self.inc(v.memset(wt[:], 0), "v")

        @blk.scalar
        def _(s):
            self.w(s, "d_c")

        @blk.tensor
        def _(t):
            self.w(t, "d_c")

        @blk.gpsimd
        def _(g):
            g.load_library(MLP_LIB)
            self.inc(g.dma_start(out=self.wgate[:],
                                 in_=self.w_gate_in.rearrange("(c p) o -> p c o", p=128)),
                     "d_w", 16)
            self.inc(g.dma_start(out=self.wq[:],
                                 in_=self.w_q_in.rearrange("(c p) o -> p c o", p=128)),
                     "d_w", 16)
            self.w(g, "d_c")
            g.partition_broadcast(self.t_cid[:], self.t_cid1[0:1, 0:1])
            self.inc(g.nop(), "g")
            self.w(g, "v")
            for r0 in range(0, NMEM * CAP, 128):
                self.inc(g.dma_start(
                    out=self.tx_hbm[r0:r0 + 128, :],
                    in_=self.zf32[:, :]), "d_z", 16)
            self.inc(g.dma_start(out=self.tx_hbm[NMEM * CAP:NMEM * CAP + 8, :],
                                 in_=self.zf32[0:8, :]), "d_z", 16)
            self.inc(g.dma_start(out=self.om_hbm[NMEM * CAP:NMEM * CAP + 8, :],
                                 in_=self.zf32[0:8, :]), "d_z", 16)
            self.inc(g.dma_start(
                out=self.tbl_hbm[0:KEXP, :].rearrange("(c p) k -> p c k", p=128),
                in_=self.zf32.rearrange("p (c k) -> p c k", k=64)), "d_z", 16)
            self.inc(g.dma_start(out=self.tbl_hbm[KEXP:KEXP + 1, :],
                                 in_=self.zf32[0:1, 0:64]), "d_z", 16)
            for r0 in range(0, 20480, 2048):
                self.inc(g.dma_start(
                    out=self.tbl2_hbm[r0:r0 + 2048, :].rearrange("(c p) k -> p c k", p=128),
                    in_=self.zf32.rearrange("p (c k) -> p c k", k=64)), "d_z", 16)


    def p0b_precast_w1(self):
        blk = self.block

        @blk.gpsimd
        def _(g):
            for dc in range(8):
                for ch in range(2):
                    self.inc(g.dma_start(
                        out=self.t_w1s[:, :],
                        in_=self.W1_in[dc * 128:(dc + 1) * 128,
                                       ch * 2048:(ch + 1) * 2048]), "d_w1a", 16)
                    self.w(g, "d_w1a")
                    self.inc(g.dma_start(
                        out=self.w1bf_hbm.rearrange("(f c p) k -> c p f k", c=8, p=128)[
                            dc][:, ch * 16:(ch + 1) * 16, :],
                        in_=self.t_w1s.rearrange("p (f k) -> p f k", k=128)), "d_w1b", 16)
                    self.w(g, "d_w1b")


_NC = None


def kernel(**inputs):
    global _NC
    in_maps = host_inputs(inputs)
    if _NC is None:
        _NC = MK(upto=12).build()
    res = run_bass_kernel_spmd(_NC, in_maps, core_ids=list(range(8)))
    out = np.stack([np.asarray(r["out"], np.float32) for r in res.results], 0)
    return out.reshape(8, 2048, 1024)


# version 03
# speedup vs baseline: 2.5604x; 2.5604x over previous
from concourse.bass_utils import run_bass_kernel_spmd

"""Megakernel builder: GLA mixture-of-memories attention + expert-choice MoE FFN.

One raw-Bass SPMD program (identical across cores). Core c's in_map carries batch
row c (attention) and expert c's FFN weights. Cross-core via AllGather. NOTES.md
has the full design. v1: conservative (serializing) semaphore discipline.
"""
import numpy as np
import concourse.bass as bass
import concourse.mybir as mybir
from concourse.library_config import mlp as MLP_LIB
from concourse.library_overlay import lower_extended_insts

F32 = mybir.dt.float32
BF16 = mybir.dt.bfloat16
I16 = mybir.dt.int16
I32 = mybir.dt.int32
U8 = mybir.dt.uint8
AF = mybir.ActivationFunctionType
OP = mybir.AluOpType
AX = mybir.AxisListType

S, D = 2048, 1024
NMEM, CAP, H = 4, 1024, 4
DK, DV, HDK, HDV, LOW = 512, 1024, 128, 256, 16
NE, FF, KEXP, NTOK = 8, 4096, 2048, 16384
NT = S // 128
GATE_NORM, EPS = 16.0, 1e-5
QSCALE = float(HDK) ** -0.5
TBF = 512
NTB = KEXP // TBF
NBIS = 44

CC = dict(zero=0, one=1, eps=2, lnqs=3, inv16=4, neg16=5, negone=6, half=7)
NCC = 8


def host_constants():
    import ml_dtypes
    cc = np.zeros((128, NCC), np.float32)
    cc[:, CC["one"]] = 1.0
    cc[:, CC["eps"]] = EPS
    cc[:, CC["lnqs"]] = np.log(QSCALE)
    cc[:, CC["inv16"]] = 1.0 / GATE_NORM
    cc[:, CC["neg16"]] = -1.0 / GATE_NORM
    cc[:, CC["negone"]] = -1.0
    cc[:, CC["half"]] = 0.5
    p = np.arange(128)
    return {
        "cc": cc,
        "ident_f": np.eye(128, dtype=np.float32),
        "ident_b": np.eye(128, dtype=ml_dtypes.bfloat16),
        "triu_inc": np.triu(np.ones((128, 128), np.float32)).astype(ml_dtypes.bfloat16),
        "upper_strict": np.triu(np.ones((128, 128), np.float32), 1),
        "ones_mat": np.ones((128, 128), np.float32),
        "iota4_bc": np.tile(np.arange(4, dtype=np.float32), (128, 1)),
        "iota_tg": (128.0 * p[:, None] + np.arange(128)[None, :]).astype(np.float32),
        "iota_cm": (p[:, None] + 128.0 * np.arange(128)[None, :]).astype(np.float32),
        "r128s": (128.0 * (p // 16) + (p % 16)).astype(np.float32).reshape(128, 1),
        "iota_rows": ((np.arange(128)[None, :] // 16) * 2048.0
                      + (np.arange(128)[None, :] % 16) * 128.0
                      + p[:, None] * 1.0).astype(np.float32),
        "ones_w": np.ones((128, S), ml_dtypes.bfloat16),
    }


def host_inputs(inputs):
    consts = host_constants()
    x = np.asarray(inputs["L0_attn_x"], np.float32)
    bgk = np.asarray(inputs["L0_attn_b_gk"], np.float32)
    bgk_t = np.zeros((128, NMEM * H), np.float32)
    for m in range(NMEM):
        for h in range(H):
            bgk_t[:, m * H + h] = -bgk[m, h * 128:(h + 1) * 128]
    bgks = np.asarray(inputs["L0_attn_b_gks"], np.float32)
    bgks_t = np.zeros((128, H), np.float32)
    for h in range(H):
        bgks_t[:, h] = -bgks[h * 128:(h + 1) * 128]
    normw = np.asarray(inputs["L0_attn_norm_w"], np.float32)
    b1 = np.asarray(inputs["L0_ffn_b1"], np.float32)
    b2 = np.asarray(inputs["L0_ffn_b2"], np.float32)
    W1 = np.asarray(inputs["L0_ffn_W1"], np.float32)
    W2 = np.asarray(inputs["L0_ffn_W2"], np.float32)
    base = {
        "w_gate": np.asarray(inputs["L0_attn_w_gate"], np.float32),
        "w_q": np.asarray(inputs["L0_attn_w_q"], np.float32),
        "w_k": np.asarray(inputs["L0_attn_w_k"], np.float32),
        "w_v": np.asarray(inputs["L0_attn_w_v"], np.float32),
        "w_gk1": np.asarray(inputs["L0_attn_w_gk1"], np.float32),
        "w_gk2": np.asarray(inputs["L0_attn_w_gk2"], np.float32),
        "b_gk_t": bgk_t,
        "w_ks": np.asarray(inputs["L0_attn_w_ks"], np.float32),
        "w_vs": np.asarray(inputs["L0_attn_w_vs"], np.float32),
        "w_gk1s": np.asarray(inputs["L0_attn_w_gk1s"], np.float32),
        "w_gk2s": np.asarray(inputs["L0_attn_w_gk2s"], np.float32),
        "b_gks_t": bgks_t,
        "w_g": np.asarray(inputs["L0_attn_w_g"], np.float32),
        "w_o": np.asarray(inputs["L0_attn_w_o"], np.float32),
        "normw_bc": np.tile(np.tile(normw, H)[None, :], (128, 1)).astype(np.float32),
        "ffn_Ws": np.asarray(inputs["L0_ffn_Ws"], np.float32),
        "ffn_bs_bc": np.tile(np.asarray(inputs["L0_ffn_bs"], np.float32)[None, :], (128, 1)),
    }
    base.update(consts)
    in_maps = []
    for c in range(8):
        m = dict(base)
        m["x"] = np.ascontiguousarray(x[c])
        m["ffn_W1"] = np.ascontiguousarray(W1[c])
        m["ffn_b1c"] = np.ascontiguousarray(b1[c].reshape(FF // 128, 128).T)
        m["ffn_W2"] = np.ascontiguousarray(W2[c])
        m["ffn_b2_bc"] = np.tile(b2[c][None, :], (128, 1)).astype(np.float32)
        m["coreid"] = np.full((1, 1), float(c), np.float32)
        in_maps.append(m)
    return in_maps


class MK:
    def __init__(self, upto=99, debug=(), gla_order=(4, 0, 1, 2, 3), p12mode="full"):
        self.p12mode = p12mode
        self.gla_order = list(gla_order)
        self.upto = upto
        self.debug = set(debug)
        self.dbg_t = {}
        self.nc = bass.Bass("TRN2", target_bir_lowering=False, debug=False,
                            detect_race_conditions=True)
        self._sems = {}
        self.cnt = {}

    def sem(self, name):
        if name not in self._sems:
            self._sems[name] = self.nc.alloc_semaphore(name)
            self.cnt[name] = 0
        return self._sems[name]

    def inc(self, instr, name, k=1):
        h = self.sem(name)
        instr.then_inc(h, k)
        self.cnt[name] += k
        return self.cnt[name]

    def incw(self, eng, instr, name, k=1):
        n = self.inc(instr, name, k)
        eng.wait_ge(self.sem(name), n)
        return n

    def w(self, eng, name, val=None):
        h = self.sem(name)
        v = self.cnt[name] if val is None else val
        if v > 0:
            eng.wait_ge(h, v)

    def din(self, name, shape, dtype=F32):
        return self.nc.dram_tensor(name, list(shape), dtype, kind="ExternalInput").ap()

    def dout(self, name, shape, dtype=F32):
        return self.nc.dram_tensor(name, list(shape), dtype, kind="ExternalOutput").ap()

    def dint(self, name, shape, dtype=F32):
        kind = "ExternalOutput" if "omout" in self.debug else "Internal"
        return self.nc.dram_tensor(name, list(shape), dtype, kind=kind).ap()

    def sb(self, name, shape, dtype=F32):
        return self.nc.alloc_sbuf_tensor(name, list(shape), dtype).ap()

    def sbctx(self, stack, name, shape, dtype=F32):
        t = stack.enter_context(self.nc.sbuf_tensor(name, list(shape), dtype))
        return t.ap() if hasattr(t, "ap") else t[:]

    def dbg(self, name):
        return name in self.debug

    def dbg_out(self, name, shape, dtype=F32):
        self.dbg_t[name] = self.dout("dbg_" + name, shape, dtype)
        return self.dbg_t[name]

    def cc1(self, key):
        return self.t_cc[:, CC[key]:CC[key] + 1]

    def dump_tile(self, tile_ap, name, shape, dtype=F32):
        if not self.dbg(name):
            return
        dst = self.dbg_out(name, shape, dtype)
        blk = self.block

        @blk.sync
        def _(q):
            self.w(q, "v")
            self.w(q, "s")
            self.inc(q.dma_start(out=dst[:, :] if len(shape) == 2 else dst[:, :, :],
                                 in_=tile_ap), "d_dbg", 16)

    def declare(self):
        self.x_in = self.din("x", (S, D))
        self.w_gate_in = self.din("w_gate", (D, NMEM))
        self.w_q_in = self.din("w_q", (D, DK))
        self.w_k_in = self.din("w_k", (NMEM, D, DK))
        self.w_v_in = self.din("w_v", (NMEM, D, DV))
        self.w_gk1_in = self.din("w_gk1", (NMEM, D, LOW))
        self.w_gk2_in = self.din("w_gk2", (NMEM, LOW, DK))
        self.bgk_in = self.din("b_gk_t", (128, NMEM * H))
        self.w_ks_in = self.din("w_ks", (D, DK))
        self.w_vs_in = self.din("w_vs", (D, DV))
        self.w_gk1s_in = self.din("w_gk1s", (D, LOW))
        self.w_gk2s_in = self.din("w_gk2s", (LOW, DK))
        self.bgks_in = self.din("b_gks_t", (128, H))
        self.w_g_in = self.din("w_g", (D, DV))
        self.w_o_in = self.din("w_o", (DV, D))
        self.normw_in = self.din("normw_bc", (128, DV))
        self.Ws_in = self.din("ffn_Ws", (D, NE))
        self.bs_in = self.din("ffn_bs_bc", (128, NE))
        self.W1_in = self.din("ffn_W1", (D, FF))
        self.b1_in = self.din("ffn_b1c", (128, FF // 128))
        self.W2_in = self.din("ffn_W2", (FF, D))
        self.b2_in = self.din("ffn_b2_bc", (128, D))
        self.cid_in = self.din("coreid", (1, 1))
        self.cc_in = self.din("cc", (128, NCC))
        self.identf_in = self.din("ident_f", (128, 128))
        self.identb_in = self.din("ident_b", (128, 128), BF16)
        self.triu_in = self.din("triu_inc", (128, 128), BF16)
        self.upst_in = self.din("upper_strict", (128, 128))
        self.onesm_in = self.din("ones_mat", (128, 128))
        self.iota4_in = self.din("iota4_bc", (128, 4))
        self.iotatg_in = self.din("iota_tg", (128, 128))
        self.iotacm_in = self.din("iota_cm", (128, 128))
        self.r128s_in = self.din("r128s", (128, 1))
        self.iotarows_in = self.din("iota_rows", (128, 128))
        self.onesw_in = self.din("ones_w", (128, S), BF16)

        self.out = self.dout("out", (S, D))

        self.tx_hbm = self.dint("tx_hbm", (NMEM * CAP + 8, D), BF16)
        self.om_hbm = self.dint("om_hbm", (NMEM * CAP + 8, D), BF16)
        self.sh_hbm = self.dint("sh_hbm", (S, D), BF16)
        self.w1bf_hbm = self.dint("w1bf_hbm", (FF // 128 * D, 128), BF16)
        self.yn_loc = self.dint("yn_loc", (S, D), BF16)
        self.yn_all = self.dint("yn_all", (NTOK, D), BF16)
        self.pT_loc = self.dint("pT_loc", (NE, S))
        self.pT_all = self.dint("pT_all", (NE * NE, S))
        self.h9_hbm = self.dint("h9_hbm", (128, 256), BF16)
        self.dense_hbm = self.dint("dense_hbm", (NTOK, D))
        self.dense_rs = self.dint("dense_rs", (S, D))

        self.t_cc = self.sb("t_cc", (128, NCC))
        self.t_identf = self.sb("t_identf", (128, 128))
        self.t_identb = self.sb("t_identb", (128, 128), BF16)
        self.t_triu = self.sb("t_triu", (128, 128), BF16)
        self.t_upst = self.sb("t_upst", (128, 128))
        self.t_ones = self.sb("t_ones", (128, 128))
        self.t_iota4 = self.sb("t_iota4", (128, 4))
        self.t_iotatg = self.sb("t_iotatg", (128, 128))
        self.t_iotacm = self.sb("t_iotacm", (128, 128))
        self.t_r128s = self.sb("t_r128s", (128, 1))
        self.t_iotarows = self.sb("t_iotarows", (128, 128))
        self.t_onesw = self.sb("t_onesw", (128, S), BF16)
        self.t_normw = self.sb("t_normw", (128, DV))
        self.t_bsbc = self.sb("t_bsbc", (128, NE))
        self.t_b2bc = self.sb("t_b2bc", (128, D))
        self.t_b1c = self.sb("t_b1c", (128, FF // 128))
        self.t_bgk = self.sb("t_bgk", (128, NMEM * H))
        self.t_bgks = self.sb("t_bgks", (128, H))
        self.t_cid = self.sb("t_cid", (128, 1))
        self.t_cid1 = self.sb("t_cid1", (1, 1))
        self.t_w1s = self.sb("t_w1s", (128, 2048), BF16)
        self.rw = self.sb("rw", (128, NT, 2))
        self.disp_idx = self.sb("disp_idx", (128, 2 * NT))
        self.wrap0 = self.sb("wrap0", (128, S // 16), I16)
        self.wrap1 = self.sb("wrap1", (128, S // 16), I16)
        self.wrapC = self.sb("wrapC", (128, KEXP // 16), I16)
        self.wq = self.sb("wq", (128, 8, DK), BF16)
        self.wgate = self.sb("wgate", (128, 8, NMEM), BF16)
        self.wWs = self.sb("wWs", (128, 8, NE), BF16)
        self.zf32 = self.sb("zf32", (128, 1024))
        self.ln_a = self.sb("ln_a", (128, 8))
        self.snk = self.sb("snk", (128, 1024))
        self.snk_v = self.sb("snk_v", (128, 1024))
        self.wrA = self.sb("wrA", (128, 128))
        self.sc1 = self.sb("sc1", (128, 1024))
        self.sc3 = self.sb("sc3", (128, 128))
        self.sc4 = self.sb("sc4", (128, 128))
        self.sc5 = self.sb("sc5", (128, 128))
        self.sc6 = self.sb("sc6", (128, 128))
        self.scu8 = self.sb("scu8", (128, 128), U8)
        self.sccol = self.sb("sccol", (128, 16))
        self.sci32 = self.sb("sci32", (1, 8), I32)
        self.tid = self.sb("tid", (128, 16))
        self.cnt8 = self.sb("cnt8", (128, 8))
        self.rwd = self.sb("rwd", (128, 2 * NT))
        self.tval = self.sb("tval", (128, 16))

        self.nc.const_aps.register(F32, 0.0, self.cc1("zero"))
        self.nc.const_aps.register(F32, 1.0, self.cc1("one"))

        self.pA = self.sb_ps("pA", F32)
        self.pO = self.sb_ps("pO", F32)
        self.pD = self.sb_ps("pD", F32)
        self.pT1 = self.sb_ps("pT1", BF16)
        self.pT2 = self.sb_ps("pT2", BF16)
        self.pW = [self.sb_ps("pW0", F32), self.sb_ps("pW1", F32)]
        self.pM = self.sb_ps("pM", F32)

    def sb_ps(self, name, dtype):
        n = 512 if dtype == F32 else 1024
        return self.nc.alloc_psum_tensor(name, [128, n], dtype).ap()

    @staticmethod
    def bcast_part(ap_1xN, P=128):
        import dataclasses
        st, n = ap_1xN.ap[-1]
        return dataclasses.replace(ap_1xN, ap=[[0, P], [st, n]])

    @staticmethod
    def bcast_free(ap_Px1, N):
        import dataclasses
        pst, pn = ap_Px1.ap[0]
        return dataclasses.replace(ap_Px1, ap=[[pst, pn], [0, N]])

    def phase_barrier(self):
        blk = self.block

        @blk.gpsimd
        def _(g):
            for name in list(self._sems):
                if name.startswith("d_"):
                    self.w(g, name)
            self.w(g, "t")
            self.w(g, "v")
            self.w(g, "s")
            self.inc(g.nop(), "g")

        for mk_sec in (blk.vector, blk.scalar, blk.tensor, blk.sync):
            @mk_sec
            def _(e):
                self.w(e, "g")

    def wrap_build(self, n, nat_ap, out_tile, replicate=2):
        m = n // 128
        blk = self.block

        @blk.tensor
        def _(t):
            self.w(t, "v")
            self.w(t, "s")
            self.inc(t.transpose(self.pM[0:m, 0:128], nat_ap, self.t_identf), "t")

        @blk.vector
        def _(v):
            self.w(v, "t")
            self.inc(v.tensor_copy(self.wrA[0:m, 0:128], self.pM[0:m, 0:128]), "v")

        for g8 in range(8):
            @blk.tensor
            def _(t, g8=g8):
                self.w(t, "v")
                self.inc(t.transpose(self.pM[0:16, 0:m], self.wrA[0:m, 16 * g8:16 * g8 + 16],
                                     self.t_identf[0:m, 0:m]), "t")

            @blk.vector
            def _(v, g8=g8):
                self.w(v, "t")
                dst = out_tile[0:16, 0:8 * m].rearrange("r (c g) -> r c g", g=8)[:, :, g8]
                self.inc(v.tensor_copy(dst, self.pM[0:16, 0:m]), "v")

        @blk.gpsimd
        def _(g):
            self.w(g, "v")
            for r in range(1, replicate):
                self.inc(g.dma_start(out=out_tile[16 * r:16 * r + 16, 0:8 * m],
                                     in_=out_tile[0:16, 0:8 * m]), "d_wr", 16)
            self.w(g, "d_wr")

    def ln_tile(self, s, v, src_f32, dst_bf16):
        raise RuntimeError("inline in phases")

    def p1_ln(self, stack):
        blk = self.block
        self.xn = self.sbctx(stack, "xn", (128, NT, D), BF16)
        xb = [self.sbctx(stack, f"p1x{i}", (128, D)) for i in range(2)]

        @blk.sync
        def _(q):
            self.w(q, "d_c")
            for j in range(2):
                self.inc(q.dma_start(out=xb[j][:],
                                     in_=self.x_in.rearrange("(c p) d -> p c d", p=128)[:, j, :]),
                         f"d_x{j % 2}", 16)

        for j in range(NT):
            xt = xb[j % 2]

            @blk.vector
            def _(v, j=j, xt=xt):
                self.w(v, f"d_x{j % 2}")
                self.incw(v, v.tensor_scalar(out=self.snk_v[:, 0:D], in0=xt[:], scalar1=1.0,
                                         scalar2=0.0, op0=OP.mult, op1=OP.add,
                                         accum_out=self.ln_a[:, 0:1]), "v")

            @blk.scalar
            def _(s, j=j, xt=xt):
                self.w(s, f"d_x{j % 2}")
                self.inc(s.activation(out=self.snk[:, 0:D], in_=xt[:], func=AF.Square,
                                      bias=self.cc1("zero"), scale=self.cc1("one"),
                                      accum_out=self.ln_a[:, 1:2]), "s")

            @blk.vector
            def _(v, j=j):
                self.w(v, "s")
                self.w(v, "v")
                la = self.ln_a
                self.incw(v, v.tensor_scalar(out=la[:, 2:3], in0=la[:, 0:1], scalar1=1.0 / D,
                                         scalar2=None, op0=OP.mult), "v")
                self.incw(v, v.tensor_scalar(out=la[:, 3:4], in0=la[:, 1:2], scalar1=1.0 / D,
                                         scalar2=None, op0=OP.mult), "v")
                self.incw(v, v.tensor_tensor(out=la[:, 4:5], in0=la[:, 2:3], in1=la[:, 2:3],
                                         op=OP.mult), "v")
                self.incw(v, v.tensor_tensor(out=la[:, 5:6], in0=la[:, 3:4], in1=la[:, 4:5],
                                         op=OP.subtract), "v")

            @blk.scalar
            def _(s, j=j):
                self.w(s, "v")
                self.inc(s.activation(out=self.ln_a[:, 6:7], in_=self.ln_a[:, 5:6],
                                      func=AF.Ln, bias=self.cc1("eps"),
                                      scale=self.cc1("one")), "s")
                self.w(s, "s")
                self.inc(s.activation(out=self.ln_a[:, 6:7], in_=self.ln_a[:, 6:7],
                                      func=AF.Exp, bias=self.cc1("zero"),
                                      scale=self.cc1("half")), "s")

            @blk.vector
            def _(v, j=j, xt=xt):
                self.w(v, "s")
                self.incw(v, v.reciprocal(self.ln_a[:, 7:8], self.ln_a[:, 6:7]), "v")
                self.w(v, "v")
                self.incw(v, v.tensor_scalar(out=self.xn[:, j, :], in0=xt[:],
                                         scalar1=self.ln_a[:, 2:3], scalar2=self.ln_a[:, 7:8],
                                         op0=OP.subtract, op1=OP.mult), "v")

            if j + 2 < NT:
                @blk.sync
                def _(q, j=j, xt=xt):
                    self.w(q, "v")
                    self.inc(q.dma_start(
                        out=xt[:],
                        in_=self.x_in.rearrange("(c p) d -> p c d", p=128)[:, j + 2, :]),
                        f"d_x{j % 2}", 16)

            @blk.tensor
            def _(t, j=j):
                self.w(t, "v")
                for dc in range(8):
                    self.inc(t.transpose(self.pT1[:, dc * 128:(dc + 1) * 128],
                                         self.xn[:, j, dc * 128:(dc + 1) * 128],
                                         self.t_identb), "t")

            @blk.vector
            def _(v, j=j):
                self.w(v, "t")
                for dc in range(8):
                    self.incw(v, v.tensor_copy(self.xnT[:, dc, j * 128:(j + 1) * 128],
                                           self.pT1[:, dc * 128:(dc + 1) * 128]), "v")

        if self.dbg("xnT"):
            dstT = self.dbg_out("xnT", (128, 8, S))

            @blk.gpsimd
            def _(g):
                self.w(g, "v")
                self.inc(g.dma_start(out=dstT[:, :, :], in_=self.xnT[:, :, :]), "d_dbgg", 16)

        if self.dbg("xn"):
            dst = self.dbg_out("xn", (128, NT, D))

            @blk.gpsimd
            def _(g):
                self.w(g, "v")
                self.inc(g.dma_start(out=dst[:, :, :], in_=self.xn[:, :, :]), "d_dbgg", 16)

    def p2_gate(self, stack):
        blk = self.block
        gtile = self.sbctx(stack, "gtile", (128, 4))
        gdbg = self.sbctx(stack, "gdbg", (128, NT * 4)) if self.dbg("gate") else None
        ftile = self.sbctx(stack, "ftile", (128, NT, 4))
        stile = self.sbctx(stack, "stile", (128, NT, 4))
        maskT = self.sbctx(stack, "maskT", (4, S))
        slotT = self.sbctx(stack, "slotT", (4, S))
        col = self.sccol

        for j in range(NT):
            @blk.tensor
            def _(t, j=j):
                self.w(t, "v")
                self.w(t, "d_w", 32)
                for dc in range(8):
                    self.inc(t.matmul(self.pW[0][:, 0:4], self.xnT[:, dc, j * 128:(j + 1) * 128],
                                      self.wgate[:, dc, :], start=(dc == 0), stop=(dc == 7)), "t")

            @blk.vector
            def _(v, j=j):
                self.w(v, "t")
                G = gtile
                self.incw(v, v.tensor_copy(G[:], self.pW[0][:, 0:4]), "v")
                if gdbg is not None:
                    self.incw(v, v.tensor_copy(gdbg[:, 4 * j:4 * j + 4], self.pW[0][:, 0:4]), "v")
                self.incw(v, v.tensor_reduce(out=col[:, 0:1], in_=G[:], axis=AX.X, op=OP.max), "v")
                self.w(v, "v")
                e1 = self.sc3[:, 0:4]
                self.incw(v, v.tensor_scalar(out=e1, in0=G[:], scalar1=col[:, 0:1],
                                         scalar2=None, op0=OP.is_equal), "v")
                c1 = self.sc4[:, 0:4]
                self.incw(v, v.tensor_tensor_scan(out=c1, data0=self.t_onesw[:, 0:4], data1=e1,
                                              initial=0.0, op0=OP.mult, op1=OP.add), "v")
                t1 = self.sc5[:, 0:4]
                self.incw(v, v.tensor_scalar(out=t1, in0=c1, scalar1=1.0, scalar2=None,
                                         op0=OP.is_equal), "v")
                f = ftile[:, j, :]
                self.incw(v, v.tensor_tensor(out=f, in0=e1, in1=t1, op=OP.mult), "v")
                Gm = self.sc6[:, 0:4]
                self.incw(v, v.scalar_tensor_tensor(out=Gm, in0=f, scalar=-1e30, in1=G[:],
                                                op0=OP.mult, op1=OP.add), "v")
                self.incw(v, v.tensor_reduce(out=col[:, 1:2], in_=Gm, axis=AX.X, op=OP.max), "v")
                self.w(v, "v")
                self.incw(v, v.tensor_scalar(out=e1, in0=Gm, scalar1=col[:, 1:2], scalar2=None,
                                         op0=OP.is_equal), "v")
                self.incw(v, v.tensor_tensor_scan(out=c1, data0=self.t_onesw[:, 0:4], data1=e1,
                                              initial=0.0, op0=OP.mult, op1=OP.add), "v")
                self.incw(v, v.tensor_scalar(out=t1, in0=c1, scalar1=1.0, scalar2=None,
                                         op0=OP.is_equal), "v")
                sflag = stile[:, j, :]
                self.incw(v, v.tensor_tensor(out=sflag, in0=e1, in1=t1, op=OP.mult), "v")
                self.incw(v, v.tensor_tensor(out=col[:, 2:3], in0=col[:, 0:1], in1=col[:, 1:2],
                                         op=OP.subtract), "v")
                self.incw(v, v.tensor_tensor(out=self.sc3[:, 0:4], in0=f, in1=sflag, op=OP.add), "v")

            @blk.vector
            def _(v, j=j):
                self.incw(v, v.tensor_copy(self.rwd[:, j:j + 1], col[:, 2:3]), "v")

            @blk.tensor
            def _(t, j=j):
                self.w(t, "v")
                self.inc(t.transpose(self.pM[0:4, 0:128], self.sc3[:, 0:4], self.t_identf), "t")

            @blk.vector
            def _(v, j=j):
                self.w(v, "t")
                self.incw(v, v.tensor_copy(maskT[:, j * 128:(j + 1) * 128], self.pM[0:4, 0:128]), "v")

        @blk.scalar
        def _(s):
            self.w(s, "v")
            self.inc(s.activation(out=self.rwd[:, NT:2 * NT], in_=self.rwd[:, 0:NT],
                                  func=AF.Exp, bias=self.cc1("zero"),
                                  scale=self.cc1("negone")), "s")

        @blk.vector
        def _(v):
            self.w(v, "s")
            self.incw(v, v.tensor_scalar(out=self.rwd[:, NT:2 * NT], in0=self.rwd[:, NT:2 * NT],
                                     scalar1=1.0, scalar2=None, op0=OP.add), "v")
            self.incw(v, v.reciprocal(self.rwd[:, NT:2 * NT], self.rwd[:, NT:2 * NT]), "v")
            for j in range(NT):
                self.incw(v, v.tensor_copy(self.rw[:, j, 0:1], self.rwd[:, NT + j:NT + j + 1]), "v")
                self.incw(v, v.tensor_scalar(out=self.rw[:, j, 1:2], in0=self.rw[:, j, 0:1],
                                         scalar1=-1.0, scalar2=1.0, op0=OP.mult, op1=OP.add), "v")

        @blk.vector
        def _(v):
            self.incw(v, v.tensor_tensor_scan(out=slotT[:], data0=self.t_onesw[0:4, :],
                                          data1=maskT[:], initial=0.0,
                                          op0=OP.mult, op1=OP.add), "v")
            self.incw(v, v.tensor_scalar(out=col[0:4, 8:9], in0=slotT[:, S - 1:S], scalar1=-1.0,
                                     scalar2=float(CAP - 1), op0=OP.mult, op1=OP.add), "v")
            self.w(v, "v")
            self.incw(v, v.tensor_scalar(out=slotT[:], in0=slotT[:], scalar1=col[0:4, 8:9],
                                     scalar2=None, op0=OP.add), "v")

        for j in range(NT):
            @blk.tensor
            def _(t, j=j):
                self.w(t, "v")
                self.inc(t.transpose(self.pM[0:128, 0:4], slotT[:, j * 128:(j + 1) * 128],
                                     self.t_identf[0:4, 0:4]), "t")

            @blk.vector
            def _(v, j=j):
                self.w(v, "t")
                st4 = self.sc4[:, 0:4]
                self.incw(v, v.tensor_copy(st4, self.pM[0:128, 0:4]), "v")
                for c, ftl in ((0, ftile), (1, stile)):
                    fj = ftl[:, j, :]
                    self.incw(v, v.tensor_tensor(out=self.sc5[:, 0:4], in0=fj, in1=self.t_iota4,
                                             op=OP.mult), "v")
                    self.incw(v, v.tensor_reduce(out=col[:, 3:4], in_=self.sc5[:, 0:4],
                                             axis=AX.X, op=OP.add), "v")
                    self.incw(v, v.tensor_tensor(out=self.sc5[:, 0:4], in0=fj, in1=st4,
                                             op=OP.mult), "v")
                    self.incw(v, v.tensor_reduce(out=col[:, 4:5], in_=self.sc5[:, 0:4],
                                             axis=AX.X, op=OP.add), "v")
                    self.incw(v, v.tensor_scalar(out=col[:, 5:6], in0=col[:, 4:5], scalar1=0.0,
                                             scalar2=None, op0=OP.is_ge), "v")
                    self.incw(v, v.scalar_tensor_tensor(out=col[:, 6:7], in0=col[:, 3:4],
                                                    scalar=float(CAP), in1=col[:, 4:5],
                                                    op0=OP.mult, op1=OP.add), "v")
                    self.incw(v, v.tensor_scalar(out=col[:, 6:7], in0=col[:, 6:7], scalar1=-4096.0,
                                             scalar2=None, op0=OP.add), "v")
                    self.incw(v, v.tensor_tensor(out=col[:, 6:7], in0=col[:, 6:7], in1=col[:, 5:6],
                                             op=OP.mult), "v")
                    self.incw(v, v.tensor_scalar(out=self.disp_idx[:, c * NT + j:c * NT + j + 1],
                                             in0=col[:, 6:7], scalar1=4096.0,
                                             scalar2=None, op0=OP.add), "v")

        if self.dbg("fs"):
            dstf = self.dbg_out("fs", (128, NT, 4))
            dsts = self.dbg_out("ss", (128, NT, 4))
            dstm = self.dbg_out("maskT", (4, S))
            dstsl = self.dbg_out("slotT", (4, S))

            @blk.sync
            def _(q):
                self.w(q, "v")
                self.inc(q.dma_start(out=dstf[:, :, :], in_=ftile[:, :, :]), "d_dbg", 16)
                self.inc(q.dma_start(out=dsts[:, :, :], in_=stile[:, :, :]), "d_dbg", 16)
                self.inc(q.dma_start(out=dstm[:, :], in_=maskT[:, :]), "d_dbg", 16)
                self.inc(q.dma_start(out=dstsl[:, :], in_=slotT[:, :]), "d_dbg", 16)

        if gdbg is not None:
            dstg = self.dbg_out("gate", (128, NT * 4))

            @blk.sync
            def _(q):
                self.w(q, "v")
                self.inc(q.dma_start(out=dstg[:, :], in_=gdbg[:, :]), "d_dbg", 16)

        self.wrap_build(S, self.disp_idx[:, 0:NT], self.wrap0)
        self.wrap_build(S, self.disp_idx[:, NT:2 * NT], self.wrap1)

        if self.dbg("disp"):
            dst = self.dbg_out("disp", (128, 2 * NT))

            @blk.sync
            def _(q):
                self.w(q, "v")
                self.inc(q.dma_start(out=dst[:, :], in_=self.disp_idx[:, :]), "d_dbg", 16)
        if self.dbg("rw"):
            dst = self.dbg_out("rw", (128, NT, 2))

            @blk.sync
            def _(q):
                self.w(q, "v")
                self.inc(q.dma_start(out=dst[:, :, :], in_=self.rw[:, :, :]), "d_dbg", 16)

        @blk.gpsimd
        def _(g):
            self.w(g, "v")
            self.w(g, "d_z")
            for wt in (self.wrap0, self.wrap1):
                self.inc(g.dma_scatter_add(out_ap=self.tx_hbm[:, :], in_ap=self.xn[:, :, :],
                                           idxs_ap=wt[:, :], num_idxs=S, num_idxs_reg=S,
                                           elem_size=D, single_packet=False), "d_g", 16)
                self.w(g, "d_g")

    def gla_group(self, stack, gi, tiles):
        blk = self.block
        routed = gi < 4
        cap = CAP if routed else S
        nmh = cap // 1024
        (txT, vsb, u2m, qfh, kfh, lamt, emt, lpt, tmpT, cs_c, eCf, Ssb,
         k2c4, amt4, k2t4, wkb, wvb, wg1b, wg2b) = tiles

        @blk.gpsimd
        def _(g):
            self.w(g, "t")
            self.w(g, "v")
            if routed:
                self.inc(g.dma_start(out=wkb[:],
                                     in_=self.w_k_in[gi].rearrange("(c p) o -> p c o", p=128)),
                         "d_wm", 16)
                self.inc(g.dma_start(out=wvb[:],
                                     in_=self.w_v_in[gi].rearrange("(c p) o -> p c o", p=128)),
                         "d_wm", 16)
                self.inc(g.dma_start(out=wg1b[:],
                                     in_=self.w_gk1_in[gi].rearrange("(c p) o -> p c o", p=128)),
                         "d_wm", 16)
                self.inc(g.dma_start(out=wg2b[0:16, :], in_=self.w_gk2_in[gi][:, :]),
                         "d_wm", 16)
            else:
                self.inc(g.dma_start(out=wkb[:],
                                     in_=self.w_ks_in.rearrange("(c p) o -> p c o", p=128)),
                         "d_wm", 16)
                self.inc(g.dma_start(out=wvb[:],
                                     in_=self.w_vs_in.rearrange("(c p) o -> p c o", p=128)),
                         "d_wm", 16)
                self.inc(g.dma_start(out=wg1b[:],
                                     in_=self.w_gk1s_in.rearrange("(c p) o -> p c o", p=128)),
                         "d_wm", 16)
                self.inc(g.dma_start(out=wg2b[0:16, :], in_=self.w_gk2s_in[:, :]),
                         "d_wm", 16)

        wm_tok = self.cnt["d_wm"]
        bcol = (lambda h: self.t_bgk[:, gi * H + h:gi * H + h + 1]) if routed else                (lambda h: self.t_bgks[:, h:h + 1])

        if routed:
            @blk.sync
            def _(q):
                self.w(q, "d_g")
                self.w(q, "v")
                self.w(q, "d_om")
                self.w(q, "d_sh")
                self.inc(q.dma_start(
                    out=u2m[:, :, :],
                    in_=self.tx_hbm[gi * CAP:(gi + 1) * CAP, :].rearrange(
                        "(c p) d -> p c d", p=128)), "d_tx", 16)

            for tb in range(8):
                @blk.tensor
                def _(t, tb=tb):
                    self.w(t, "d_tx")
                    self.w(t, "v")
                    for dc in range(8):
                        self.inc(t.transpose(self.pT1[:, dc * 128:(dc + 1) * 128],
                                             u2m[:, tb, dc * 128:(dc + 1) * 128],
                                             self.t_identb), "t")

                @blk.vector
                def _(v, tb=tb):
                    self.w(v, "t")
                    for dc in range(8):
                        self.inc(v.tensor_copy(txT[:, dc, tb * 128:(tb + 1) * 128],
                                               self.pT1[:, dc * 128:(dc + 1) * 128]), "v")

        @blk.vector
        def _(v):
            for h in range(H):
                self.inc(v.memset(Ssb[h][:], 0.0), "v")

        for mh in range(nmh):
            def srcT(dc, sl):
                if routed:
                    return txT[:, dc, sl]
                base = mh * 1024
                return self.xnT[:, dc, base + sl.start: base + sl.stop]

            for tb in range(8):
                @blk.tensor
                def _(t, tb=tb, mh=mh):
                    self.w(t, "v")
                    self.w(t, "s")
                    self.w(t, "d_wm", wm_tok)
                    for dvh in range(2):
                        pb = self.pW[dvh]
                        for dc in range(8):
                            self.inc(t.matmul(pb[:, 0:512],
                                              srcT(dc, slice(tb * 128, (tb + 1) * 128)),
                                              wvb[:, dc, dvh * 512:(dvh + 1) * 512],
                                              start=(dc == 0), stop=(dc == 7)), "t")

                @blk.vector
                def _(v, tb=tb):
                    self.w(v, "t")
                    self.inc(v.tensor_copy(vsb[:, tb, 0:512], self.pW[0][:, 0:512]), "v")

                @blk.scalar
                def _(s, tb=tb):
                    self.w(s, "t")
                    self.inc(s.activation(out=vsb[:, tb, 512:1024], in_=self.pW[1][:, 0:512],
                                          func=AF.Copy, bias=0.0, scale=1.0), "s")

            for h in range(H):
                for hf in range(2):
                    @blk.tensor
                    def _(t, h=h, hf=hf, mh=mh):
                        self.w(t, "v")
                        self.w(t, "s")
                        for dc in range(8):
                            self.inc(t.matmul(self.pW[0][0:16, 0:512],
                                              wg1b[:, dc, :],
                                              srcT(dc, slice(hf * 512, (hf + 1) * 512)),
                                              start=(dc == 0), stop=(dc == 7)), "t")

                    @blk.scalar
                    def _(s, h=h, hf=hf):
                        self.w(s, "t")
                        self.inc(s.activation(out=tmpT[0:16, hf * 512:(hf + 1) * 512],
                                              in_=self.pW[0][0:16, 0:512], func=AF.Copy, bias=0.0, scale=1.0), "s")

                    @blk.tensor
                    def _(t, h=h, hf=hf):
                        self.w(t, "s")
                        self.inc(t.matmul(self.pW[1][:, 0:512],
                                          wg2b[0:16, h * 128:(h + 1) * 128],
                                          tmpT[0:16, hf * 512:(hf + 1) * 512],
                                          start=True, stop=True), "t")

                    @blk.scalar
                    def _(s, h=h, hf=hf):
                        self.w(s, "t")
                        self.inc(s.activation(out=self.sc1[:, 0:512], in_=self.pW[1][:, 0:512],
                                              func=AF.Exp, bias=bcol(h),
                                              scale=self.cc1("negone")), "s")
                        self.w(s, "s")
                        self.inc(s.activation(out=lpt[:, hf * 512:(hf + 1) * 512],
                                              in_=self.sc1[:, 0:512], func=AF.Ln,
                                              bias=self.cc1("one"), scale=self.cc1("one")), "s")

                @blk.vector
                def _(v, h=h):
                    self.w(v, "s")
                    for j in range(8):
                        self.inc(v.tensor_tensor_scan(
                            out=cs_c[:, j * 128:(j + 1) * 128],
                            data0=self.t_onesw[:, 0:128],
                            data1=lpt[:, j * 128:(j + 1) * 128],
                            initial=0.0, op0=OP.mult, op1=OP.add), "v")

                @blk.scalar
                def _(s, h=h):
                    self.w(s, "v")
                    self.inc(s.activation(out=lamt[:], in_=cs_c[:, 0:1024], func=AF.Exp,
                                          bias=self.cc1("lnqs"), scale=self.cc1("neg16")), "s")
                    self.inc(s.activation(out=emt[:], in_=cs_c[:, 0:1024], func=AF.Exp,
                                          bias=self.cc1("zero"), scale=self.cc1("inv16")), "s")
                    for j in range(8):
                        self.inc(s.activation(out=eCf[:, h * 8 + j:h * 8 + j + 1],
                                              in_=cs_c[:, j * 128 + 127:j * 128 + 128],
                                              func=AF.Exp, bias=self.cc1("zero"),
                                              scale=self.cc1("neg16")), "s")

                for hf in range(2):
                    @blk.tensor
                    def _(t, h=h, hf=hf, mh=mh):
                        self.w(t, "s")
                        self.w(t, "v")
                        for dc in range(8):
                            self.inc(t.matmul(self.pW[0][:, 0:512],
                                              self.wq[:, dc, h * 128:(h + 1) * 128],
                                              srcT(dc, slice(hf * 512, (hf + 1) * 512)),
                                              start=(dc == 0), stop=(dc == 7)), "t")
                        for dc in range(8):
                            self.inc(t.matmul(self.pW[1][:, 0:512],
                                              wkb[:, dc, h * 128:(h + 1) * 128],
                                              srcT(dc, slice(hf * 512, (hf + 1) * 512)),
                                              start=(dc == 0), stop=(dc == 7)), "t")

                    @blk.vector
                    def _(v, h=h, hf=hf):
                        self.w(v, "t")
                        sl = slice(hf * 512, (hf + 1) * 512)
                        self.inc(v.tensor_tensor(out=qfh[h][:, sl], in0=self.pW[0][:, 0:512],
                                                 in1=lamt[:, sl], op=OP.mult), "v")
                        self.inc(v.tensor_tensor(out=kfh[h][:, sl], in0=self.pW[1][:, 0:512],
                                                 in1=emt[:, sl], op=OP.mult), "v")

            for jl in range(8):
                jb = slice(jl * 128, (jl + 1) * 128)
                jg = mh * 8 + jl

                @blk.tensor
                def _(t, jl=jl, jb=jb):
                    self.w(t, "v")
                    for h in range(H):
                        self.inc(t.matmul(self.pA[:, h * 128:(h + 1) * 128],
                                          kfh[h][:, jb], qfh[h][:, jb],
                                          start=True, stop=True), "t")

                @blk.vector
                def _(v, jl=jl, jb=jb):
                    self.w(v, "t")
                    for h in range(H):
                        self.inc(v.tensor_tensor(out=amt4[:, h * 128:(h + 1) * 128],
                                                 in0=self.pA[:, h * 128:(h + 1) * 128],
                                                 in1=self.t_triu, op=OP.mult), "v")
                        self.inc(v.tensor_scalar(out=k2c4[:, h * 128:(h + 1) * 128],
                                                 in0=kfh[h][:, jb],
                                                 scalar1=eCf[:, h * 8 + jl:h * 8 + jl + 1],
                                                 scalar2=None, op0=OP.mult), "v")

                @blk.tensor
                def _(t, jl=jl, jb=jb, jg=jg, mh=mh):
                    self.w(t, "v")
                    for h in range(H):
                        self.inc(t.transpose(self.pT1[:, h * 128:(h + 1) * 128],
                                             k2c4[:, h * 128:(h + 1) * 128], self.t_identb), "t")
                    for h in range(H):
                        pOb = self.pO if h < 2 else self.pD
                        po = pOb[:, (h % 2) * 256:(h % 2) * 256 + 256]
                        first = (jg == 0)
                        self.inc(t.matmul(po, amt4[:, h * 128:(h + 1) * 128],
                                          vsb[:, jl, h * 256:(h + 1) * 256],
                                          start=True, stop=first), "t")
                        if not first:
                            self.inc(t.matmul(po, qfh[h][:, jb], Ssb[h][:],
                                              start=False, stop=True), "t")

                @blk.vector
                def _(v, jl=jl):
                    self.w(v, "t")
                    for h in range(H):
                        self.inc(v.tensor_copy(k2t4[:, h * 128:(h + 1) * 128],
                                               self.pT1[:, h * 128:(h + 1) * 128]), "v")

                @blk.tensor
                def _(t, jl=jl):
                    self.w(t, "v")
                    for h in range(H):
                        pDb = self.pW[0] if h < 2 else self.pW[1]
                        self.inc(t.matmul(pDb[:, (h % 2) * 256:(h % 2) * 256 + 256],
                                          k2t4[:, h * 128:(h + 1) * 128],
                                          vsb[:, jl, h * 256:(h + 1) * 256],
                                          start=True, stop=True), "t")

                @blk.vector
                def _(v, jl=jl, jg=jg):
                    self.w(v, "t")
                    if jl == 0:
                        self.w(v, "d_om")
                        self.w(v, "d_sh")
                    for h in range(H):
                        pOb = self.pO if h < 2 else self.pD
                        po = pOb[:, (h % 2) * 256:(h % 2) * 256 + 256]
                        self.inc(v.tensor_copy(u2m[:, jl, h * 256:(h + 1) * 256], po), "v")
                    for h in range(H):
                        pDb = self.pW[0] if h < 2 else self.pW[1]
                        self.inc(v.scalar_tensor_tensor(
                            out=Ssb[h][:], in0=Ssb[h][:],
                            scalar=eCf[:, h * 8 + jl:h * 8 + jl + 1],
                            in1=pDb[:, (h % 2) * 256:(h % 2) * 256 + 256],
                            op0=OP.mult, op1=OP.add), "v")

            @blk.sync
            def _(q, mh=mh):
                self.w(q, "v")
                if routed:
                    dst = self.om_hbm[gi * CAP:(gi + 1) * CAP, :]
                    self.inc(q.dma_start(out=dst.rearrange("(c p) d -> p c d", p=128),
                                         in_=u2m[:, :, :]), "d_om", 16)
                else:
                    dst = self.sh_hbm[mh * 1024:(mh + 1) * 1024, :]
                    self.inc(q.dma_start(out=dst.rearrange("(c p) d -> p c d", p=128),
                                         in_=u2m[:, :, :]), "d_sh", 16)

    def p_gla(self, stack):
        blk = self.block
        txT = self.sbctx(stack, "txT", (128, 8, CAP), BF16)
        vsb = self.sbctx(stack, "vsb", (128, 8, DV), BF16)
        u2m = self.sbctx(stack, "u2m", (128, 8, D), BF16)
        qfh = [self.sbctx(stack, f"qfh{h}", (128, 1024), BF16) for h in range(H)]
        kfh = [self.sbctx(stack, f"kfh{h}", (128, 1024), BF16) for h in range(H)]
        emt = self.sbctx(stack, "emt", (128, 1024), BF16)
        lpt = self.sbctx(stack, "lpt", (128, 1024), BF16)
        lamt = lpt
        tmpT = self.sbctx(stack, "tmpT", (16, 1024), BF16)
        cs_c = self.sbctx(stack, "cs_c", (128, 1024))
        eCf = self.sbctx(stack, "eCf", (128, H * 8))
        Ssb = [self.sbctx(stack, f"Ssb{h}", (128, HDV), BF16) for h in range(H)]
        k2c4 = self.sbctx(stack, "k2c4", (128, 512), BF16)
        amt4 = self.sbctx(stack, "amt4", (128, 512), BF16)
        k2t4 = self.sbctx(stack, "k2t4", (128, 512), BF16)
        wkb = self.sbctx(stack, "wkb", (128, 8, DK), BF16)
        wvb = self.sbctx(stack, "wvb", (128, 8, DV), BF16)
        wg1b = self.sbctx(stack, "wg1b", (128, 8, LOW), BF16)
        wg2b = self.sbctx(stack, "wg2b", (16, DK), BF16)
        tiles = (txT, vsb, u2m, qfh, kfh, lamt, emt, lpt, tmpT, cs_c, eCf, Ssb,
                 k2c4, amt4, k2t4, wkb, wvb, wg1b, wg2b)

        @blk.vector
        def _(v):
            self.w(v, "d_g")

        @blk.scalar
        def _(s):
            self.w(s, "d_g")

        for gi in self.gla_order:
            self.gla_group(stack, gi, tiles)

        if self.dbg("om"):
            pass

    def p6_reco(self, stack):
        blk = self.block
        self.sc2 = self.sbctx(stack, "sc2", (128, 1024))
        gat0 = self.sbctx(stack, "gat0", (128, 8, DV), BF16)
        gat1 = self.sbctx(stack, "gat1", (128, 8, DV), BF16)
        shb = self.sbctx(stack, "shb", (128, 8, DV), BF16)
        wgt = self.sbctx(stack, "wgt", (128, 8, DV), BF16)
        sg = self.sbctx(stack, "sg", (128, DV))

        @blk.gpsimd
        def _(g):
            self.w(g, "d_wm")
            self.w(g, "t")
            self.w(g, "v")
            self.w(g, "s")
            self.inc(g.dma_start(out=wgt[:],
                                 in_=self.w_g_in.rearrange("(c p) o -> p c o", p=128)),
                     "d_wg", 16)

        wg_tok = self.cnt["d_wg"]

        for hf in range(2):
            @blk.gpsimd
            def _(g, hf=hf):
                self.w(g, "d_om")
                self.w(g, "d_z")
                self.w(g, "s")
                self.w(g, "v")
                for c, (wt, gat) in enumerate(((self.wrap0, gat0), (self.wrap1, gat1))):
                    self.inc(g.dma_gather(out_ap=gat[:, :, :], in_ap=self.om_hbm[:, :],
                                          idxs_ap=wt[:, 64 * hf:64 * hf + 64],
                                          num_idxs=1024, num_idxs_reg=1024,
                                          elem_size=D, single_packet=False), "d_g", 16)
                self.w(g, "d_g")

            @blk.sync
            def _(q, hf=hf):
                self.w(q, "d_sh")
                self.w(q, "v")
                self.inc(q.dma_start(
                    out=shb[:, :, :],
                    in_=self.sh_hbm[hf * 1024:(hf + 1) * 1024, :].rearrange(
                        "(c p) d -> p c d", p=128)), "d_tx", 16)

            for jg in range(8):
                j = hf * 8 + jg

                @blk.scalar
                def _(s, j=j, jg=jg):
                    self.w(s, "d_g")
                    self.w(s, "v")
                    self.inc(s.activation(out=self.sc1[:, 0:DV], in_=gat0[:, jg, :],
                                          func=AF.Copy, bias=0.0,
                                          scale=self.rw[:, j, 0:1]), "s")

                @blk.vector
                def _(v, j=j, jg=jg):
                    self.w(v, "s")
                    self.w(v, "d_tx")
                    self.incw(v, v.scalar_tensor_tensor(out=self.sc2[:, 0:DV], in0=gat1[:, jg, :],
                                                    scalar=self.rw[:, j, 1:2],
                                                    in1=self.sc1[:, 0:DV],
                                                    op0=OP.mult, op1=OP.add), "v")
                    self.incw(v, v.tensor_tensor(out=self.sc1[:, 0:DV], in0=self.sc2[:, 0:DV],
                                             in1=shb[:, jg, :], op=OP.add), "v")

                @blk.scalar
                def _(s, j=j):
                    self.w(s, "v")
                    for h in range(H):
                        self.inc(s.activation(out=self.snk[:, h * HDV:(h + 1) * HDV],
                                              in_=self.sc1[:, h * HDV:(h + 1) * HDV],
                                              func=AF.Square, bias=self.cc1("zero"),
                                              scale=self.cc1("one"),
                                              accum_out=self.sccol[:, 8 + h:9 + h]), "s")

                @blk.vector
                def _(v, j=j):
                    self.w(v, "s")
                    self.incw(v, v.tensor_scalar(out=self.sccol[:, 12:16],
                                             in0=self.sccol[:, 8:12], scalar1=1.0 / HDV,
                                             scalar2=EPS, op0=OP.mult, op1=OP.add), "v")

                @blk.scalar
                def _(s, j=j):
                    self.w(s, "v")
                    self.inc(s.activation(out=self.sccol[:, 8:12], in_=self.sccol[:, 12:16],
                                          func=AF.Ln, bias=self.cc1("zero"),
                                          scale=self.cc1("one")), "s")
                    self.w(s, "s")
                    self.inc(s.activation(out=self.sccol[:, 8:12], in_=self.sccol[:, 8:12],
                                          func=AF.Exp, bias=self.cc1("zero"),
                                          scale=self.cc1("half")), "s")

                @blk.vector
                def _(v, j=j):
                    self.w(v, "s")
                    self.incw(v, v.reciprocal(self.sccol[:, 12:16], self.sccol[:, 8:12]), "v")
                    self.w(v, "v")
                    for h in range(H):
                        sl = slice(h * HDV, (h + 1) * HDV)
                        self.incw(v, v.scalar_tensor_tensor(out=self.sc2[:, sl],
                                                        in0=self.sc1[:, sl],
                                                        scalar=self.sccol[:, 12 + h:13 + h],
                                                        in1=self.t_normw[:, sl],
                                                        op0=OP.mult, op1=OP.mult), "v")

                @blk.tensor
                def _(t, j=j):
                    self.w(t, "v")
                    self.w(t, "d_wg", wg_tok)
                    for dvh in range(2):
                        for dc in range(8):
                            self.inc(t.matmul(self.pW[dvh][:, 0:512],
                                              self.xnT[:, dc, j * 128:(j + 1) * 128],
                                              wgt[:, dc, dvh * 512:(dvh + 1) * 512],
                                              start=(dc == 0), stop=(dc == 7)), "t")

                @blk.scalar
                def _(s, j=j):
                    self.w(s, "t")
                    for dvh in range(2):
                        self.inc(s.activation(out=sg[:, dvh * 512:(dvh + 1) * 512],
                                              in_=self.pW[dvh][:, 0:512], func=AF.Exp,
                                              bias=self.cc1("zero"),
                                              scale=self.cc1("negone")), "s")
                        self.inc(s.activation(out=self.snk[:, dvh * 512:(dvh + 1) * 512],
                                              in_=self.pW[dvh][:, 0:512], func=AF.Copy,
                                              bias=0.0, scale=1.0), "s")

                @blk.vector
                def _(v, j=j):
                    self.w(v, "s")
                    self.incw(v, v.tensor_scalar(out=sg[:], in0=sg[:], scalar1=1.0,
                                             scalar2=None, op0=OP.add), "v")
                    self.incw(v, v.reciprocal(sg[:], sg[:]), "v")
                    self.incw(v, v.tensor_tensor(out=sg[:], in0=sg[:], in1=self.snk[:, 0:DV],
                                             op=OP.mult), "v")

                @blk.vector
                def _(v, j=j):
                    self.w(v, "s")
                    self.incw(v, v.tensor_tensor(out=self.og[:, j, :], in0=self.sc2[:, 0:DV],
                                             in1=sg[:], op=OP.mult), "v")

    def p7_out(self, stack):
        blk = self.block
        wot = self.sbctx(stack, "wot", (128, 8, D), BF16)
        ynT = self.sbctx(stack, "ynT", (128, 8, S), BF16)
        ogT = self.sbctx(stack, "ogT", (128, 8, 128), BF16)
        xb = [self.sbctx(stack, f"p7x{i}", (128, D)) for i in range(2)]
        yb = [self.sbctx(stack, f"p7y{i}", (128, D)) for i in range(2)]
        ynb = [self.sbctx(stack, f"p7n{i}", (128, D), BF16) for i in range(2)]
        pTl = self.sbctx(stack, "pTl", (NE, S))
        zr = self.sbctx(stack, "zr", (128, NE))
        er = self.sbctx(stack, "er", (128, NE))

        @blk.gpsimd
        def _(g):
            self.w(g, "d_wg")
            self.w(g, "t")
            self.w(g, "v")
            self.w(g, "s")
            self.inc(g.dma_start(out=wot[:],
                                 in_=self.w_o_in.rearrange("(c p) o -> p c o", p=128)),
                     "d_wo", 16)
            self.inc(g.dma_start(out=self.wWs[:],
                                 in_=self.Ws_in.rearrange("(c p) o -> p c o", p=128)),
                     "d_wo", 16)

        wo_tok = self.cnt["d_wo"]

        @blk.sync
        def _(q):
            self.w(q, "t")
            self.w(q, "v")
            self.w(q, "s")
            for j in range(2):
                self.inc(q.dma_start(out=xb[j][:],
                                     in_=self.x_in.rearrange("(c p) d -> p c d", p=128)[:, j, :]),
                         f"d_x{j % 2}", 16)

        for j in range(NT):
            xt = xb[j % 2]
            yt = yb[j % 2]
            ynt = ynb[j % 2]

            @blk.tensor
            def _(t, j=j):
                self.w(t, "v")
                self.w(t, "d_wo", wo_tok)
                for dc in range(8):
                    self.inc(t.transpose(self.pT1[:, dc * 128:(dc + 1) * 128],
                                         self.og[:, j, dc * 128:(dc + 1) * 128],
                                         self.t_identb), "t")

            @blk.vector
            def _(v, j=j):
                self.w(v, "t")
                for dc in range(8):
                    self.incw(v, v.tensor_copy(ogT[:, dc, :],
                                           self.pT1[:, dc * 128:(dc + 1) * 128]), "v")

            @blk.tensor
            def _(t, j=j):
                self.w(t, "v")
                for half in range(2):
                    for dc in range(8):
                        self.inc(t.matmul(self.pW[half][:, 0:512],
                                          ogT[:, dc, :],
                                          wot[:, dc, half * 512:(half + 1) * 512],
                                          start=(dc == 0), stop=(dc == 7)), "t")

            @blk.vector
            def _(v, j=j, xt=xt, yt=yt):
                self.w(v, "t")
                self.w(v, f"d_x{j % 2}")
                self.w(v, f"d_y{j % 2}")
                for half in range(2):
                    self.incw(v, v.tensor_tensor(out=yt[:, half * 512:(half + 1) * 512],
                                             in0=self.pW[half][:, 0:512],
                                             in1=xt[:, half * 512:(half + 1) * 512],
                                             op=OP.add), "v")
                self.incw(v, v.tensor_scalar(out=self.snk_v[:, 0:D], in0=yt[:], scalar1=1.0,
                                         scalar2=0.0, op0=OP.mult, op1=OP.add,
                                         accum_out=self.ln_a[:, 0:1]), "v")

            @blk.sync
            def _(q, j=j, yt=yt, xt=xt):
                self.w(q, "v")
                self.inc(q.dma_start(
                    out=self.out.rearrange("(c p) d -> p c d", p=128)[:, j, :],
                    in_=yt[:]), f"d_y{j % 2}", 16)
                if j + 2 < NT:
                    self.inc(q.dma_start(
                        out=xt[:],
                        in_=self.x_in.rearrange("(c p) d -> p c d", p=128)[:, j + 2, :]),
                        f"d_x{j % 2}", 16)

            @blk.scalar
            def _(s, j=j, yt=yt):
                self.w(s, "v")
                self.inc(s.activation(out=self.snk[:, 0:D], in_=yt[:], func=AF.Square,
                                      bias=self.cc1("zero"), scale=self.cc1("one"),
                                      accum_out=self.ln_a[:, 1:2]), "s")

            @blk.vector
            def _(v, j=j):
                self.w(v, "s")
                self.w(v, "v")
                la = self.ln_a
                self.incw(v, v.tensor_scalar(out=la[:, 2:3], in0=la[:, 0:1], scalar1=1.0 / D,
                                         scalar2=None, op0=OP.mult), "v")
                self.incw(v, v.tensor_scalar(out=la[:, 3:4], in0=la[:, 1:2], scalar1=1.0 / D,
                                         scalar2=None, op0=OP.mult), "v")
                self.incw(v, v.tensor_tensor(out=la[:, 4:5], in0=la[:, 2:3], in1=la[:, 2:3],
                                         op=OP.mult), "v")
                self.incw(v, v.tensor_tensor(out=la[:, 5:6], in0=la[:, 3:4], in1=la[:, 4:5],
                                         op=OP.subtract), "v")

            @blk.scalar
            def _(s, j=j):
                self.w(s, "v")
                self.inc(s.activation(out=self.ln_a[:, 6:7], in_=self.ln_a[:, 5:6],
                                      func=AF.Ln, bias=self.cc1("eps"),
                                      scale=self.cc1("one")), "s")
                self.w(s, "s")
                self.inc(s.activation(out=self.ln_a[:, 6:7], in_=self.ln_a[:, 6:7],
                                      func=AF.Exp, bias=self.cc1("zero"),
                                      scale=self.cc1("half")), "s")

            @blk.vector
            def _(v, j=j, yt=yt, ynt=ynt):
                self.w(v, "s")
                self.w(v, f"d_yn{j % 2}")
                self.incw(v, v.reciprocal(self.ln_a[:, 7:8], self.ln_a[:, 6:7]), "v")
                self.w(v, "v")
                self.incw(v, v.tensor_scalar(out=ynt[:], in0=yt[:],
                                         scalar1=self.ln_a[:, 2:3], scalar2=self.ln_a[:, 7:8],
                                         op0=OP.subtract, op1=OP.mult), "v")

            @blk.sync
            def _(q, j=j, ynt=ynt):
                self.w(q, "v")
                self.inc(q.dma_start(
                    out=self.yn_loc.rearrange("(c p) d -> p c d", p=128)[:, j, :],
                    in_=ynt[:]), f"d_yn{j % 2}", 16)

            @blk.tensor
            def _(t, j=j, ynt=ynt):
                self.w(t, "v")
                for dc in range(8):
                    self.inc(t.transpose(self.pT2[:, dc * 128:(dc + 1) * 128],
                                         ynt[:, dc * 128:(dc + 1) * 128],
                                         self.t_identb), "t")

            @blk.vector
            def _(v, j=j):
                self.w(v, "t")
                for dc in range(8):
                    self.incw(v, v.tensor_copy(ynT[:, dc, j * 128:(j + 1) * 128],
                                           self.pT2[:, dc * 128:(dc + 1) * 128]), "v")

            @blk.tensor
            def _(t, j=j):
                self.w(t, "v")
                for dc in range(8):
                    self.inc(t.matmul(self.pW[0][:, 0:8],
                                      ynT[:, dc, j * 128:(j + 1) * 128],
                                      self.wWs[:, dc, :], start=(dc == 0), stop=(dc == 7)), "t")

            @blk.vector
            def _(v, j=j):
                self.w(v, "t")
                self.incw(v, v.tensor_tensor(out=zr[:], in0=self.pW[0][:, 0:8],
                                         in1=self.t_bsbc, op=OP.add), "v")
                self.incw(v, v.tensor_reduce(out=self.sccol[:, 0:1], in_=zr[:], axis=AX.X,
                                         op=OP.max), "v")
                self.incw(v, v.tensor_scalar(out=self.sccol[:, 1:2], in0=self.sccol[:, 0:1],
                                         scalar1=-1.0, scalar2=None, op0=OP.mult), "v")

            @blk.scalar
            def _(s, j=j):
                self.w(s, "v")
                self.inc(s.activation(out=er[:], in_=zr[:], func=AF.Exp,
                                      bias=self.sccol[:, 1:2], scale=self.cc1("one"),
                                      accum_out=self.sccol[:, 2:3]), "s")

            @blk.vector
            def _(v, j=j):
                self.w(v, "s")
                self.incw(v, v.reciprocal(self.sccol[:, 3:4], self.sccol[:, 2:3]), "v")
                self.w(v, "v")
                self.incw(v, v.tensor_scalar(out=er[:], in0=er[:], scalar1=self.sccol[:, 3:4],
                                         scalar2=None, op0=OP.mult), "v")

            @blk.tensor
            def _(t, j=j):
                self.w(t, "v")
                self.inc(t.transpose(self.pM[0:8, 0:128], er[:], self.t_identf), "t")

            @blk.vector
            def _(v, j=j):
                self.w(v, "t")
                self.incw(v, v.tensor_copy(pTl[:, j * 128:(j + 1) * 128], self.pM[0:8, 0:128]), "v")

        @blk.sync
        def _(q):
            self.w(q, "v")
            self.inc(q.dma_start(out=self.pT_loc[:, :], in_=pTl[:]), "d_pt", 16)

        if self.dbg("y"):
            pass

    def p8_collect(self):
        blk = self.block

        @blk.gpsimd
        def _(g):
            self.w(g, "d_yn0")
            self.w(g, "d_yn1")
            self.w(g, "d_pt")
            self.inc(g.collective_compute(
                "AllGather", OP.bypass, replica_groups=[list(range(8))],
                ins=[self.yn_loc[:, :]], outs=[self.yn_all[:, :]]), "coll")
            self.inc(g.collective_compute(
                "AllGather", OP.bypass, replica_groups=[list(range(8))],
                ins=[self.pT_loc[:, :]], outs=[self.pT_all[:, :]]), "coll")

    def p9_route(self, stack):
        blk = self.block
        pc = self.sbctx(stack, "pc", (128, 1, 128))
        self.wrapB = self.sbctx(stack, "wrapB", (128, NTOK // 16), I16)

        @blk.vector
        def _(v):
            self.incw(v, v.memset(self.wrapB[:], 0), "v")
        lo = self.sccol[:, 4:5]
        hi = self.sccol[:, 5:6]
        cntc = self.sccol[:, 6:7]
        gec = self.sccol[:, 7:8]
        idx8 = self.sbctx(stack, "idx8", (128, 8), I16)
        gtf = self.sbctx(stack, "gtf", (128, 128))
        tie = self.sbctx(stack, "tie", (128, 128))
        csr = self.sbctx(stack, "csr", (128, 128))
        hstage = self.sbctx(stack, "hstage", (128, 256), BF16)
        hgat = self.sbctx(stack, "hgat", (128, 16, 256), BF16)
        offrow = self.sbctx(stack, "offrow", (1, 128))
        orbc = self.sbctx(stack, "orbc", (128, 128))
        pqt = self.sbctx(stack, "pqt", (128, 16))
        kqt = self.sbctx(stack, "kqt", (128, 32))

        idxn = self.sbctx(stack, "idxn", (128, 1))

        @blk.vector
        def _(v):
            self.incw(v, v.memset(idx8[:], 0), "v")

        @blk.vector
        def _(v):
            self.w(v, "coll")
            self.incw(v, v.tensor_scalar(out=self.sccol[:, 0:1], in0=self.t_cid, scalar1=16.0,
                                     scalar2=None, op0=OP.mult), "v")
            self.incw(v, v.tensor_tensor(out=idxn[:], in0=self.t_r128s,
                                     in1=self.sccol[:, 0:1], op=OP.add), "v")

        self.wrap_build(128, idxn[:, :], idx8)

        @blk.gpsimd
        def _(g):
            self.w(g, "v")
            self.inc(g.dma_gather(
                out_ap=pc[:, :, :],
                in_ap=self.pT_all.rearrange("e (c j) -> (e c) j", j=128),
                idxs_ap=idx8[:, :], num_idxs=128, num_idxs_reg=128,
                elem_size=128), "d_g", 16)

        @blk.vector
        def _(v):
            self.w(v, "d_g")
            self.incw(v, v.memset(lo, 0.0), "v")
            self.incw(v, v.memset(hi, 1.0), "v")

        for it in range(NBIS):
            @blk.vector
            def _(v, it=it):
                self.w(v, "t")
                self.incw(v, v.tensor_tensor(out=self.sccol[:, 0:1], in0=lo, in1=hi, op=OP.add), "v")
                self.incw(v, v.tensor_scalar(out=self.sccol[:, 0:1], in0=self.sccol[:, 0:1],
                                         scalar1=0.5, scalar2=None, op0=OP.mult), "v")
                self.w(v, "v")
                self.incw(v, v.tensor_scalar(out=self.scu8[:, 0:128], in0=pc[:, 0, :],
                                         scalar1=self.sccol[:, 0:1], scalar2=0.0,
                                         op0=OP.is_gt, op1=OP.add,
                                         accum_out=cntc), "v")

            @blk.tensor
            def _(t, it=it):
                self.w(t, "v")
                self.inc(t.matmul(self.pM[:, 0:1], self.t_ones, cntc,
                                  start=True, stop=True), "t")

            @blk.vector
            def _(v, it=it):
                self.w(v, "t")
                self.incw(v, v.tensor_copy(cntc, self.pM[:, 0:1]), "v")
                self.incw(v, v.tensor_scalar(out=gec, in0=cntc, scalar1=float(KEXP),
                                         scalar2=None, op0=OP.is_ge), "v")
                self.incw(v, v.tensor_tensor(out=self.sccol[:, 2:3], in0=self.sccol[:, 0:1],
                                         in1=lo, op=OP.subtract), "v")
                self.w(v, "v")
                self.incw(v, v.scalar_tensor_tensor(out=lo, in0=self.sccol[:, 2:3],
                                                scalar=gec, in1=lo,
                                                op0=OP.mult, op1=OP.add), "v")
                self.incw(v, v.tensor_scalar(out=self.sccol[:, 3:4], in0=gec, scalar1=-1.0,
                                         scalar2=1.0, op0=OP.mult, op1=OP.add), "v")
                self.incw(v, v.tensor_tensor(out=self.sccol[:, 2:3], in0=self.sccol[:, 0:1],
                                         in1=hi, op=OP.subtract), "v")
                self.w(v, "v")
                self.incw(v, v.scalar_tensor_tensor(out=hi, in0=self.sccol[:, 2:3],
                                                scalar=self.sccol[:, 3:4], in1=hi,
                                                op0=OP.mult, op1=OP.add), "v")

        @blk.vector
        def _(v):
            self.w(v, "v")
            self.incw(v, v.tensor_scalar(out=gtf[:], in0=pc[:, 0, :], scalar1=hi,
                                     scalar2=0.0, op0=OP.is_gt, op1=OP.add,
                                     accum_out=cntc), "v")

        @blk.tensor
        def _(t):
            self.w(t, "v")
            self.inc(t.matmul(self.pM[:, 0:1], self.t_ones, cntc, start=True, stop=True), "t")

        @blk.vector
        def _(v):
            self.w(v, "t")
            self.incw(v, v.tensor_copy(cntc, self.pM[:, 0:1]), "v")
            self.incw(v, v.tensor_scalar(out=self.sccol[:, 2:3], in0=cntc, scalar1=-1.0,
                                     scalar2=float(KEXP), op0=OP.mult, op1=OP.add), "v")
            self.w(v, "v")
            self.incw(v, v.tensor_scalar(out=tie[:], in0=pc[:, 0, :], scalar1=lo,
                                     scalar2=None, op0=OP.is_gt), "v")
            self.incw(v, v.tensor_tensor(out=tie[:], in0=tie[:], in1=gtf[:], op=OP.subtract), "v")
            self.incw(v, v.tensor_tensor_scan(out=csr[:], data0=self.t_onesw[:, 0:128],
                                          data1=tie[:], initial=0.0,
                                          op0=OP.mult, op1=OP.add), "v")
            self.incw(v, v.tensor_copy(self.sccol[:, 0:1], csr[:, 127:128]), "v")

        @blk.tensor
        def _(t):
            self.w(t, "v")
            self.inc(t.matmul(self.pM[:, 0:1], self.t_upst, self.sccol[:, 0:1],
                              start=True, stop=True), "t")

        @blk.vector
        def _(v):
            self.incw(v, v.tensor_tensor_scan(out=csr[:], data0=self.t_onesw[:, 0:128],
                                          data1=gtf[:], initial=0.0,
                                          op0=OP.mult, op1=OP.add), "v")
            self.incw(v, v.tensor_copy(self.sccol[:, 0:1], csr[:, 127:128]), "v")

        @blk.tensor
        def _(t):
            self.w(t, "v")
            self.inc(t.matmul(self.pM[:, 0:1], self.t_upst, self.sccol[:, 0:1],
                              start=True, stop=True), "t")

        @blk.vector
        def _(v):
            self.w(v, "t")
            self.incw(v, v.tensor_copy(self.sccol[:, 1:2], self.pM[:, 0:1]), "v")
            self.incw(v, v.tensor_scalar(out=self.sccol[:, 2:3], in0=self.sccol[:, 0:1],
                                     scalar1=0.0, scalar2=None, op0=OP.is_equal), "v")
            self.incw(v, v.scalar_tensor_tensor(out=self.sccol[:, 2:3], in0=self.sccol[:, 2:3],
                                            scalar=1e9, in1=self.sccol[:, 1:2],
                                            op0=OP.mult, op1=OP.add), "v")
            self.incw(v, v.tensor_tensor(out=tie[:], in0=pc[:, 0, :], in1=gtf[:],
                                     op=OP.mult), "v")
            self.incw(v, v.tensor_copy(hstage[:, 0:128], csr[:]), "v")
            self.incw(v, v.tensor_copy(hstage[:, 128:256], tie[:]), "v")

        @blk.tensor
        def _(t):
            self.w(t, "v")
            self.inc(t.transpose(self.pM[0:1, 0:128], self.sccol[:, 2:3], self.t_identf), "t")

        @blk.vector
        def _(v):
            self.w(v, "t")
            self.incw(v, v.tensor_copy(offrow[0:1, 0:128], self.pM[0:1, 0:128]), "v")

        @blk.gpsimd
        def _(g):
            self.w(g, "v")
            g.partition_broadcast(orbc[:, :], offrow[0:1, 0:128])
            self.inc(g.nop(), "g")

        @blk.vector
        def _(v):
            self.w(v, "g")
            for qb in range(16):
                qv = self.t_iotacm[:, qb:qb + 1]
                orB = orbc[:, :]
                qvB = self.bcast_free(qv, 128)
                self.incw(v, v.tensor_tensor(out=csr[:], in0=orB, in1=qvB, op=OP.is_le), "v")
                self.incw(v, v.tensor_reduce(out=pqt[:, qb:qb + 1], in_=csr[:], axis=AX.X,
                                         op=OP.add), "v")
                self.incw(v, v.tensor_tensor(out=csr[:], in0=csr[:], in1=orB, op=OP.mult), "v")
                self.incw(v, v.tensor_reduce(out=kqt[:, qb:qb + 1], in_=csr[:], axis=AX.X,
                                         op=OP.max), "v")
            self.incw(v, v.tensor_scalar(out=pqt[:, 0:16], in0=pqt[:, 0:16], scalar1=-1.0,
                                     scalar2=None, op0=OP.add), "v")
            self.incw(v, v.tensor_tensor(out=kqt[:, 0:16], in0=self.t_iotacm[:, 0:16],
                                     in1=kqt[:, 0:16], op=OP.subtract), "v")
            self.incw(v, v.tensor_scalar(out=kqt[:, 16:32], in0=kqt[:, 0:16], scalar1=1.0,
                                     scalar2=None, op0=OP.add), "v")

        @blk.sync
        def _(q):
            self.w(q, "v")
            self.inc(q.dma_start(out=self.h9_hbm[:, :], in_=hstage[:, 0:256]), "d_tx", 16)

        self.wrap_build(KEXP, pqt[:, 0:16], self.wrapC)

        @blk.gpsimd
        def _(g):
            self.w(g, "v")
            self.w(g, "d_tx")
            self.inc(g.dma_gather(out_ap=hgat[:, :, :], in_ap=self.h9_hbm[:, :],
                                  idxs_ap=self.wrapC[:, :], num_idxs=KEXP,
                                  num_idxs_reg=KEXP, elem_size=256,
                                  single_packet=False), "d_g", 16)

        @blk.vector
        def _(v):
            self.w(v, "d_g")
            for qb in range(16):
                self.incw(v, v.tensor_scalar(out=csr[:], in0=hgat[:, qb, 0:128],
                                         scalar1=kqt[:, qb:qb + 1], scalar2=0.0,
                                         op0=OP.is_le, op1=OP.add,
                                         accum_out=self.sccol[:, 3:4]), "v")
                self.incw(v, v.scalar_tensor_tensor(out=self.tid[:, qb:qb + 1],
                                                in0=pqt[:, qb:qb + 1], scalar=128.0,
                                                in1=self.sccol[:, 3:4],
                                                op0=OP.mult, op1=OP.add), "v")
                self.incw(v, v.tensor_scalar(out=csr[:], in0=hgat[:, qb, 0:128],
                                         scalar1=kqt[:, 16 + qb:17 + qb], scalar2=None,
                                         op0=OP.is_equal), "v")
                self.incw(v, v.tensor_tensor_reduce(out=gtf[:], in0=csr[:],
                                                in1=hgat[:, qb, 128:256], scale=1.0,
                                                scalar=0.0, op0=OP.mult, op1=OP.add,
                                                accum_out=self.tval[:, qb:qb + 1]), "v")

        self.wrap_build(KEXP, self.tid[:, :], self.wrapC)

        if self.dbg("omout") or self.dbg("dbg9"):
            dst = self.dbg_out("tid", (128, 16))
            dst2 = self.dbg_out("tval", (128, 16))

            @blk.sync
            def _(q):
                self.w(q, "v")
                self.inc(q.dma_start(out=dst[:, :], in_=self.tid[:, :]), "d_dbg", 16)
                self.inc(q.dma_start(out=dst2[:, :], in_=self.tval[:, :]), "d_dbg", 16)

    def p10_ffn(self, stack):
        blk = self.block
        w2t = self.sbctx(stack, "w2t", (128, 32, D), BF16)
        hT = self.sbctx(stack, "hT", (128, 32, TBF), BF16)
        xg = self.sbctx(stack, "xg", (128, 4, D), BF16)
        xinT = self.sbctx(stack, "xinT", (128, 8, TBF), BF16)
        w1t = [self.sbctx(stack, f"w1t{i}", (128, 8, 128), BF16) for i in range(2)]
        ob = [self.sbctx(stack, f"ob{i}", (128, D)) for i in range(2)]

        @blk.gpsimd
        def _(g):
            self.w(g, "d_wo")
            self.w(g, "t")
            self.w(g, "v")
            self.w(g, "s")
            self.inc(g.dma_start(out=w2t[:],
                                 in_=self.W2_in.rearrange("(c p) o -> p c o", p=128)),
                     "d_w2", 16)

        w2_tok = self.cnt["d_w2"]

        for tb in range(NTB):
            @blk.gpsimd
            def _(g, tb=tb):
                self.w(g, "v")
                self.inc(g.dma_gather(out_ap=xg[:, :, :], in_ap=self.yn_all[:, :],
                                      idxs_ap=self.wrapC[:, 32 * tb:32 * tb + 32],
                                      num_idxs=TBF, num_idxs_reg=TBF, elem_size=D),
                         "d_g", 16)

            for sb4 in range(4):
                @blk.tensor
                def _(t, tb=tb, sb4=sb4):
                    self.w(t, "d_g")
                    self.w(t, "v")
                    for dc in range(8):
                        self.inc(t.transpose(self.pT1[:, dc * 128:(dc + 1) * 128],
                                             xg[:, sb4, dc * 128:(dc + 1) * 128],
                                             self.t_identb), "t")

                @blk.vector
                def _(v, tb=tb, sb4=sb4):
                    self.w(v, "t")
                    for dc in range(8):
                        self.incw(v, v.tensor_copy(xinT[:, dc, sb4 * 128:(sb4 + 1) * 128],
                                               self.pT1[:, dc * 128:(dc + 1) * 128]), "v")

            @blk.gpsimd
            def _(g, tb=tb):
                self.w(g, "v")
                self.w(g, "t")
                self.inc(g.dma_start(
                    out=w1t[0][:, :, :],
                    in_=self.w1bf_hbm[0:D, :].rearrange("(c p) k -> p c k", p=128)),
                    "d_w1_0", 16)

            for fb in range(32):
                @blk.gpsimd
                def _(g, tb=tb, fb=fb):
                    if fb + 1 < 32:
                        self.w(g, "t")
                        self.inc(g.dma_start(
                            out=w1t[(fb + 1) % 2][:, :, :],
                            in_=self.w1bf_hbm[(fb + 1) * D:(fb + 2) * D, :].rearrange(
                                "(c p) k -> p c k", p=128)), f"d_w1_{(fb + 1) % 2}", 16)

                @blk.tensor
                def _(t, tb=tb, fb=fb):
                    self.w(t, f"d_w1_{fb % 2}")
                    self.w(t, "s")
                    for dc in range(8):
                        self.inc(t.matmul(self.pW[fb % 2][:, 0:TBF],
                                          w1t[fb % 2][:, dc, :],
                                          xinT[:, dc, :],
                                          start=(dc == 0), stop=(dc == 7)), "t")

                @blk.scalar
                def _(s, tb=tb, fb=fb):
                    self.w(s, "t")
                    self.inc(s.activation(out=hT[:, fb, :], in_=self.pW[fb % 2][:, 0:TBF],
                                          func=AF.Gelu_apprx_tanh,
                                          bias=self.t_b1c[:, fb:fb + 1],
                                          scale=self.cc1("one")), "s")

            for obi in range(TBF // 128):
                qi = tb * 4 + obi

                @blk.tensor
                def _(t, tb=tb, obi=obi):
                    self.w(t, "s")
                    self.w(t, "d_w2", w2_tok)
                    self.w(t, "v")
                    for half in range(2):
                        for fb in range(32):
                            self.inc(t.matmul(self.pW[half][:, 0:512],
                                              hT[:, fb, obi * 128:(obi + 1) * 128],
                                              w2t[:, fb, half * 512:(half + 1) * 512],
                                              start=(fb == 0), stop=(fb == 31)), "t")

                @blk.vector
                def _(v, tb=tb, obi=obi, qi=qi):
                    self.w(v, "t")
                    self.w(v, f"d_ob{qi % 2}")
                    for half in range(2):
                        sl = slice(half * 512, (half + 1) * 512)
                        self.incw(v, v.tensor_tensor(out=self.sc1[:, sl],
                                                 in0=self.pW[half][:, 0:512],
                                                 in1=self.t_b2bc[:, sl], op=OP.add), "v")
                    self.incw(v, v.tensor_scalar(out=ob[qi % 2][:], in0=self.sc1[:, 0:D],
                                             scalar1=self.tval[:, qi:qi + 1],
                                             scalar2=None, op0=OP.mult), "v")

                @blk.gpsimd
                def _(g, qi=qi):
                    self.w(g, "v")
                    self.w(g, "d_z2")
                    self.inc(g.dma_scatter_add(
                        out_ap=self.dense_hbm[:, :],
                        in_ap=ob[qi % 2].rearrange("p (a d) -> p a d", a=1),
                        idxs_ap=self.wrapC[:, 8 * qi:8 * qi + 8], num_idxs=128,
                        num_idxs_reg=128, elem_size=D, single_packet=False),
                        f"d_ob{qi % 2}", 16)

                if self.dbg("obdbg"):
                    if not hasattr(self, "_obdbg"):
                        self._obdbg = self.dbg_out("obuf", (KEXP, D))

                    @blk.sync
                    def _(q, qi=qi):
                        self.w(q, "v")
                        self.inc(q.dma_start(
                            out=self._obdbg.rearrange("(c p) d -> p c d", p=128)[:, qi, :],
                            in_=ob[qi % 2][:]), "d_dbg", 16)

    def p11_collect(self):
        blk = self.block

        @blk.gpsimd
        def _(g):
            self.w(g, "d_ob0")
            self.w(g, "d_ob1")
            self.inc(g.collective_compute(
                "ReduceScatter", OP.add, replica_groups=[list(range(8))],
                ins=[self.dense_hbm[:, :]], outs=[self.dense_rs[:, :]]), "coll")

    def p12_combine(self, stack):
        blk = self.block
        ab = [self.sbctx(stack, f"adb{i}", (128, D)) for i in range(2)]

        @blk.sync
        def _(q):
            self.w(q, "coll")
            for j in range(2):
                self.inc(q.dma_start(
                    out=ab[j][:],
                    in_=self.dense_rs.rearrange("(c p) d -> p c d", p=128)[:, j, :]),
                    f"d_x{j % 2}", 16)

        for j in range(NT):
            @blk.gpsimd
            def _(g, j=j):
                self.w(g, f"d_x{j % 2}")
                self.w(g, "d_y0")
                self.w(g, "d_y1")
                self.inc(g.dma_start(
                    out=self.out.rearrange("(c p) d -> p c d", p=128)[:, j, :],
                    in_=ab[j % 2][:], accum_op=OP.add), "d_comb", 16)

            if j + 2 < NT:
                @blk.sync
                def _(q, j=j):
                    self.w(q, "d_comb")
                    self.inc(q.dma_start(
                        out=ab[j % 2][:],
                        in_=self.dense_rs.rearrange("(c p) d -> p c d", p=128)[:, j + 2, :]),
                        f"d_x{j % 2}", 16)

    def finale(self):
        blk = self.block

        @blk.gpsimd
        def _(g):
            for name in list(self._sems):
                if name.startswith("d_") or name in ("v", "s", "t", "g", "coll"):
                    self.w(g, name)

    def build(self):
        from contextlib import ExitStack
        self.declare()
        with ExitStack() as top:
            self.block = top.enter_context(self.nc.Block())
            for nm in ("d_y0", "d_y1", "d_yn0", "d_yn1", "d_ob0", "d_ob1",
                       "d_om", "d_sh", "d_g", "d_wm", "coll"):
                self.sem(nm)
            self.sem("t")
            self.sem("v")
            self.sem("s")
            self.sem("g")
            self.p0_setup()
            if self.upto >= 10:
                self.p0b_precast_w1()
            if self.upto >= 1:
                with ExitStack() as stA:
                    self.xnT = self.sbctx(stA, "xnT", (128, 8, S), BF16)
                    self.p1_ln_outer(stA)
                    if self.upto >= 6:
                        with ExitStack() as st67:
                            self.og = self.sbctx(st67, "og", (128, NT, DV), BF16)
                            with ExitStack() as st:
                                self.p6_reco(st)
                            if self.upto >= 7:
                                with ExitStack() as st:
                                    self.p7_out(st)
            if self.upto >= 8:
                self.p8_collect()
            if self.upto >= 9:
                self.phase_barrier()
                with ExitStack() as st:
                    self.p9_route(st)
            if self.upto >= 10:
                self.phase_barrier()
                with ExitStack() as st:
                    self.p10_ffn(st)
            if self.upto >= 11:
                self.p11_collect()
            if self.upto >= 12:
                self.phase_barrier()
                with ExitStack() as st:
                    self.p12_combine(st)
            self.finale()
        lower_extended_insts(self.nc)
        return self.nc

    def p1_ln_outer(self, st_outer):
        from contextlib import ExitStack
        with ExitStack() as st:
            self.p1_ln(st)
            if self.upto >= 2:
                self.p2_gate(st)
        if self.upto >= 4:
            with ExitStack() as st:
                self.p_gla(st)

    def p0_setup(self):
        blk = self.block

        @blk.sync
        def _(q):
            for dst, src in [
                (self.t_cc, self.cc_in), (self.t_identf, self.identf_in),
                (self.t_identb, self.identb_in), (self.t_triu, self.triu_in),
                (self.t_upst, self.upst_in), (self.t_ones, self.onesm_in),
                (self.t_iota4, self.iota4_in), (self.t_iotatg, self.iotatg_in),
                (self.t_iotacm, self.iotacm_in), (self.t_r128s, self.r128s_in),
                (self.t_iotarows, self.iotarows_in), (self.t_onesw, self.onesw_in),
                (self.t_normw, self.normw_in), (self.t_bsbc, self.bs_in),
                (self.t_b2bc, self.b2_in), (self.t_b1c, self.b1_in),
                (self.t_bgk, self.bgk_in), (self.t_bgks, self.bgks_in),
                (self.t_cid1, self.cid_in),
            ]:
                self.inc(q.dma_start(out=dst[:], in_=src[:, :]), "d_c", 16)

        @blk.vector
        def _(v):
            self.w(v, "d_c")
            self.inc(v.memset(self.zf32[:], 0.0), "v")
            for wt in (self.wrap0, self.wrap1, self.wrapC):
                self.inc(v.memset(wt[:], 0), "v")

        @blk.scalar
        def _(s):
            self.w(s, "d_c")

        @blk.tensor
        def _(t):
            self.w(t, "d_c")

        @blk.gpsimd
        def _(g):
            g.load_library(MLP_LIB)
            self.inc(g.dma_start(out=self.wgate[:],
                                 in_=self.w_gate_in.rearrange("(c p) o -> p c o", p=128)),
                     "d_w", 16)
            self.inc(g.dma_start(out=self.wq[:],
                                 in_=self.w_q_in.rearrange("(c p) o -> p c o", p=128)),
                     "d_w", 16)
            self.w(g, "d_c")
            g.partition_broadcast(self.t_cid[:], self.t_cid1[0:1, 0:1])
            self.inc(g.nop(), "g")
            self.w(g, "v")
            for r0 in range(0, NMEM * CAP, 128):
                self.inc(g.dma_start(
                    out=self.tx_hbm[r0:r0 + 128, :],
                    in_=self.zf32[:, :]), "d_z", 16)
            self.inc(g.dma_start(out=self.tx_hbm[NMEM * CAP:NMEM * CAP + 8, :],
                                 in_=self.zf32[0:8, :]), "d_z", 16)
            self.inc(g.dma_start(out=self.om_hbm[NMEM * CAP:NMEM * CAP + 8, :],
                                 in_=self.zf32[0:8, :]), "d_z", 16)

        @blk.scalar
        def _(s):
            self.w(s, "v")
            for r0 in range(0, NTOK, 128):
                self.inc(s.dma_start(out=self.dense_hbm[r0:r0 + 128, :],
                                     in_=self.zf32[:, :]), "d_z2", 16)


    def p0b_precast_w1(self):
        blk = self.block

        @blk.gpsimd
        def _(g):
            for dc in range(8):
                for ch in range(2):
                    self.inc(g.dma_start(
                        out=self.t_w1s[:, :],
                        in_=self.W1_in[dc * 128:(dc + 1) * 128,
                                       ch * 2048:(ch + 1) * 2048]), "d_w1a", 16)
                    self.w(g, "d_w1a")
                    self.inc(g.dma_start(
                        out=self.w1bf_hbm.rearrange("(f c p) k -> c p f k", c=8, p=128)[
                            dc][:, ch * 16:(ch + 1) * 16, :],
                        in_=self.t_w1s.rearrange("p (f k) -> p f k", k=128)), "d_w1b", 16)
                    self.w(g, "d_w1b")


_NC = None


def kernel(**inputs):
    global _NC
    in_maps = host_inputs(inputs)
    if _NC is None:
        _NC = MK(upto=12).build()
    res = run_bass_kernel_spmd(_NC, in_maps, core_ids=list(range(8)))
    out = np.stack([np.asarray(r["out"], np.float32) for r in res.results], 0)
    return out.reshape(8, 2048, 1024)


# version 05
# speedup vs baseline: 2.6400x; 1.0311x over previous
from concourse.bass_utils import run_bass_kernel_spmd

"""Megakernel builder: GLA mixture-of-memories attention + expert-choice MoE FFN.

One raw-Bass SPMD program (identical across cores). Core c's in_map carries batch
row c (attention) and expert c's FFN weights. Cross-core via AllGather. NOTES.md
has the full design. v1: conservative (serializing) semaphore discipline.
"""
import numpy as np
import concourse.bass as bass
import concourse.mybir as mybir
from concourse.library_config import mlp as MLP_LIB
from concourse.library_overlay import lower_extended_insts

F32 = mybir.dt.float32
BF16 = mybir.dt.bfloat16
I16 = mybir.dt.int16
I32 = mybir.dt.int32
U8 = mybir.dt.uint8
AF = mybir.ActivationFunctionType
OP = mybir.AluOpType
AX = mybir.AxisListType

S, D = 2048, 1024
NMEM, CAP, H = 4, 1024, 4
DK, DV, HDK, HDV, LOW = 512, 1024, 128, 256, 16
NE, FF, KEXP, NTOK = 8, 4096, 2048, 16384
NT = S // 128
GATE_NORM, EPS = 16.0, 1e-5
QSCALE = float(HDK) ** -0.5
TBF = 512
NTB = KEXP // TBF
NBIS = 44

CC = dict(zero=0, one=1, eps=2, lnqs=3, inv16=4, neg16=5, negone=6, half=7)
NCC = 8


def host_constants():
    import ml_dtypes
    cc = np.zeros((128, NCC), np.float32)
    cc[:, CC["one"]] = 1.0
    cc[:, CC["eps"]] = EPS
    cc[:, CC["lnqs"]] = np.log(QSCALE)
    cc[:, CC["inv16"]] = 1.0 / GATE_NORM
    cc[:, CC["neg16"]] = -1.0 / GATE_NORM
    cc[:, CC["negone"]] = -1.0
    cc[:, CC["half"]] = 0.5
    p = np.arange(128)
    return {
        "cc": cc,
        "ident_f": np.eye(128, dtype=np.float32),
        "ident_b": np.eye(128, dtype=ml_dtypes.bfloat16),
        "triu_inc": np.triu(np.ones((128, 128), np.float32)).astype(ml_dtypes.bfloat16),
        "upper_strict": np.triu(np.ones((128, 128), np.float32), 1),
        "ones_mat": np.ones((128, 128), np.float32),
        "iota4_bc": np.tile(np.arange(4, dtype=np.float32), (128, 1)),
        "iota_tg": (128.0 * p[:, None] + np.arange(128)[None, :]).astype(np.float32),
        "iota_cm": (p[:, None] + 128.0 * np.arange(128)[None, :]).astype(np.float32),
        "r128s": (128.0 * (p // 16) + (p % 16)).astype(np.float32).reshape(128, 1),
        "iota_rows": ((np.arange(128)[None, :] // 16) * 2048.0
                      + (np.arange(128)[None, :] % 16) * 128.0
                      + p[:, None] * 1.0).astype(np.float32),
        "ones_w": np.ones((128, S), ml_dtypes.bfloat16),
    }


def host_inputs(inputs):
    consts = host_constants()
    x = np.asarray(inputs["L0_attn_x"], np.float32)
    bgk = np.asarray(inputs["L0_attn_b_gk"], np.float32)
    bgk_t = np.zeros((128, NMEM * H), np.float32)
    for m in range(NMEM):
        for h in range(H):
            bgk_t[:, m * H + h] = -bgk[m, h * 128:(h + 1) * 128]
    bgks = np.asarray(inputs["L0_attn_b_gks"], np.float32)
    bgks_t = np.zeros((128, H), np.float32)
    for h in range(H):
        bgks_t[:, h] = -bgks[h * 128:(h + 1) * 128]
    normw = np.asarray(inputs["L0_attn_norm_w"], np.float32)
    b1 = np.asarray(inputs["L0_ffn_b1"], np.float32)
    b2 = np.asarray(inputs["L0_ffn_b2"], np.float32)
    W1 = np.asarray(inputs["L0_ffn_W1"], np.float32)
    W2 = np.asarray(inputs["L0_ffn_W2"], np.float32)
    base = {
        "w_gate": np.asarray(inputs["L0_attn_w_gate"], np.float32),
        "w_q": np.asarray(inputs["L0_attn_w_q"], np.float32),
        "w_k": np.asarray(inputs["L0_attn_w_k"], np.float32),
        "w_v": np.asarray(inputs["L0_attn_w_v"], np.float32),
        "w_gk1": np.asarray(inputs["L0_attn_w_gk1"], np.float32),
        "w_gk2": np.asarray(inputs["L0_attn_w_gk2"], np.float32),
        "b_gk_t": bgk_t,
        "w_ks": np.asarray(inputs["L0_attn_w_ks"], np.float32),
        "w_vs": np.asarray(inputs["L0_attn_w_vs"], np.float32),
        "w_gk1s": np.asarray(inputs["L0_attn_w_gk1s"], np.float32),
        "w_gk2s": np.asarray(inputs["L0_attn_w_gk2s"], np.float32),
        "b_gks_t": bgks_t,
        "w_g": np.asarray(inputs["L0_attn_w_g"], np.float32),
        "w_o": np.asarray(inputs["L0_attn_w_o"], np.float32),
        "normw_bc": np.tile(np.tile(normw, H)[None, :], (128, 1)).astype(np.float32),
        "ffn_Ws": np.asarray(inputs["L0_ffn_Ws"], np.float32),
        "ffn_bs_bc": np.tile(np.asarray(inputs["L0_ffn_bs"], np.float32)[None, :], (128, 1)),
    }
    base.update(consts)
    in_maps = []
    for c in range(8):
        m = dict(base)
        m["x"] = np.ascontiguousarray(x[c])
        m["ffn_W1"] = np.ascontiguousarray(W1[c])
        m["ffn_b1c"] = np.ascontiguousarray(b1[c].reshape(FF // 128, 128).T)
        m["ffn_W2"] = np.ascontiguousarray(W2[c])
        m["ffn_b2_bc"] = np.tile(b2[c][None, :], (128, 1)).astype(np.float32)
        m["coreid"] = np.full((1, 1), float(c), np.float32)
        in_maps.append(m)
    return in_maps


class MK:
    def __init__(self, upto=99, debug=(), gla_order=(4, 0, 1, 2, 3), p12mode="full"):
        self.p12mode = p12mode
        self.gla_order = list(gla_order)
        self.upto = upto
        self.debug = set(debug)
        self.dbg_t = {}
        self.nc = bass.Bass("TRN2", target_bir_lowering=False, debug=False,
                            detect_race_conditions=True)
        self._sems = {}
        self.cnt = {}

    def sem(self, name):
        if name not in self._sems:
            self._sems[name] = self.nc.alloc_semaphore(name)
            self.cnt[name] = 0
        return self._sems[name]

    def inc(self, instr, name, k=1):
        h = self.sem(name)
        instr.then_inc(h, k)
        self.cnt[name] += k
        return self.cnt[name]

    def incw(self, eng, instr, name, k=1):
        n = self.inc(instr, name, k)
        eng.wait_ge(self.sem(name), n)
        return n

    def w(self, eng, name, val=None):
        h = self.sem(name)
        v = self.cnt[name] if val is None else val
        if v > 0:
            eng.wait_ge(h, v)

    def din(self, name, shape, dtype=F32):
        return self.nc.dram_tensor(name, list(shape), dtype, kind="ExternalInput").ap()

    def dout(self, name, shape, dtype=F32):
        return self.nc.dram_tensor(name, list(shape), dtype, kind="ExternalOutput").ap()

    def dint(self, name, shape, dtype=F32):
        kind = "ExternalOutput" if "omout" in self.debug else "Internal"
        return self.nc.dram_tensor(name, list(shape), dtype, kind=kind).ap()

    def sb(self, name, shape, dtype=F32):
        return self.nc.alloc_sbuf_tensor(name, list(shape), dtype).ap()

    def sbctx(self, stack, name, shape, dtype=F32):
        t = stack.enter_context(self.nc.sbuf_tensor(name, list(shape), dtype))
        return t.ap() if hasattr(t, "ap") else t[:]

    def dbg(self, name):
        return name in self.debug

    def dbg_out(self, name, shape, dtype=F32):
        self.dbg_t[name] = self.dout("dbg_" + name, shape, dtype)
        return self.dbg_t[name]

    def cc1(self, key):
        return self.t_cc[:, CC[key]:CC[key] + 1]

    def dump_tile(self, tile_ap, name, shape, dtype=F32):
        if not self.dbg(name):
            return
        dst = self.dbg_out(name, shape, dtype)
        blk = self.block

        @blk.sync
        def _(q):
            self.w(q, "v")
            self.w(q, "s")
            self.inc(q.dma_start(out=dst[:, :] if len(shape) == 2 else dst[:, :, :],
                                 in_=tile_ap), "d_dbg", 16)

    def declare(self):
        self.x_in = self.din("x", (S, D))
        self.w_gate_in = self.din("w_gate", (D, NMEM))
        self.w_q_in = self.din("w_q", (D, DK))
        self.w_k_in = self.din("w_k", (NMEM, D, DK))
        self.w_v_in = self.din("w_v", (NMEM, D, DV))
        self.w_gk1_in = self.din("w_gk1", (NMEM, D, LOW))
        self.w_gk2_in = self.din("w_gk2", (NMEM, LOW, DK))
        self.bgk_in = self.din("b_gk_t", (128, NMEM * H))
        self.w_ks_in = self.din("w_ks", (D, DK))
        self.w_vs_in = self.din("w_vs", (D, DV))
        self.w_gk1s_in = self.din("w_gk1s", (D, LOW))
        self.w_gk2s_in = self.din("w_gk2s", (LOW, DK))
        self.bgks_in = self.din("b_gks_t", (128, H))
        self.w_g_in = self.din("w_g", (D, DV))
        self.w_o_in = self.din("w_o", (DV, D))
        self.normw_in = self.din("normw_bc", (128, DV))
        self.Ws_in = self.din("ffn_Ws", (D, NE))
        self.bs_in = self.din("ffn_bs_bc", (128, NE))
        self.W1_in = self.din("ffn_W1", (D, FF))
        self.b1_in = self.din("ffn_b1c", (128, FF // 128))
        self.W2_in = self.din("ffn_W2", (FF, D))
        self.b2_in = self.din("ffn_b2_bc", (128, D))
        self.cid_in = self.din("coreid", (1, 1))
        self.cc_in = self.din("cc", (128, NCC))
        self.identf_in = self.din("ident_f", (128, 128))
        self.identb_in = self.din("ident_b", (128, 128), BF16)
        self.triu_in = self.din("triu_inc", (128, 128), BF16)
        self.upst_in = self.din("upper_strict", (128, 128))
        self.onesm_in = self.din("ones_mat", (128, 128))
        self.iota4_in = self.din("iota4_bc", (128, 4))
        self.iotatg_in = self.din("iota_tg", (128, 128))
        self.iotacm_in = self.din("iota_cm", (128, 128))
        self.r128s_in = self.din("r128s", (128, 1))
        self.iotarows_in = self.din("iota_rows", (128, 128))
        self.onesw_in = self.din("ones_w", (128, S), BF16)

        self.out = self.dout("out", (S, D))

        self.tx_hbm = self.dint("tx_hbm", (NMEM * CAP + 8, D), BF16)
        self.om_hbm = self.dint("om_hbm", (NMEM * CAP + 8, D), BF16)
        self.sh_hbm = self.dint("sh_hbm", (S, D), BF16)
        self.w1bf_hbm = self.dint("w1bf_hbm", (FF // 128 * D, 128), BF16)
        self.yn_loc = self.dint("yn_loc", (S, D), BF16)
        self.yn_all = self.dint("yn_all", (NTOK, D), BF16)
        self.pT_loc = self.dint("pT_loc", (NE, S))
        self.pT_all = self.dint("pT_all", (NE * NE, S))
        self.h9_hbm = self.dint("h9_hbm", (128, 256), BF16)
        self.dense_hbm = self.dint("dense_hbm", (NTOK, D), BF16)
        self.dense_rs = self.dint("dense_rs", (S, D), BF16)

        self.t_cc = self.sb("t_cc", (128, NCC))
        self.t_identf = self.sb("t_identf", (128, 128))
        self.t_identb = self.sb("t_identb", (128, 128), BF16)
        self.t_triu = self.sb("t_triu", (128, 128), BF16)
        self.t_upst = self.sb("t_upst", (128, 128))
        self.t_ones = self.sb("t_ones", (128, 128))
        self.t_iota4 = self.sb("t_iota4", (128, 4))
        self.t_iotatg = self.sb("t_iotatg", (128, 128))
        self.t_iotacm = self.sb("t_iotacm", (128, 128))
        self.t_r128s = self.sb("t_r128s", (128, 1))
        self.t_iotarows = self.sb("t_iotarows", (128, 128))
        self.t_onesw = self.sb("t_onesw", (128, S), BF16)
        self.t_normw = self.sb("t_normw", (128, DV))
        self.t_bsbc = self.sb("t_bsbc", (128, NE))
        self.t_b2bc = self.sb("t_b2bc", (128, D))
        self.t_b1c = self.sb("t_b1c", (128, FF // 128))
        self.t_bgk = self.sb("t_bgk", (128, NMEM * H))
        self.t_bgks = self.sb("t_bgks", (128, H))
        self.t_cid = self.sb("t_cid", (128, 1))
        self.t_cid1 = self.sb("t_cid1", (1, 1))
        self.t_w1s = self.sb("t_w1s", (128, 2048), BF16)
        self.rw = self.sb("rw", (128, NT, 2))
        self.disp_idx = self.sb("disp_idx", (128, 2 * NT))
        self.wrap0 = self.sb("wrap0", (128, S // 16), I16)
        self.wrap1 = self.sb("wrap1", (128, S // 16), I16)
        self.wrapC = self.sb("wrapC", (128, KEXP // 16), I16)
        self.wq = self.sb("wq", (128, 8, DK), BF16)
        self.wgate = self.sb("wgate", (128, 8, NMEM), BF16)
        self.wWs = self.sb("wWs", (128, 8, NE), BF16)
        self.zf32 = self.sb("zf32", (128, 1024))
        self.t_zbf = self.sb("t_zbf", (128, 1024), BF16)
        self.ln_a = self.sb("ln_a", (128, 8))
        self.snk = self.sb("snk", (128, 1024))
        self.snk_v = self.sb("snk_v", (128, 1024))
        self.wrA = self.sb("wrA", (128, 128))
        self.sc1 = self.sb("sc1", (128, 1024))
        self.sc3 = self.sb("sc3", (128, 128))
        self.sc4 = self.sb("sc4", (128, 128))
        self.sc5 = self.sb("sc5", (128, 128))
        self.sc6 = self.sb("sc6", (128, 128))
        self.scu8 = self.sb("scu8", (128, 128), U8)
        self.sccol = self.sb("sccol", (128, 16))
        self.sci32 = self.sb("sci32", (1, 8), I32)
        self.tid = self.sb("tid", (128, 16))
        self.cnt8 = self.sb("cnt8", (128, 8))
        self.rwd = self.sb("rwd", (128, 2 * NT))
        self.tval = self.sb("tval", (128, 16))

        self.nc.const_aps.register(F32, 0.0, self.cc1("zero"))
        self.nc.const_aps.register(F32, 1.0, self.cc1("one"))

        self.pA = self.sb_ps("pA", F32)
        self.pO = self.sb_ps("pO", F32)
        self.pD = self.sb_ps("pD", F32)
        self.pT1 = self.sb_ps("pT1", BF16)
        self.pT2 = self.sb_ps("pT2", BF16)
        self.pW = [self.sb_ps("pW0", F32), self.sb_ps("pW1", F32)]
        self.pM = self.sb_ps("pM", F32)

    def sb_ps(self, name, dtype):
        n = 512 if dtype == F32 else 1024
        return self.nc.alloc_psum_tensor(name, [128, n], dtype).ap()

    @staticmethod
    def bcast_part(ap_1xN, P=128):
        import dataclasses
        st, n = ap_1xN.ap[-1]
        return dataclasses.replace(ap_1xN, ap=[[0, P], [st, n]])

    @staticmethod
    def bcast_free(ap_Px1, N):
        import dataclasses
        pst, pn = ap_Px1.ap[0]
        return dataclasses.replace(ap_Px1, ap=[[pst, pn], [0, N]])

    @staticmethod
    def bcast_mid(ap_PxN, M):
        import dataclasses
        return dataclasses.replace(
            ap_PxN, ap=[list(ap_PxN.ap[0]), [0, M], list(ap_PxN.ap[-1])])

    def phase_barrier(self):
        blk = self.block

        @blk.gpsimd
        def _(g):
            for name in list(self._sems):
                if name.startswith("d_"):
                    self.w(g, name)
            self.w(g, "t")
            self.w(g, "v")
            self.w(g, "s")
            self.inc(g.nop(), "g")

        for mk_sec in (blk.vector, blk.scalar, blk.tensor, blk.sync):
            @mk_sec
            def _(e):
                self.w(e, "g")

    def wrap_build(self, n, nat_ap, out_tile, replicate=2):
        m = n // 128
        blk = self.block

        @blk.tensor
        def _(t):
            self.w(t, "v")
            self.w(t, "s")
            self.inc(t.transpose(self.pM[0:m, 0:128], nat_ap, self.t_identf), "t")

        @blk.vector
        def _(v):
            self.w(v, "t")
            self.inc(v.tensor_copy(self.wrA[0:m, 0:128], self.pM[0:m, 0:128]), "v")

        for g8 in range(8):
            @blk.tensor
            def _(t, g8=g8):
                self.w(t, "v")
                self.inc(t.transpose(self.pM[0:16, 0:m], self.wrA[0:m, 16 * g8:16 * g8 + 16],
                                     self.t_identf[0:m, 0:m]), "t")

            @blk.vector
            def _(v, g8=g8):
                self.w(v, "t")
                dst = out_tile[0:16, 0:8 * m].rearrange("r (c g) -> r c g", g=8)[:, :, g8]
                self.inc(v.tensor_copy(dst, self.pM[0:16, 0:m]), "v")

        @blk.gpsimd
        def _(g):
            self.w(g, "v")
            for r in range(1, replicate):
                self.inc(g.dma_start(out=out_tile[16 * r:16 * r + 16, 0:8 * m],
                                     in_=out_tile[0:16, 0:8 * m]), "d_wr", 16)
            self.w(g, "d_wr")

    def ln_tile(self, s, v, src_f32, dst_bf16):
        raise RuntimeError("inline in phases")

    def p1_ln(self, stack):
        blk = self.block
        self.xn = self.sbctx(stack, "xn", (128, NT, D), BF16)
        xb = [self.sbctx(stack, f"p1x{i}", (128, D)) for i in range(2)]

        @blk.sync
        def _(q):
            self.w(q, "d_c")
            for j in range(2):
                self.inc(q.dma_start(out=xb[j][:],
                                     in_=self.x_in.rearrange("(c p) d -> p c d", p=128)[:, j, :]),
                         f"d_x{j % 2}", 16)

        for j in range(NT):
            xt = xb[j % 2]

            @blk.vector
            def _(v, j=j, xt=xt):
                self.w(v, f"d_x{j % 2}")
                self.incw(v, v.tensor_scalar(out=self.snk_v[:, 0:D], in0=xt[:], scalar1=1.0,
                                         scalar2=0.0, op0=OP.mult, op1=OP.add,
                                         accum_out=self.ln_a[:, 0:1]), "v")

            @blk.scalar
            def _(s, j=j, xt=xt):
                self.w(s, f"d_x{j % 2}")
                self.inc(s.activation(out=self.snk[:, 0:D], in_=xt[:], func=AF.Square,
                                      bias=self.cc1("zero"), scale=self.cc1("one"),
                                      accum_out=self.ln_a[:, 1:2]), "s")

            @blk.vector
            def _(v, j=j):
                self.w(v, "s")
                self.w(v, "v")
                la = self.ln_a
                self.incw(v, v.tensor_scalar(out=la[:, 2:3], in0=la[:, 0:1], scalar1=1.0 / D,
                                         scalar2=None, op0=OP.mult), "v")
                self.incw(v, v.tensor_scalar(out=la[:, 3:4], in0=la[:, 1:2], scalar1=1.0 / D,
                                         scalar2=None, op0=OP.mult), "v")
                self.incw(v, v.tensor_tensor(out=la[:, 4:5], in0=la[:, 2:3], in1=la[:, 2:3],
                                         op=OP.mult), "v")
                self.incw(v, v.tensor_tensor(out=la[:, 5:6], in0=la[:, 3:4], in1=la[:, 4:5],
                                         op=OP.subtract), "v")

            @blk.scalar
            def _(s, j=j):
                self.w(s, "v")
                self.inc(s.activation(out=self.ln_a[:, 6:7], in_=self.ln_a[:, 5:6],
                                      func=AF.Ln, bias=self.cc1("eps"),
                                      scale=self.cc1("one")), "s")
                self.w(s, "s")
                self.inc(s.activation(out=self.ln_a[:, 6:7], in_=self.ln_a[:, 6:7],
                                      func=AF.Exp, bias=self.cc1("zero"),
                                      scale=self.cc1("half")), "s")

            @blk.vector
            def _(v, j=j, xt=xt):
                self.w(v, "s")
                self.incw(v, v.reciprocal(self.ln_a[:, 7:8], self.ln_a[:, 6:7]), "v")
                self.w(v, "v")
                self.incw(v, v.tensor_scalar(out=self.xn[:, j, :], in0=xt[:],
                                         scalar1=self.ln_a[:, 2:3], scalar2=self.ln_a[:, 7:8],
                                         op0=OP.subtract, op1=OP.mult), "v")

            if j + 2 < NT:
                @blk.sync
                def _(q, j=j, xt=xt):
                    self.w(q, "v")
                    self.inc(q.dma_start(
                        out=xt[:],
                        in_=self.x_in.rearrange("(c p) d -> p c d", p=128)[:, j + 2, :]),
                        f"d_x{j % 2}", 16)

            @blk.tensor
            def _(t, j=j):
                self.w(t, "v")
                for dc in range(8):
                    self.inc(t.transpose(self.pT1[:, dc * 128:(dc + 1) * 128],
                                         self.xn[:, j, dc * 128:(dc + 1) * 128],
                                         self.t_identb), "t")

            @blk.vector
            def _(v, j=j):
                self.w(v, "t")
                for dc in range(8):
                    self.incw(v, v.tensor_copy(self.xnT[:, dc, j * 128:(j + 1) * 128],
                                           self.pT1[:, dc * 128:(dc + 1) * 128]), "v")

        if self.dbg("xnT"):
            dstT = self.dbg_out("xnT", (128, 8, S))

            @blk.gpsimd
            def _(g):
                self.w(g, "v")
                self.inc(g.dma_start(out=dstT[:, :, :], in_=self.xnT[:, :, :]), "d_dbgg", 16)

        if self.dbg("xn"):
            dst = self.dbg_out("xn", (128, NT, D))

            @blk.gpsimd
            def _(g):
                self.w(g, "v")
                self.inc(g.dma_start(out=dst[:, :, :], in_=self.xn[:, :, :]), "d_dbgg", 16)

    def p2_gate(self, stack):
        blk = self.block
        gtile = self.sbctx(stack, "gtile", (128, 4))
        gdbg = self.sbctx(stack, "gdbg", (128, NT * 4)) if self.dbg("gate") else None
        ftile = self.sbctx(stack, "ftile", (128, NT, 4))
        stile = self.sbctx(stack, "stile", (128, NT, 4))
        maskT = self.sbctx(stack, "maskT", (4, S))
        slotT = self.sbctx(stack, "slotT", (4, S))
        col = self.sccol

        for j in range(NT):
            @blk.tensor
            def _(t, j=j):
                self.w(t, "v")
                self.w(t, "d_w", 32)
                for dc in range(8):
                    self.inc(t.matmul(self.pW[0][:, 0:4], self.xnT[:, dc, j * 128:(j + 1) * 128],
                                      self.wgate[:, dc, :], start=(dc == 0), stop=(dc == 7)), "t")

            @blk.vector
            def _(v, j=j):
                self.w(v, "t")
                G = gtile
                self.incw(v, v.tensor_copy(G[:], self.pW[0][:, 0:4]), "v")
                if gdbg is not None:
                    self.incw(v, v.tensor_copy(gdbg[:, 4 * j:4 * j + 4], self.pW[0][:, 0:4]), "v")
                self.incw(v, v.tensor_reduce(out=col[:, 0:1], in_=G[:], axis=AX.X, op=OP.max), "v")
                self.w(v, "v")
                e1 = self.sc3[:, 0:4]
                self.incw(v, v.tensor_scalar(out=e1, in0=G[:], scalar1=col[:, 0:1],
                                         scalar2=None, op0=OP.is_equal), "v")
                c1 = self.sc4[:, 0:4]
                self.incw(v, v.tensor_tensor_scan(out=c1, data0=self.t_onesw[:, 0:4], data1=e1,
                                              initial=0.0, op0=OP.mult, op1=OP.add), "v")
                t1 = self.sc5[:, 0:4]
                self.incw(v, v.tensor_scalar(out=t1, in0=c1, scalar1=1.0, scalar2=None,
                                         op0=OP.is_equal), "v")
                f = ftile[:, j, :]
                self.incw(v, v.tensor_tensor(out=f, in0=e1, in1=t1, op=OP.mult), "v")
                Gm = self.sc6[:, 0:4]
                self.incw(v, v.scalar_tensor_tensor(out=Gm, in0=f, scalar=-1e30, in1=G[:],
                                                op0=OP.mult, op1=OP.add), "v")
                self.incw(v, v.tensor_reduce(out=col[:, 1:2], in_=Gm, axis=AX.X, op=OP.max), "v")
                self.w(v, "v")
                self.incw(v, v.tensor_scalar(out=e1, in0=Gm, scalar1=col[:, 1:2], scalar2=None,
                                         op0=OP.is_equal), "v")
                self.incw(v, v.tensor_tensor_scan(out=c1, data0=self.t_onesw[:, 0:4], data1=e1,
                                              initial=0.0, op0=OP.mult, op1=OP.add), "v")
                self.incw(v, v.tensor_scalar(out=t1, in0=c1, scalar1=1.0, scalar2=None,
                                         op0=OP.is_equal), "v")
                sflag = stile[:, j, :]
                self.incw(v, v.tensor_tensor(out=sflag, in0=e1, in1=t1, op=OP.mult), "v")
                self.incw(v, v.tensor_tensor(out=col[:, 2:3], in0=col[:, 0:1], in1=col[:, 1:2],
                                         op=OP.subtract), "v")
                self.incw(v, v.tensor_tensor(out=self.sc3[:, 0:4], in0=f, in1=sflag, op=OP.add), "v")

            @blk.vector
            def _(v, j=j):
                self.incw(v, v.tensor_copy(self.rwd[:, j:j + 1], col[:, 2:3]), "v")

            @blk.tensor
            def _(t, j=j):
                self.w(t, "v")
                self.inc(t.transpose(self.pM[0:4, 0:128], self.sc3[:, 0:4], self.t_identf), "t")

            @blk.vector
            def _(v, j=j):
                self.w(v, "t")
                self.incw(v, v.tensor_copy(maskT[:, j * 128:(j + 1) * 128], self.pM[0:4, 0:128]), "v")

        @blk.scalar
        def _(s):
            self.w(s, "v")
            self.inc(s.activation(out=self.rwd[:, NT:2 * NT], in_=self.rwd[:, 0:NT],
                                  func=AF.Exp, bias=self.cc1("zero"),
                                  scale=self.cc1("negone")), "s")

        @blk.vector
        def _(v):
            self.w(v, "s")
            self.incw(v, v.tensor_scalar(out=self.rwd[:, NT:2 * NT], in0=self.rwd[:, NT:2 * NT],
                                     scalar1=1.0, scalar2=None, op0=OP.add), "v")
            self.incw(v, v.reciprocal(self.rwd[:, NT:2 * NT], self.rwd[:, NT:2 * NT]), "v")
            for j in range(NT):
                self.incw(v, v.tensor_copy(self.rw[:, j, 0:1], self.rwd[:, NT + j:NT + j + 1]), "v")
                self.incw(v, v.tensor_scalar(out=self.rw[:, j, 1:2], in0=self.rw[:, j, 0:1],
                                         scalar1=-1.0, scalar2=1.0, op0=OP.mult, op1=OP.add), "v")

        @blk.vector
        def _(v):
            self.incw(v, v.tensor_tensor_scan(out=slotT[:], data0=self.t_onesw[0:4, :],
                                          data1=maskT[:], initial=0.0,
                                          op0=OP.mult, op1=OP.add), "v")
            self.incw(v, v.tensor_scalar(out=col[0:4, 8:9], in0=slotT[:, S - 1:S], scalar1=-1.0,
                                     scalar2=float(CAP - 1), op0=OP.mult, op1=OP.add), "v")
            self.w(v, "v")
            self.incw(v, v.tensor_scalar(out=slotT[:], in0=slotT[:], scalar1=col[0:4, 8:9],
                                     scalar2=None, op0=OP.add), "v")

        for j in range(NT):
            @blk.tensor
            def _(t, j=j):
                self.w(t, "v")
                self.inc(t.transpose(self.pM[0:128, 0:4], slotT[:, j * 128:(j + 1) * 128],
                                     self.t_identf[0:4, 0:4]), "t")

            @blk.vector
            def _(v, j=j):
                self.w(v, "t")
                st4 = self.sc4[:, 0:4]
                self.incw(v, v.tensor_copy(st4, self.pM[0:128, 0:4]), "v")
                for c, ftl in ((0, ftile), (1, stile)):
                    fj = ftl[:, j, :]
                    self.incw(v, v.tensor_tensor(out=self.sc5[:, 0:4], in0=fj, in1=self.t_iota4,
                                             op=OP.mult), "v")
                    self.incw(v, v.tensor_reduce(out=col[:, 3:4], in_=self.sc5[:, 0:4],
                                             axis=AX.X, op=OP.add), "v")
                    self.incw(v, v.tensor_tensor(out=self.sc5[:, 0:4], in0=fj, in1=st4,
                                             op=OP.mult), "v")
                    self.incw(v, v.tensor_reduce(out=col[:, 4:5], in_=self.sc5[:, 0:4],
                                             axis=AX.X, op=OP.add), "v")
                    self.incw(v, v.tensor_scalar(out=col[:, 5:6], in0=col[:, 4:5], scalar1=0.0,
                                             scalar2=None, op0=OP.is_ge), "v")
                    self.incw(v, v.scalar_tensor_tensor(out=col[:, 6:7], in0=col[:, 3:4],
                                                    scalar=float(CAP), in1=col[:, 4:5],
                                                    op0=OP.mult, op1=OP.add), "v")
                    self.incw(v, v.tensor_scalar(out=col[:, 6:7], in0=col[:, 6:7], scalar1=-4096.0,
                                             scalar2=None, op0=OP.add), "v")
                    self.incw(v, v.tensor_tensor(out=col[:, 6:7], in0=col[:, 6:7], in1=col[:, 5:6],
                                             op=OP.mult), "v")
                    self.incw(v, v.tensor_scalar(out=self.disp_idx[:, c * NT + j:c * NT + j + 1],
                                             in0=col[:, 6:7], scalar1=4096.0,
                                             scalar2=None, op0=OP.add), "v")

        if self.dbg("fs"):
            dstf = self.dbg_out("fs", (128, NT, 4))
            dsts = self.dbg_out("ss", (128, NT, 4))
            dstm = self.dbg_out("maskT", (4, S))
            dstsl = self.dbg_out("slotT", (4, S))

            @blk.sync
            def _(q):
                self.w(q, "v")
                self.inc(q.dma_start(out=dstf[:, :, :], in_=ftile[:, :, :]), "d_dbg", 16)
                self.inc(q.dma_start(out=dsts[:, :, :], in_=stile[:, :, :]), "d_dbg", 16)
                self.inc(q.dma_start(out=dstm[:, :], in_=maskT[:, :]), "d_dbg", 16)
                self.inc(q.dma_start(out=dstsl[:, :], in_=slotT[:, :]), "d_dbg", 16)

        if gdbg is not None:
            dstg = self.dbg_out("gate", (128, NT * 4))

            @blk.sync
            def _(q):
                self.w(q, "v")
                self.inc(q.dma_start(out=dstg[:, :], in_=gdbg[:, :]), "d_dbg", 16)

        self.wrap_build(S, self.disp_idx[:, 0:NT], self.wrap0)
        self.wrap_build(S, self.disp_idx[:, NT:2 * NT], self.wrap1)

        if self.dbg("disp"):
            dst = self.dbg_out("disp", (128, 2 * NT))

            @blk.sync
            def _(q):
                self.w(q, "v")
                self.inc(q.dma_start(out=dst[:, :], in_=self.disp_idx[:, :]), "d_dbg", 16)
        if self.dbg("rw"):
            dst = self.dbg_out("rw", (128, NT, 2))

            @blk.sync
            def _(q):
                self.w(q, "v")
                self.inc(q.dma_start(out=dst[:, :, :], in_=self.rw[:, :, :]), "d_dbg", 16)

        @blk.gpsimd
        def _(g):
            self.w(g, "v")
            self.w(g, "d_z")
            for wt in (self.wrap0, self.wrap1):
                self.inc(g.dma_scatter_add(out_ap=self.tx_hbm[:, :], in_ap=self.xn[:, :, :],
                                           idxs_ap=wt[:, :], num_idxs=S, num_idxs_reg=S,
                                           elem_size=D, single_packet=False), "d_g", 16)
                self.w(g, "d_g")

    def gla_group(self, stack, gi, tiles):
        blk = self.block
        routed = gi < 4
        cap = CAP if routed else S
        nmh = cap // 1024
        (txT, vsb, u2m, qfh, kfh, lamt, emt, lpt, tmpT, cs_c, eCf, Ssb,
         k2c4, amt4, k2t4, wkb, wvb, wg1b, wg2b) = tiles

        @blk.gpsimd
        def _(g):
            self.w(g, "t")
            self.w(g, "v")
            if routed:
                self.inc(g.dma_start(out=wkb[:],
                                     in_=self.w_k_in[gi].rearrange("(c p) o -> p c o", p=128)),
                         "d_wm", 16)
                self.inc(g.dma_start(out=wvb[:],
                                     in_=self.w_v_in[gi].rearrange("(c p) o -> p c o", p=128)),
                         "d_wm", 16)
                self.inc(g.dma_start(out=wg1b[:],
                                     in_=self.w_gk1_in[gi].rearrange("(c p) o -> p c o", p=128)),
                         "d_wm", 16)
                self.inc(g.dma_start(out=wg2b[0:16, :], in_=self.w_gk2_in[gi][:, :]),
                         "d_wm", 16)
            else:
                self.inc(g.dma_start(out=wkb[:],
                                     in_=self.w_ks_in.rearrange("(c p) o -> p c o", p=128)),
                         "d_wm", 16)
                self.inc(g.dma_start(out=wvb[:],
                                     in_=self.w_vs_in.rearrange("(c p) o -> p c o", p=128)),
                         "d_wm", 16)
                self.inc(g.dma_start(out=wg1b[:],
                                     in_=self.w_gk1s_in.rearrange("(c p) o -> p c o", p=128)),
                         "d_wm", 16)
                self.inc(g.dma_start(out=wg2b[0:16, :], in_=self.w_gk2s_in[:, :]),
                         "d_wm", 16)

        wm_tok = self.cnt["d_wm"]
        bcol = (lambda h: self.t_bgk[:, gi * H + h:gi * H + h + 1]) if routed else                (lambda h: self.t_bgks[:, h:h + 1])

        if routed:
            @blk.sync
            def _(q):
                self.w(q, "d_g")
                self.w(q, "v")
                self.w(q, "d_om")
                self.w(q, "d_sh")
                self.inc(q.dma_start(
                    out=u2m[:, :, :],
                    in_=self.tx_hbm[gi * CAP:(gi + 1) * CAP, :].rearrange(
                        "(c p) d -> p c d", p=128)), "d_tx", 16)

            for tb in range(8):
                @blk.tensor
                def _(t, tb=tb):
                    self.w(t, "d_tx")
                    self.w(t, "v")
                    for dc in range(8):
                        self.inc(t.transpose(self.pT1[:, dc * 128:(dc + 1) * 128],
                                             u2m[:, tb, dc * 128:(dc + 1) * 128],
                                             self.t_identb), "t")

                @blk.vector
                def _(v, tb=tb):
                    self.w(v, "t")
                    for dc in range(8):
                        self.inc(v.tensor_copy(txT[:, dc, tb * 128:(tb + 1) * 128],
                                               self.pT1[:, dc * 128:(dc + 1) * 128]), "v")

        @blk.vector
        def _(v):
            for h in range(H):
                self.inc(v.memset(Ssb[h][:], 0.0), "v")

        for mh in range(nmh):
            def srcT(dc, sl):
                if routed:
                    return txT[:, dc, sl]
                base = mh * 1024
                return self.xnT[:, dc, base + sl.start: base + sl.stop]

            for tb in range(8):
                @blk.tensor
                def _(t, tb=tb, mh=mh):
                    self.w(t, "v")
                    self.w(t, "s")
                    self.w(t, "d_wm", wm_tok)
                    for dvh in range(2):
                        pb = self.pW[dvh]
                        for dc in range(8):
                            self.inc(t.matmul(pb[:, 0:512],
                                              srcT(dc, slice(tb * 128, (tb + 1) * 128)),
                                              wvb[:, dc, dvh * 512:(dvh + 1) * 512],
                                              start=(dc == 0), stop=(dc == 7)), "t")

                @blk.vector
                def _(v, tb=tb):
                    self.w(v, "t")
                    self.inc(v.tensor_copy(vsb[:, tb, 0:512], self.pW[0][:, 0:512]), "v")

                @blk.scalar
                def _(s, tb=tb):
                    self.w(s, "t")
                    self.inc(s.activation(out=vsb[:, tb, 512:1024], in_=self.pW[1][:, 0:512],
                                          func=AF.Copy, bias=0.0, scale=1.0), "s")

            for h in range(H):
                for hf in range(2):
                    @blk.tensor
                    def _(t, h=h, hf=hf, mh=mh):
                        self.w(t, "v")
                        self.w(t, "s")
                        for dc in range(8):
                            self.inc(t.matmul(self.pW[0][0:16, 0:512],
                                              wg1b[:, dc, :],
                                              srcT(dc, slice(hf * 512, (hf + 1) * 512)),
                                              start=(dc == 0), stop=(dc == 7)), "t")

                    @blk.scalar
                    def _(s, h=h, hf=hf):
                        self.w(s, "t")
                        self.inc(s.activation(out=tmpT[0:16, hf * 512:(hf + 1) * 512],
                                              in_=self.pW[0][0:16, 0:512], func=AF.Copy, bias=0.0, scale=1.0), "s")

                    @blk.tensor
                    def _(t, h=h, hf=hf):
                        self.w(t, "s")
                        self.inc(t.matmul(self.pW[1][:, 0:512],
                                          wg2b[0:16, h * 128:(h + 1) * 128],
                                          tmpT[0:16, hf * 512:(hf + 1) * 512],
                                          start=True, stop=True), "t")

                    @blk.scalar
                    def _(s, h=h, hf=hf):
                        self.w(s, "t")
                        self.inc(s.activation(out=self.sc1[:, 0:512], in_=self.pW[1][:, 0:512],
                                              func=AF.Exp, bias=bcol(h),
                                              scale=self.cc1("negone")), "s")
                        self.w(s, "s")
                        self.inc(s.activation(out=lpt[:, hf * 512:(hf + 1) * 512],
                                              in_=self.sc1[:, 0:512], func=AF.Ln,
                                              bias=self.cc1("one"), scale=self.cc1("one")), "s")

                @blk.vector
                def _(v, h=h):
                    self.w(v, "s")
                    for j in range(8):
                        self.inc(v.tensor_tensor_scan(
                            out=cs_c[:, j * 128:(j + 1) * 128],
                            data0=self.t_onesw[:, 0:128],
                            data1=lpt[:, j * 128:(j + 1) * 128],
                            initial=0.0, op0=OP.mult, op1=OP.add), "v")

                @blk.scalar
                def _(s, h=h):
                    self.w(s, "v")
                    self.inc(s.activation(out=lamt[:], in_=cs_c[:, 0:1024], func=AF.Exp,
                                          bias=self.cc1("lnqs"), scale=self.cc1("neg16")), "s")
                    self.inc(s.activation(out=emt[:], in_=cs_c[:, 0:1024], func=AF.Exp,
                                          bias=self.cc1("zero"), scale=self.cc1("inv16")), "s")
                    for j in range(8):
                        self.inc(s.activation(out=eCf[:, h * 8 + j:h * 8 + j + 1],
                                              in_=cs_c[:, j * 128 + 127:j * 128 + 128],
                                              func=AF.Exp, bias=self.cc1("zero"),
                                              scale=self.cc1("neg16")), "s")

                for hf in range(2):
                    @blk.tensor
                    def _(t, h=h, hf=hf, mh=mh):
                        self.w(t, "s")
                        self.w(t, "v")
                        for dc in range(8):
                            self.inc(t.matmul(self.pW[0][:, 0:512],
                                              self.wq[:, dc, h * 128:(h + 1) * 128],
                                              srcT(dc, slice(hf * 512, (hf + 1) * 512)),
                                              start=(dc == 0), stop=(dc == 7)), "t")
                        for dc in range(8):
                            self.inc(t.matmul(self.pW[1][:, 0:512],
                                              wkb[:, dc, h * 128:(h + 1) * 128],
                                              srcT(dc, slice(hf * 512, (hf + 1) * 512)),
                                              start=(dc == 0), stop=(dc == 7)), "t")

                    @blk.vector
                    def _(v, h=h, hf=hf):
                        self.w(v, "t")
                        sl = slice(hf * 512, (hf + 1) * 512)
                        self.inc(v.tensor_tensor(out=qfh[h][:, sl], in0=self.pW[0][:, 0:512],
                                                 in1=lamt[:, sl], op=OP.mult), "v")
                        self.inc(v.tensor_tensor(out=kfh[h][:, sl], in0=self.pW[1][:, 0:512],
                                                 in1=emt[:, sl], op=OP.mult), "v")

            for jl in range(8):
                jb = slice(jl * 128, (jl + 1) * 128)
                jg = mh * 8 + jl

                @blk.tensor
                def _(t, jl=jl, jb=jb):
                    self.w(t, "v")
                    for h in range(H):
                        self.inc(t.matmul(self.pA[:, h * 128:(h + 1) * 128],
                                          kfh[h][:, jb], qfh[h][:, jb],
                                          start=True, stop=True), "t")

                @blk.vector
                def _(v, jl=jl, jb=jb):
                    self.w(v, "t")
                    for h in range(H):
                        self.inc(v.tensor_tensor(out=amt4[:, h * 128:(h + 1) * 128],
                                                 in0=self.pA[:, h * 128:(h + 1) * 128],
                                                 in1=self.t_triu, op=OP.mult), "v")
                        self.inc(v.tensor_scalar(out=k2c4[:, h * 128:(h + 1) * 128],
                                                 in0=kfh[h][:, jb],
                                                 scalar1=eCf[:, h * 8 + jl:h * 8 + jl + 1],
                                                 scalar2=None, op0=OP.mult), "v")

                @blk.tensor
                def _(t, jl=jl, jb=jb, jg=jg, mh=mh):
                    self.w(t, "v")
                    for h in range(H):
                        self.inc(t.transpose(self.pT1[:, h * 128:(h + 1) * 128],
                                             k2c4[:, h * 128:(h + 1) * 128], self.t_identb), "t")
                    for h in range(H):
                        pOb = self.pO if h < 2 else self.pD
                        po = pOb[:, (h % 2) * 256:(h % 2) * 256 + 256]
                        first = (jg == 0)
                        self.inc(t.matmul(po, amt4[:, h * 128:(h + 1) * 128],
                                          vsb[:, jl, h * 256:(h + 1) * 256],
                                          start=True, stop=first), "t")
                        if not first:
                            self.inc(t.matmul(po, qfh[h][:, jb], Ssb[h][:],
                                              start=False, stop=True), "t")

                @blk.vector
                def _(v, jl=jl):
                    self.w(v, "t")
                    for h in range(H):
                        self.inc(v.tensor_copy(k2t4[:, h * 128:(h + 1) * 128],
                                               self.pT1[:, h * 128:(h + 1) * 128]), "v")

                @blk.tensor
                def _(t, jl=jl):
                    self.w(t, "v")
                    for h in range(H):
                        pDb = self.pW[0] if h < 2 else self.pW[1]
                        self.inc(t.matmul(pDb[:, (h % 2) * 256:(h % 2) * 256 + 256],
                                          k2t4[:, h * 128:(h + 1) * 128],
                                          vsb[:, jl, h * 256:(h + 1) * 256],
                                          start=True, stop=True), "t")

                @blk.vector
                def _(v, jl=jl, jg=jg):
                    self.w(v, "t")
                    if jl == 0:
                        self.w(v, "d_om")
                        self.w(v, "d_sh")
                    for h in range(H):
                        pOb = self.pO if h < 2 else self.pD
                        po = pOb[:, (h % 2) * 256:(h % 2) * 256 + 256]
                        self.inc(v.tensor_copy(u2m[:, jl, h * 256:(h + 1) * 256], po), "v")
                    for h in range(H):
                        pDb = self.pW[0] if h < 2 else self.pW[1]
                        self.inc(v.scalar_tensor_tensor(
                            out=Ssb[h][:], in0=Ssb[h][:],
                            scalar=eCf[:, h * 8 + jl:h * 8 + jl + 1],
                            in1=pDb[:, (h % 2) * 256:(h % 2) * 256 + 256],
                            op0=OP.mult, op1=OP.add), "v")

            @blk.sync
            def _(q, mh=mh):
                self.w(q, "v")
                if routed:
                    dst = self.om_hbm[gi * CAP:(gi + 1) * CAP, :]
                    self.inc(q.dma_start(out=dst.rearrange("(c p) d -> p c d", p=128),
                                         in_=u2m[:, :, :]), "d_om", 16)
                else:
                    dst = self.sh_hbm[mh * 1024:(mh + 1) * 1024, :]
                    self.inc(q.dma_start(out=dst.rearrange("(c p) d -> p c d", p=128),
                                         in_=u2m[:, :, :]), "d_sh", 16)

    def p_gla(self, stack):
        blk = self.block
        txT = self.sbctx(stack, "txT", (128, 8, CAP), BF16)
        vsb = self.sbctx(stack, "vsb", (128, 8, DV), BF16)
        u2m = self.sbctx(stack, "u2m", (128, 8, D), BF16)
        qfh = [self.sbctx(stack, f"qfh{h}", (128, 1024), BF16) for h in range(H)]
        kfh = [self.sbctx(stack, f"kfh{h}", (128, 1024), BF16) for h in range(H)]
        emt = self.sbctx(stack, "emt", (128, 1024), BF16)
        lpt = self.sbctx(stack, "lpt", (128, 1024), BF16)
        lamt = lpt
        tmpT = self.sbctx(stack, "tmpT", (16, 1024), BF16)
        cs_c = self.sbctx(stack, "cs_c", (128, 1024))
        eCf = self.sbctx(stack, "eCf", (128, H * 8))
        Ssb = [self.sbctx(stack, f"Ssb{h}", (128, HDV), BF16) for h in range(H)]
        k2c4 = self.sbctx(stack, "k2c4", (128, 512), BF16)
        amt4 = self.sbctx(stack, "amt4", (128, 512), BF16)
        k2t4 = self.sbctx(stack, "k2t4", (128, 512), BF16)
        wkb = self.sbctx(stack, "wkb", (128, 8, DK), BF16)
        wvb = self.sbctx(stack, "wvb", (128, 8, DV), BF16)
        wg1b = self.sbctx(stack, "wg1b", (128, 8, LOW), BF16)
        wg2b = self.sbctx(stack, "wg2b", (16, DK), BF16)
        tiles = (txT, vsb, u2m, qfh, kfh, lamt, emt, lpt, tmpT, cs_c, eCf, Ssb,
                 k2c4, amt4, k2t4, wkb, wvb, wg1b, wg2b)

        @blk.vector
        def _(v):
            self.w(v, "d_g")

        @blk.scalar
        def _(s):
            self.w(s, "d_g")

        for gi in self.gla_order:
            self.gla_group(stack, gi, tiles)

        if self.dbg("om"):
            pass

    def p6_reco(self, stack):
        blk = self.block
        self.sc2 = self.sbctx(stack, "sc2", (128, 1024))
        gat0 = self.sbctx(stack, "gat0", (128, 8, DV), BF16)
        gat1 = self.sbctx(stack, "gat1", (128, 8, DV), BF16)
        shb = self.sbctx(stack, "shb", (128, 8, DV), BF16)
        wgt = self.sbctx(stack, "wgt", (128, 8, DV), BF16)
        sg = self.sbctx(stack, "sg", (128, DV))

        @blk.gpsimd
        def _(g):
            self.w(g, "d_wm")
            self.w(g, "t")
            self.w(g, "v")
            self.w(g, "s")
            self.inc(g.dma_start(out=wgt[:],
                                 in_=self.w_g_in.rearrange("(c p) o -> p c o", p=128)),
                     "d_wg", 16)

        wg_tok = self.cnt["d_wg"]

        for hf in range(2):
            @blk.gpsimd
            def _(g, hf=hf):
                self.w(g, "d_om")
                self.w(g, "d_z")
                self.w(g, "s")
                self.w(g, "v")
                for c, (wt, gat) in enumerate(((self.wrap0, gat0), (self.wrap1, gat1))):
                    self.inc(g.dma_gather(out_ap=gat[:, :, :], in_ap=self.om_hbm[:, :],
                                          idxs_ap=wt[:, 64 * hf:64 * hf + 64],
                                          num_idxs=1024, num_idxs_reg=1024,
                                          elem_size=D, single_packet=False), "d_g", 16)
                self.w(g, "d_g")

            @blk.sync
            def _(q, hf=hf):
                self.w(q, "d_sh")
                self.w(q, "v")
                self.inc(q.dma_start(
                    out=shb[:, :, :],
                    in_=self.sh_hbm[hf * 1024:(hf + 1) * 1024, :].rearrange(
                        "(c p) d -> p c d", p=128)), "d_tx", 16)

            for jg in range(8):
                j = hf * 8 + jg

                @blk.scalar
                def _(s, j=j, jg=jg):
                    self.w(s, "d_g")
                    self.w(s, "v")
                    self.inc(s.activation(out=self.sc1[:, 0:DV], in_=gat0[:, jg, :],
                                          func=AF.Copy, bias=0.0,
                                          scale=self.rw[:, j, 0:1]), "s")

                @blk.vector
                def _(v, j=j, jg=jg):
                    self.w(v, "s")
                    self.w(v, "d_tx")
                    self.incw(v, v.scalar_tensor_tensor(out=self.sc2[:, 0:DV], in0=gat1[:, jg, :],
                                                    scalar=self.rw[:, j, 1:2],
                                                    in1=self.sc1[:, 0:DV],
                                                    op0=OP.mult, op1=OP.add), "v")
                    self.incw(v, v.tensor_tensor(out=self.sc1[:, 0:DV], in0=self.sc2[:, 0:DV],
                                             in1=shb[:, jg, :], op=OP.add), "v")

                @blk.scalar
                def _(s, j=j):
                    self.w(s, "v")
                    for h in range(H):
                        self.inc(s.activation(out=self.snk[:, h * HDV:(h + 1) * HDV],
                                              in_=self.sc1[:, h * HDV:(h + 1) * HDV],
                                              func=AF.Square, bias=self.cc1("zero"),
                                              scale=self.cc1("one"),
                                              accum_out=self.sccol[:, 8 + h:9 + h]), "s")

                @blk.vector
                def _(v, j=j):
                    self.w(v, "s")
                    self.incw(v, v.tensor_scalar(out=self.sccol[:, 12:16],
                                             in0=self.sccol[:, 8:12], scalar1=1.0 / HDV,
                                             scalar2=EPS, op0=OP.mult, op1=OP.add), "v")

                @blk.scalar
                def _(s, j=j):
                    self.w(s, "v")
                    self.inc(s.activation(out=self.sccol[:, 8:12], in_=self.sccol[:, 12:16],
                                          func=AF.Ln, bias=self.cc1("zero"),
                                          scale=self.cc1("one")), "s")
                    self.w(s, "s")
                    self.inc(s.activation(out=self.sccol[:, 8:12], in_=self.sccol[:, 8:12],
                                          func=AF.Exp, bias=self.cc1("zero"),
                                          scale=self.cc1("half")), "s")

                @blk.vector
                def _(v, j=j):
                    self.w(v, "s")
                    self.incw(v, v.reciprocal(self.sccol[:, 12:16], self.sccol[:, 8:12]), "v")
                    self.w(v, "v")
                    for h in range(H):
                        sl = slice(h * HDV, (h + 1) * HDV)
                        self.incw(v, v.scalar_tensor_tensor(out=self.sc2[:, sl],
                                                        in0=self.sc1[:, sl],
                                                        scalar=self.sccol[:, 12 + h:13 + h],
                                                        in1=self.t_normw[:, sl],
                                                        op0=OP.mult, op1=OP.mult), "v")

                @blk.tensor
                def _(t, j=j):
                    self.w(t, "v")
                    self.w(t, "d_wg", wg_tok)
                    for dvh in range(2):
                        for dc in range(8):
                            self.inc(t.matmul(self.pW[dvh][:, 0:512],
                                              self.xnT[:, dc, j * 128:(j + 1) * 128],
                                              wgt[:, dc, dvh * 512:(dvh + 1) * 512],
                                              start=(dc == 0), stop=(dc == 7)), "t")

                @blk.scalar
                def _(s, j=j):
                    self.w(s, "t")
                    for dvh in range(2):
                        self.inc(s.activation(out=sg[:, dvh * 512:(dvh + 1) * 512],
                                              in_=self.pW[dvh][:, 0:512], func=AF.Exp,
                                              bias=self.cc1("zero"),
                                              scale=self.cc1("negone")), "s")
                        self.inc(s.activation(out=self.snk[:, dvh * 512:(dvh + 1) * 512],
                                              in_=self.pW[dvh][:, 0:512], func=AF.Copy,
                                              bias=0.0, scale=1.0), "s")

                @blk.vector
                def _(v, j=j):
                    self.w(v, "s")
                    self.incw(v, v.tensor_scalar(out=sg[:], in0=sg[:], scalar1=1.0,
                                             scalar2=None, op0=OP.add), "v")
                    self.incw(v, v.reciprocal(sg[:], sg[:]), "v")
                    self.incw(v, v.tensor_tensor(out=sg[:], in0=sg[:], in1=self.snk[:, 0:DV],
                                             op=OP.mult), "v")

                @blk.vector
                def _(v, j=j):
                    self.w(v, "s")
                    self.incw(v, v.tensor_tensor(out=self.og[:, j, :], in0=self.sc2[:, 0:DV],
                                             in1=sg[:], op=OP.mult), "v")

    def p7_out(self, stack):
        blk = self.block
        wot = self.sbctx(stack, "wot", (128, 8, D), BF16)
        ynT = self.sbctx(stack, "ynT", (128, 8, S), BF16)
        ogT = self.sbctx(stack, "ogT", (128, 8, 128), BF16)
        xb = [self.sbctx(stack, f"p7x{i}", (128, D)) for i in range(2)]
        yb = [self.sbctx(stack, f"p7y{i}", (128, D)) for i in range(2)]
        ynb = [self.sbctx(stack, f"p7n{i}", (128, D), BF16) for i in range(2)]
        pTl = self.sbctx(stack, "pTl", (NE, S))
        zr = self.sbctx(stack, "zr", (128, NE))
        er = self.sbctx(stack, "er", (128, NE))

        @blk.gpsimd
        def _(g):
            self.w(g, "d_wg")
            self.w(g, "t")
            self.w(g, "v")
            self.w(g, "s")
            self.inc(g.dma_start(out=wot[:],
                                 in_=self.w_o_in.rearrange("(c p) o -> p c o", p=128)),
                     "d_wo", 16)
            self.inc(g.dma_start(out=self.wWs[:],
                                 in_=self.Ws_in.rearrange("(c p) o -> p c o", p=128)),
                     "d_wo", 16)

        wo_tok = self.cnt["d_wo"]

        @blk.sync
        def _(q):
            self.w(q, "t")
            self.w(q, "v")
            self.w(q, "s")
            for j in range(2):
                self.inc(q.dma_start(out=xb[j][:],
                                     in_=self.x_in.rearrange("(c p) d -> p c d", p=128)[:, j, :]),
                         f"d_x{j % 2}", 16)

        for j in range(NT):
            xt = xb[j % 2]
            yt = yb[j % 2]
            ynt = ynb[j % 2]

            @blk.tensor
            def _(t, j=j):
                self.w(t, "v")
                self.w(t, "d_wo", wo_tok)
                for dc in range(8):
                    self.inc(t.transpose(self.pT1[:, dc * 128:(dc + 1) * 128],
                                         self.og[:, j, dc * 128:(dc + 1) * 128],
                                         self.t_identb), "t")

            @blk.vector
            def _(v, j=j):
                self.w(v, "t")
                for dc in range(8):
                    self.incw(v, v.tensor_copy(ogT[:, dc, :],
                                           self.pT1[:, dc * 128:(dc + 1) * 128]), "v")

            @blk.tensor
            def _(t, j=j):
                self.w(t, "v")
                for half in range(2):
                    for dc in range(8):
                        self.inc(t.matmul(self.pW[half][:, 0:512],
                                          ogT[:, dc, :],
                                          wot[:, dc, half * 512:(half + 1) * 512],
                                          start=(dc == 0), stop=(dc == 7)), "t")

            @blk.vector
            def _(v, j=j, xt=xt, yt=yt):
                self.w(v, "t")
                self.w(v, f"d_x{j % 2}")
                self.w(v, f"d_y{j % 2}")
                for half in range(2):
                    self.incw(v, v.tensor_tensor(out=yt[:, half * 512:(half + 1) * 512],
                                             in0=self.pW[half][:, 0:512],
                                             in1=xt[:, half * 512:(half + 1) * 512],
                                             op=OP.add), "v")
                self.incw(v, v.tensor_scalar(out=self.snk_v[:, 0:D], in0=yt[:], scalar1=1.0,
                                         scalar2=0.0, op0=OP.mult, op1=OP.add,
                                         accum_out=self.ln_a[:, 0:1]), "v")

            @blk.sync
            def _(q, j=j, yt=yt, xt=xt):
                self.w(q, "v")
                self.inc(q.dma_start(
                    out=self.out.rearrange("(c p) d -> p c d", p=128)[:, j, :],
                    in_=yt[:]), f"d_y{j % 2}", 16)
                if j + 2 < NT:
                    self.inc(q.dma_start(
                        out=xt[:],
                        in_=self.x_in.rearrange("(c p) d -> p c d", p=128)[:, j + 2, :]),
                        f"d_x{j % 2}", 16)

            @blk.scalar
            def _(s, j=j, yt=yt):
                self.w(s, "v")
                self.inc(s.activation(out=self.snk[:, 0:D], in_=yt[:], func=AF.Square,
                                      bias=self.cc1("zero"), scale=self.cc1("one"),
                                      accum_out=self.ln_a[:, 1:2]), "s")

            @blk.vector
            def _(v, j=j):
                self.w(v, "s")
                self.w(v, "v")
                la = self.ln_a
                self.incw(v, v.tensor_scalar(out=la[:, 2:3], in0=la[:, 0:1], scalar1=1.0 / D,
                                         scalar2=None, op0=OP.mult), "v")
                self.incw(v, v.tensor_scalar(out=la[:, 3:4], in0=la[:, 1:2], scalar1=1.0 / D,
                                         scalar2=None, op0=OP.mult), "v")
                self.incw(v, v.tensor_tensor(out=la[:, 4:5], in0=la[:, 2:3], in1=la[:, 2:3],
                                         op=OP.mult), "v")
                self.incw(v, v.tensor_tensor(out=la[:, 5:6], in0=la[:, 3:4], in1=la[:, 4:5],
                                         op=OP.subtract), "v")

            @blk.scalar
            def _(s, j=j):
                self.w(s, "v")
                self.inc(s.activation(out=self.ln_a[:, 6:7], in_=self.ln_a[:, 5:6],
                                      func=AF.Ln, bias=self.cc1("eps"),
                                      scale=self.cc1("one")), "s")
                self.w(s, "s")
                self.inc(s.activation(out=self.ln_a[:, 6:7], in_=self.ln_a[:, 6:7],
                                      func=AF.Exp, bias=self.cc1("zero"),
                                      scale=self.cc1("half")), "s")

            @blk.vector
            def _(v, j=j, yt=yt, ynt=ynt):
                self.w(v, "s")
                self.w(v, f"d_yn{j % 2}")
                self.incw(v, v.reciprocal(self.ln_a[:, 7:8], self.ln_a[:, 6:7]), "v")
                self.w(v, "v")
                self.incw(v, v.tensor_scalar(out=ynt[:], in0=yt[:],
                                         scalar1=self.ln_a[:, 2:3], scalar2=self.ln_a[:, 7:8],
                                         op0=OP.subtract, op1=OP.mult), "v")

            @blk.sync
            def _(q, j=j, ynt=ynt):
                self.w(q, "v")
                self.inc(q.dma_start(
                    out=self.yn_loc.rearrange("(c p) d -> p c d", p=128)[:, j, :],
                    in_=ynt[:]), f"d_yn{j % 2}", 16)

            @blk.tensor
            def _(t, j=j, ynt=ynt):
                self.w(t, "v")
                for dc in range(8):
                    self.inc(t.transpose(self.pT2[:, dc * 128:(dc + 1) * 128],
                                         ynt[:, dc * 128:(dc + 1) * 128],
                                         self.t_identb), "t")

            @blk.vector
            def _(v, j=j):
                self.w(v, "t")
                for dc in range(8):
                    self.incw(v, v.tensor_copy(ynT[:, dc, j * 128:(j + 1) * 128],
                                           self.pT2[:, dc * 128:(dc + 1) * 128]), "v")

            @blk.tensor
            def _(t, j=j):
                self.w(t, "v")
                for dc in range(8):
                    self.inc(t.matmul(self.pW[0][:, 0:8],
                                      ynT[:, dc, j * 128:(j + 1) * 128],
                                      self.wWs[:, dc, :], start=(dc == 0), stop=(dc == 7)), "t")

            @blk.vector
            def _(v, j=j):
                self.w(v, "t")
                self.incw(v, v.tensor_tensor(out=zr[:], in0=self.pW[0][:, 0:8],
                                         in1=self.t_bsbc, op=OP.add), "v")
                self.incw(v, v.tensor_reduce(out=self.sccol[:, 0:1], in_=zr[:], axis=AX.X,
                                         op=OP.max), "v")
                self.incw(v, v.tensor_scalar(out=self.sccol[:, 1:2], in0=self.sccol[:, 0:1],
                                         scalar1=-1.0, scalar2=None, op0=OP.mult), "v")

            @blk.scalar
            def _(s, j=j):
                self.w(s, "v")
                self.inc(s.activation(out=er[:], in_=zr[:], func=AF.Exp,
                                      bias=self.sccol[:, 1:2], scale=self.cc1("one"),
                                      accum_out=self.sccol[:, 2:3]), "s")

            @blk.vector
            def _(v, j=j):
                self.w(v, "s")
                self.incw(v, v.reciprocal(self.sccol[:, 3:4], self.sccol[:, 2:3]), "v")
                self.w(v, "v")
                self.incw(v, v.tensor_scalar(out=er[:], in0=er[:], scalar1=self.sccol[:, 3:4],
                                         scalar2=None, op0=OP.mult), "v")

            @blk.tensor
            def _(t, j=j):
                self.w(t, "v")
                self.inc(t.transpose(self.pM[0:8, 0:128], er[:], self.t_identf), "t")

            @blk.vector
            def _(v, j=j):
                self.w(v, "t")
                self.incw(v, v.tensor_copy(pTl[:, j * 128:(j + 1) * 128], self.pM[0:8, 0:128]), "v")

        @blk.sync
        def _(q):
            self.w(q, "v")
            self.inc(q.dma_start(out=self.pT_loc[:, :], in_=pTl[:]), "d_pt", 16)

        if self.dbg("y"):
            pass

    def p8_collect(self):
        blk = self.block

        @blk.gpsimd
        def _(g):
            self.w(g, "d_yn0")
            self.w(g, "d_yn1")
            self.w(g, "d_pt")
            self.inc(g.collective_compute(
                "AllGather", OP.bypass, replica_groups=[list(range(8))],
                ins=[self.yn_loc[:, :]], outs=[self.yn_all[:, :]]), "coll")
            self.inc(g.collective_compute(
                "AllGather", OP.bypass, replica_groups=[list(range(8))],
                ins=[self.pT_loc[:, :]], outs=[self.pT_all[:, :]]), "coll")

    def p9_route(self, stack):
        blk = self.block
        pc = self.sbctx(stack, "pc", (128, 1, 128))
        self.wrapB = self.sbctx(stack, "wrapB", (128, NTOK // 16), I16)

        @blk.vector
        def _(v):
            self.incw(v, v.memset(self.wrapB[:], 0), "v")
        lo = self.sccol[:, 4:5]
        hi = self.sccol[:, 5:6]
        cntc = self.sccol[:, 6:7]
        gec = self.sccol[:, 7:8]
        idx8 = self.sbctx(stack, "idx8", (128, 8), I16)
        gtf = self.sbctx(stack, "gtf", (128, 128))
        tie = self.sbctx(stack, "tie", (128, 128))
        csr = self.sbctx(stack, "csr", (128, 128))
        hstage = self.sbctx(stack, "hstage", (128, 256), BF16)
        hgat = self.sbctx(stack, "hgat", (128, 16, 256), BF16)
        offrow = self.sbctx(stack, "offrow", (1, 128))
        orbc = self.sbctx(stack, "orbc", (128, 128))
        pqt = self.sbctx(stack, "pqt", (128, 16))
        kqt = self.sbctx(stack, "kqt", (128, 32))

        idxn = self.sbctx(stack, "idxn", (128, 1))

        @blk.vector
        def _(v):
            self.incw(v, v.memset(idx8[:], 0), "v")

        @blk.vector
        def _(v):
            self.w(v, "coll")
            self.incw(v, v.tensor_scalar(out=self.sccol[:, 0:1], in0=self.t_cid, scalar1=16.0,
                                     scalar2=None, op0=OP.mult), "v")
            self.incw(v, v.tensor_tensor(out=idxn[:], in0=self.t_r128s,
                                     in1=self.sccol[:, 0:1], op=OP.add), "v")

        self.wrap_build(128, idxn[:, :], idx8)

        @blk.gpsimd
        def _(g):
            self.w(g, "v")
            self.inc(g.dma_gather(
                out_ap=pc[:, :, :],
                in_ap=self.pT_all.rearrange("e (c j) -> (e c) j", j=128),
                idxs_ap=idx8[:, :], num_idxs=128, num_idxs_reg=128,
                elem_size=128), "d_g", 16)

        @blk.vector
        def _(v):
            self.w(v, "d_g")
            self.incw(v, v.memset(lo, 0.0), "v")
            self.incw(v, v.memset(hi, 1.0), "v")

        for it in range(NBIS):
            @blk.vector
            def _(v, it=it):
                self.w(v, "t")
                self.incw(v, v.tensor_tensor(out=self.sccol[:, 0:1], in0=lo, in1=hi, op=OP.add), "v")
                self.incw(v, v.tensor_scalar(out=self.sccol[:, 0:1], in0=self.sccol[:, 0:1],
                                         scalar1=0.5, scalar2=None, op0=OP.mult), "v")
                self.w(v, "v")
                self.incw(v, v.tensor_scalar(out=self.scu8[:, 0:128], in0=pc[:, 0, :],
                                         scalar1=self.sccol[:, 0:1], scalar2=0.0,
                                         op0=OP.is_gt, op1=OP.add,
                                         accum_out=cntc), "v")

            @blk.tensor
            def _(t, it=it):
                self.w(t, "v")
                self.inc(t.matmul(self.pM[:, 0:1], self.t_ones, cntc,
                                  start=True, stop=True), "t")

            @blk.vector
            def _(v, it=it):
                self.w(v, "t")
                self.incw(v, v.tensor_copy(cntc, self.pM[:, 0:1]), "v")
                self.incw(v, v.tensor_scalar(out=gec, in0=cntc, scalar1=float(KEXP),
                                         scalar2=None, op0=OP.is_ge), "v")
                self.incw(v, v.tensor_tensor(out=self.sccol[:, 2:3], in0=self.sccol[:, 0:1],
                                         in1=lo, op=OP.subtract), "v")
                self.w(v, "v")
                self.incw(v, v.scalar_tensor_tensor(out=lo, in0=self.sccol[:, 2:3],
                                                scalar=gec, in1=lo,
                                                op0=OP.mult, op1=OP.add), "v")
                self.incw(v, v.tensor_scalar(out=self.sccol[:, 3:4], in0=gec, scalar1=-1.0,
                                         scalar2=1.0, op0=OP.mult, op1=OP.add), "v")
                self.incw(v, v.tensor_tensor(out=self.sccol[:, 2:3], in0=self.sccol[:, 0:1],
                                         in1=hi, op=OP.subtract), "v")
                self.w(v, "v")
                self.incw(v, v.scalar_tensor_tensor(out=hi, in0=self.sccol[:, 2:3],
                                                scalar=self.sccol[:, 3:4], in1=hi,
                                                op0=OP.mult, op1=OP.add), "v")

        @blk.vector
        def _(v):
            self.w(v, "v")
            self.incw(v, v.tensor_scalar(out=gtf[:], in0=pc[:, 0, :], scalar1=hi,
                                     scalar2=0.0, op0=OP.is_gt, op1=OP.add,
                                     accum_out=cntc), "v")

        @blk.tensor
        def _(t):
            self.w(t, "v")
            self.inc(t.matmul(self.pM[:, 0:1], self.t_ones, cntc, start=True, stop=True), "t")

        @blk.vector
        def _(v):
            self.w(v, "t")
            self.incw(v, v.tensor_copy(cntc, self.pM[:, 0:1]), "v")
            self.incw(v, v.tensor_scalar(out=self.sccol[:, 2:3], in0=cntc, scalar1=-1.0,
                                     scalar2=float(KEXP), op0=OP.mult, op1=OP.add), "v")
            self.w(v, "v")
            self.incw(v, v.tensor_scalar(out=tie[:], in0=pc[:, 0, :], scalar1=lo,
                                     scalar2=None, op0=OP.is_gt), "v")
            self.incw(v, v.tensor_tensor(out=tie[:], in0=tie[:], in1=gtf[:], op=OP.subtract), "v")
            self.incw(v, v.tensor_tensor_scan(out=csr[:], data0=self.t_onesw[:, 0:128],
                                          data1=tie[:], initial=0.0,
                                          op0=OP.mult, op1=OP.add), "v")
            self.incw(v, v.tensor_copy(self.sccol[:, 0:1], csr[:, 127:128]), "v")

        @blk.tensor
        def _(t):
            self.w(t, "v")
            self.inc(t.matmul(self.pM[:, 0:1], self.t_upst, self.sccol[:, 0:1],
                              start=True, stop=True), "t")

        @blk.vector
        def _(v):
            self.incw(v, v.tensor_tensor_scan(out=csr[:], data0=self.t_onesw[:, 0:128],
                                          data1=gtf[:], initial=0.0,
                                          op0=OP.mult, op1=OP.add), "v")
            self.incw(v, v.tensor_copy(self.sccol[:, 0:1], csr[:, 127:128]), "v")

        @blk.tensor
        def _(t):
            self.w(t, "v")
            self.inc(t.matmul(self.pM[:, 0:1], self.t_upst, self.sccol[:, 0:1],
                              start=True, stop=True), "t")

        @blk.vector
        def _(v):
            self.w(v, "t")
            self.incw(v, v.tensor_copy(self.sccol[:, 1:2], self.pM[:, 0:1]), "v")
            self.incw(v, v.tensor_scalar(out=self.sccol[:, 2:3], in0=self.sccol[:, 0:1],
                                     scalar1=0.0, scalar2=None, op0=OP.is_equal), "v")
            self.incw(v, v.scalar_tensor_tensor(out=self.sccol[:, 2:3], in0=self.sccol[:, 2:3],
                                            scalar=1e9, in1=self.sccol[:, 1:2],
                                            op0=OP.mult, op1=OP.add), "v")
            self.incw(v, v.tensor_tensor(out=tie[:], in0=pc[:, 0, :], in1=gtf[:],
                                     op=OP.mult), "v")
            self.incw(v, v.tensor_copy(hstage[:, 0:128], csr[:]), "v")
            self.incw(v, v.tensor_copy(hstage[:, 128:256], tie[:]), "v")

        @blk.tensor
        def _(t):
            self.w(t, "v")
            self.inc(t.transpose(self.pM[0:1, 0:128], self.sccol[:, 2:3], self.t_identf), "t")

        @blk.vector
        def _(v):
            self.w(v, "t")
            self.incw(v, v.tensor_copy(offrow[0:1, 0:128], self.pM[0:1, 0:128]), "v")

        @blk.gpsimd
        def _(g):
            self.w(g, "v")
            g.partition_broadcast(orbc[:, :], offrow[0:1, 0:128])
            self.inc(g.nop(), "g")

        @blk.vector
        def _(v):
            self.w(v, "g")
            for qb in range(16):
                qv = self.t_iotacm[:, qb:qb + 1]
                orB = orbc[:, :]
                qvB = self.bcast_free(qv, 128)
                self.incw(v, v.tensor_tensor(out=csr[:], in0=orB, in1=qvB, op=OP.is_le), "v")
                self.incw(v, v.tensor_reduce(out=pqt[:, qb:qb + 1], in_=csr[:], axis=AX.X,
                                         op=OP.add), "v")
                self.incw(v, v.tensor_tensor(out=csr[:], in0=csr[:], in1=orB, op=OP.mult), "v")
                self.incw(v, v.tensor_reduce(out=kqt[:, qb:qb + 1], in_=csr[:], axis=AX.X,
                                         op=OP.max), "v")
            self.incw(v, v.tensor_scalar(out=pqt[:, 0:16], in0=pqt[:, 0:16], scalar1=-1.0,
                                     scalar2=None, op0=OP.add), "v")
            self.incw(v, v.tensor_tensor(out=kqt[:, 0:16], in0=self.t_iotacm[:, 0:16],
                                     in1=kqt[:, 0:16], op=OP.subtract), "v")
            self.incw(v, v.tensor_scalar(out=kqt[:, 16:32], in0=kqt[:, 0:16], scalar1=1.0,
                                     scalar2=None, op0=OP.add), "v")

        @blk.sync
        def _(q):
            self.w(q, "v")
            self.inc(q.dma_start(out=self.h9_hbm[:, :], in_=hstage[:, 0:256]), "d_tx", 16)

        self.wrap_build(KEXP, pqt[:, 0:16], self.wrapC)

        @blk.gpsimd
        def _(g):
            self.w(g, "v")
            self.w(g, "d_tx")
            self.inc(g.dma_gather(out_ap=hgat[:, :, :], in_ap=self.h9_hbm[:, :],
                                  idxs_ap=self.wrapC[:, :], num_idxs=KEXP,
                                  num_idxs_reg=KEXP, elem_size=256,
                                  single_packet=False), "d_g", 16)

        @blk.vector
        def _(v):
            self.w(v, "d_g")
            for qb in range(16):
                self.incw(v, v.tensor_scalar(out=csr[:], in0=hgat[:, qb, 0:128],
                                         scalar1=kqt[:, qb:qb + 1], scalar2=0.0,
                                         op0=OP.is_le, op1=OP.add,
                                         accum_out=self.sccol[:, 3:4]), "v")
                self.incw(v, v.scalar_tensor_tensor(out=self.tid[:, qb:qb + 1],
                                                in0=pqt[:, qb:qb + 1], scalar=128.0,
                                                in1=self.sccol[:, 3:4],
                                                op0=OP.mult, op1=OP.add), "v")
                self.incw(v, v.tensor_scalar(out=csr[:], in0=hgat[:, qb, 0:128],
                                         scalar1=kqt[:, 16 + qb:17 + qb], scalar2=None,
                                         op0=OP.is_equal), "v")
                self.incw(v, v.tensor_tensor_reduce(out=gtf[:], in0=csr[:],
                                                in1=hgat[:, qb, 128:256], scale=1.0,
                                                scalar=0.0, op0=OP.mult, op1=OP.add,
                                                accum_out=self.tval[:, qb:qb + 1]), "v")

        self.wrap_build(KEXP, self.tid[:, :], self.wrapC)

        if self.dbg("omout") or self.dbg("dbg9"):
            dst = self.dbg_out("tid", (128, 16))
            dst2 = self.dbg_out("tval", (128, 16))

            @blk.sync
            def _(q):
                self.w(q, "v")
                self.inc(q.dma_start(out=dst[:, :], in_=self.tid[:, :]), "d_dbg", 16)
                self.inc(q.dma_start(out=dst2[:, :], in_=self.tval[:, :]), "d_dbg", 16)

    def p10_ffn(self, stack):
        blk = self.block
        w2t = self.sbctx(stack, "w2t", (128, 32, D), BF16)
        hT = self.sbctx(stack, "hT", (128, 32, TBF), BF16)
        xg = self.sbctx(stack, "xg", (128, 4, D), BF16)
        xinT = self.sbctx(stack, "xinT", (128, 8, TBF), BF16)
        w1t = [self.sbctx(stack, f"w1t{i}", (128, 8, 128), BF16) for i in range(2)]
        ob = [self.sbctx(stack, f"ob{i}", (128, D), BF16) for i in range(2)]

        @blk.gpsimd
        def _(g):
            self.w(g, "d_wo")
            self.w(g, "t")
            self.w(g, "v")
            self.w(g, "s")
            self.inc(g.dma_start(out=w2t[:],
                                 in_=self.W2_in.rearrange("(c p) o -> p c o", p=128)),
                     "d_w2", 16)

        w2_tok = self.cnt["d_w2"]

        for tb in range(NTB):
            @blk.gpsimd
            def _(g, tb=tb):
                self.w(g, "v")
                self.inc(g.dma_gather(out_ap=xg[:, :, :], in_ap=self.yn_all[:, :],
                                      idxs_ap=self.wrapC[:, 32 * tb:32 * tb + 32],
                                      num_idxs=TBF, num_idxs_reg=TBF, elem_size=D),
                         "d_g", 16)

            for sb4 in range(4):
                @blk.tensor
                def _(t, tb=tb, sb4=sb4):
                    self.w(t, "d_g")
                    self.w(t, "v")
                    for dc in range(8):
                        self.inc(t.transpose(self.pT1[:, dc * 128:(dc + 1) * 128],
                                             xg[:, sb4, dc * 128:(dc + 1) * 128],
                                             self.t_identb), "t")

                @blk.vector
                def _(v, tb=tb, sb4=sb4):
                    self.w(v, "t")
                    for dc in range(8):
                        self.incw(v, v.tensor_copy(xinT[:, dc, sb4 * 128:(sb4 + 1) * 128],
                                               self.pT1[:, dc * 128:(dc + 1) * 128]), "v")

            @blk.gpsimd
            def _(g, tb=tb):
                self.w(g, "v")
                self.w(g, "t")
                self.inc(g.dma_start(
                    out=w1t[0][:, :, :],
                    in_=self.w1bf_hbm[0:D, :].rearrange("(c p) k -> p c k", p=128)),
                    "d_w1_0", 16)

            for fb in range(32):
                @blk.gpsimd
                def _(g, tb=tb, fb=fb):
                    if fb + 1 < 32:
                        self.w(g, "t")
                        self.inc(g.dma_start(
                            out=w1t[(fb + 1) % 2][:, :, :],
                            in_=self.w1bf_hbm[(fb + 1) * D:(fb + 2) * D, :].rearrange(
                                "(c p) k -> p c k", p=128)), f"d_w1_{(fb + 1) % 2}", 16)

                @blk.tensor
                def _(t, tb=tb, fb=fb):
                    self.w(t, f"d_w1_{fb % 2}")
                    self.w(t, "s")
                    for dc in range(8):
                        self.inc(t.matmul(self.pW[fb % 2][:, 0:TBF],
                                          w1t[fb % 2][:, dc, :],
                                          xinT[:, dc, :],
                                          start=(dc == 0), stop=(dc == 7)), "t")

                @blk.scalar
                def _(s, tb=tb, fb=fb):
                    self.w(s, "t")
                    self.inc(s.activation(out=hT[:, fb, :], in_=self.pW[fb % 2][:, 0:TBF],
                                          func=AF.Gelu_apprx_tanh,
                                          bias=self.t_b1c[:, fb:fb + 1],
                                          scale=self.cc1("one")), "s")

            for obi in range(TBF // 128):
                qi = tb * 4 + obi

                @blk.tensor
                def _(t, tb=tb, obi=obi):
                    self.w(t, "s")
                    self.w(t, "d_w2", w2_tok)
                    self.w(t, "v")
                    for half in range(2):
                        for fb in range(32):
                            self.inc(t.matmul(self.pW[half][:, 0:512],
                                              hT[:, fb, obi * 128:(obi + 1) * 128],
                                              w2t[:, fb, half * 512:(half + 1) * 512],
                                              start=(fb == 0), stop=(fb == 31)), "t")

                @blk.vector
                def _(v, tb=tb, obi=obi, qi=qi):
                    self.w(v, "t")
                    self.w(v, f"d_ob{qi % 2}")
                    for half in range(2):
                        sl = slice(half * 512, (half + 1) * 512)
                        self.incw(v, v.tensor_tensor(out=self.sc1[:, sl],
                                                 in0=self.pW[half][:, 0:512],
                                                 in1=self.t_b2bc[:, sl], op=OP.add), "v")
                    self.incw(v, v.tensor_scalar(out=ob[qi % 2][:], in0=self.sc1[:, 0:D],
                                             scalar1=self.tval[:, qi:qi + 1],
                                             scalar2=None, op0=OP.mult), "v")

                @blk.gpsimd
                def _(g, qi=qi):
                    self.w(g, "v")
                    self.w(g, "d_z2")
                    self.inc(g.dma_scatter_add(
                        out_ap=self.dense_hbm[:, :],
                        in_ap=ob[qi % 2].rearrange("p (a d) -> p a d", a=1),
                        idxs_ap=self.wrapC[:, 8 * qi:8 * qi + 8], num_idxs=128,
                        num_idxs_reg=128, elem_size=D, single_packet=False),
                        f"d_ob{qi % 2}", 16)

                if self.dbg("obdbg"):
                    if not hasattr(self, "_obdbg"):
                        self._obdbg = self.dbg_out("obuf", (KEXP, D), BF16)

                    @blk.sync
                    def _(q, qi=qi):
                        self.w(q, "v")
                        self.inc(q.dma_start(
                            out=self._obdbg.rearrange("(c p) d -> p c d", p=128)[:, qi, :],
                            in_=ob[qi % 2][:]), "d_dbg", 16)

    def p11_collect(self):
        blk = self.block

        @blk.gpsimd
        def _(g):
            self.w(g, "d_ob0")
            self.w(g, "d_ob1")
            self.inc(g.collective_compute(
                "ReduceScatter", OP.add, replica_groups=[list(range(8))],
                ins=[self.dense_hbm[:, :]], outs=[self.dense_rs[:, :]]), "coll")

    def p12_combine(self, stack):
        blk = self.block
        ab = [self.sbctx(stack, f"adb{i}", (128, D), BF16) for i in range(2)]

        @blk.sync
        def _(q):
            self.w(q, "coll")
            for j in range(2):
                self.inc(q.dma_start(
                    out=ab[j][:],
                    in_=self.dense_rs.rearrange("(c p) d -> p c d", p=128)[:, j, :]),
                    f"d_x{j % 2}", 16)

        for j in range(NT):
            @blk.gpsimd
            def _(g, j=j):
                self.w(g, f"d_x{j % 2}")
                self.w(g, "d_y0")
                self.w(g, "d_y1")
                self.inc(g.dma_start(
                    out=self.out.rearrange("(c p) d -> p c d", p=128)[:, j, :],
                    in_=ab[j % 2][:], accum_op=OP.add), "d_comb", 16)

            if j + 2 < NT:
                @blk.sync
                def _(q, j=j):
                    self.w(q, "d_comb")
                    self.inc(q.dma_start(
                        out=ab[j % 2][:],
                        in_=self.dense_rs.rearrange("(c p) d -> p c d", p=128)[:, j + 2, :]),
                        f"d_x{j % 2}", 16)

    def finale(self):
        blk = self.block

        @blk.gpsimd
        def _(g):
            for name in list(self._sems):
                if name.startswith("d_") or name in ("v", "s", "t", "g", "coll"):
                    self.w(g, name)

    def build(self):
        from contextlib import ExitStack
        self.declare()
        with ExitStack() as top:
            self.block = top.enter_context(self.nc.Block())
            for nm in ("d_y0", "d_y1", "d_yn0", "d_yn1", "d_ob0", "d_ob1",
                       "d_om", "d_sh", "d_g", "d_wm", "coll"):
                self.sem(nm)
            self.sem("t")
            self.sem("v")
            self.sem("s")
            self.sem("g")
            self.p0_setup()
            if self.upto >= 1:
                with ExitStack() as stA:
                    self.xnT = self.sbctx(stA, "xnT", (128, 8, S), BF16)
                    self.p1_ln_outer(stA)
                    if self.upto >= 6:
                        with ExitStack() as st67:
                            self.og = self.sbctx(st67, "og", (128, NT, DV), BF16)
                            with ExitStack() as st:
                                self.p6_reco(st)
                            if self.upto >= 7:
                                with ExitStack() as st:
                                    self.p7_out(st)
            if self.upto >= 8:
                self.p8_collect()
            if self.upto >= 9:
                self.phase_barrier()
                with ExitStack() as st:
                    self.p9_route(st)
            if self.upto >= 10:
                self.phase_barrier()
                with ExitStack() as st:
                    self.p10_ffn(st)
            if self.upto >= 11:
                self.p11_collect()
            if self.upto >= 12:
                self.phase_barrier()
                with ExitStack() as st:
                    self.p12_combine(st)
            self.finale()
        lower_extended_insts(self.nc)
        return self.nc

    def p1_ln_outer(self, st_outer):
        from contextlib import ExitStack
        with ExitStack() as st:
            self.p1_ln(st)
            if self.upto >= 2:
                self.p2_gate(st)
        if self.upto >= 10:
            self.p0b_precast_w1()
        if self.upto >= 4:
            with ExitStack() as st:
                self.p_gla(st)

    def p0_setup(self):
        blk = self.block

        @blk.sync
        def _(q):
            for dst, src in [
                (self.t_cc, self.cc_in), (self.t_identf, self.identf_in),
                (self.t_identb, self.identb_in), (self.t_triu, self.triu_in),
                (self.t_upst, self.upst_in), (self.t_ones, self.onesm_in),
                (self.t_iota4, self.iota4_in), (self.t_iotatg, self.iotatg_in),
                (self.t_iotacm, self.iotacm_in), (self.t_r128s, self.r128s_in),
                (self.t_iotarows, self.iotarows_in), (self.t_onesw, self.onesw_in),
                (self.t_normw, self.normw_in), (self.t_bsbc, self.bs_in),
                (self.t_b2bc, self.b2_in), (self.t_b1c, self.b1_in),
                (self.t_bgk, self.bgk_in), (self.t_bgks, self.bgks_in),
                (self.t_cid1, self.cid_in),
            ]:
                self.inc(q.dma_start(out=dst[:], in_=src[:, :]), "d_c", 16)

        @blk.vector
        def _(v):
            self.w(v, "d_c")
            self.inc(v.memset(self.zf32[:], 0.0), "v")
            self.inc(v.memset(self.t_zbf[:], 0.0), "v")
            for wt in (self.wrap0, self.wrap1, self.wrapC):
                self.inc(v.memset(wt[:], 0), "v")

        @blk.scalar
        def _(s):
            self.w(s, "d_c")

        @blk.tensor
        def _(t):
            self.w(t, "d_c")

        @blk.gpsimd
        def _(g):
            g.load_library(MLP_LIB)
            self.inc(g.dma_start(out=self.wgate[:],
                                 in_=self.w_gate_in.rearrange("(c p) o -> p c o", p=128)),
                     "d_w", 16)
            self.inc(g.dma_start(out=self.wq[:],
                                 in_=self.w_q_in.rearrange("(c p) o -> p c o", p=128)),
                     "d_w", 16)
            self.w(g, "d_c")
            g.partition_broadcast(self.t_cid[:], self.t_cid1[0:1, 0:1])
            self.inc(g.nop(), "g")
            self.w(g, "v")
            for r0 in range(0, NMEM * CAP, 128):
                self.inc(g.dma_start(
                    out=self.tx_hbm[r0:r0 + 128, :],
                    in_=self.zf32[:, :]), "d_z", 16)
            self.inc(g.dma_start(out=self.tx_hbm[NMEM * CAP:NMEM * CAP + 8, :],
                                 in_=self.zf32[0:8, :]), "d_z", 16)
            self.inc(g.dma_start(out=self.om_hbm[NMEM * CAP:NMEM * CAP + 8, :],
                                 in_=self.zf32[0:8, :]), "d_z", 16)

        @blk.scalar
        def _(s):
            self.w(s, "v")
            for i in range(8):
                out3 = self.dense_hbm[i * 2048:(i + 1) * 2048, :].rearrange(
                    "(c p) d -> p c d", p=128)
                self.inc(s.dma_start(out=out3,
                                     in_=self.bcast_mid(self.t_zbf[:], 16)),
                         "d_z2", 16)


    def p0b_precast_w1(self):
        blk = self.block

        @blk.gpsimd
        def _(g):
            atok = {0: 0, 1: 0}
            btok = {0: 0, 1: 0}
            i = 0
            for dc in range(8):
                for ch in range(4):
                    b = i % 2
                    half = self.t_w1s[:, b * 1024:(b + 1) * 1024]
                    if btok[b]:
                        self.w(g, f"d_w1b{b}", btok[b])
                    atok[b] = self.inc(g.dma_start(
                        out=half,
                        in_=self.W1_in[dc * 128:(dc + 1) * 128,
                                       ch * 1024:(ch + 1) * 1024]), f"d_w1a{b}", 16)
                    self.w(g, f"d_w1a{b}", atok[b])
                    btok[b] = self.inc(g.dma_start(
                        out=self.w1bf_hbm.rearrange("(f c p) k -> c p f k", c=8, p=128)[
                            dc][:, ch * 8:(ch + 1) * 8, :],
                        in_=half.rearrange("p (f k) -> p f k", k=128)), f"d_w1b{b}", 16)
                    i += 1
                    self.w(g, "d_w1b")


_NC = None


def kernel(**inputs):
    global _NC
    in_maps = host_inputs(inputs)
    if _NC is None:
        _NC = MK(upto=12).build()
    res = run_bass_kernel_spmd(_NC, in_maps, core_ids=list(range(8)))
    out = np.stack([np.asarray(r["out"], np.float32) for r in res.results], 0)
    return out.reshape(8, 2048, 1024)


# version 08 (final)
# speedup vs baseline: 2.9817x; 1.1294x over previous
from concourse.bass_utils import run_bass_kernel_spmd

"""Megakernel builder: GLA mixture-of-memories attention + expert-choice MoE FFN.

One raw-Bass SPMD program (identical across cores). Core c's in_map carries batch
row c (attention) and expert c's FFN weights. Cross-core via AllGather. NOTES.md
has the full design. v1: conservative (serializing) semaphore discipline.
"""
import numpy as np
import concourse.bass as bass
import concourse.mybir as mybir
from concourse.library_config import mlp as MLP_LIB
from concourse.library_overlay import lower_extended_insts

F32 = mybir.dt.float32
BF16 = mybir.dt.bfloat16
I16 = mybir.dt.int16
I32 = mybir.dt.int32
U8 = mybir.dt.uint8
AF = mybir.ActivationFunctionType
OP = mybir.AluOpType
AX = mybir.AxisListType

S, D = 2048, 1024
NMEM, CAP, H = 4, 1024, 4
DK, DV, HDK, HDV, LOW = 512, 1024, 128, 256, 16
NE, FF, KEXP, NTOK = 8, 4096, 2048, 16384
NT = S // 128
GATE_NORM, EPS = 16.0, 1e-5
QSCALE = float(HDK) ** -0.5
TBF = 512
NTB = KEXP // TBF
NBIS = 44

CC = dict(zero=0, one=1, eps=2, lnqs=3, inv16=4, neg16=5, negone=6, half=7)
NCC = 8


def host_constants():
    import ml_dtypes
    cc = np.zeros((128, NCC), np.float32)
    cc[:, CC["one"]] = 1.0
    cc[:, CC["eps"]] = EPS
    cc[:, CC["lnqs"]] = np.log(QSCALE)
    cc[:, CC["inv16"]] = 1.0 / GATE_NORM
    cc[:, CC["neg16"]] = -1.0 / GATE_NORM
    cc[:, CC["negone"]] = -1.0
    cc[:, CC["half"]] = 0.5
    p = np.arange(128)
    return {
        "cc": cc,
        "ident_f": np.eye(128, dtype=np.float32),
        "ident_b": np.eye(128, dtype=ml_dtypes.bfloat16),
        "triu_inc": np.triu(np.ones((128, 128), np.float32)).astype(ml_dtypes.bfloat16),
        "upper_strict": np.triu(np.ones((128, 128), np.float32), 1),
        "ones_mat": np.ones((128, 128), np.float32),
        "iota4_bc": np.tile(np.arange(4, dtype=np.float32), (128, 1)),
        "iota_tg": (128.0 * p[:, None] + np.arange(128)[None, :]).astype(np.float32),
        "iota_cm": (p[:, None] + 128.0 * np.arange(128)[None, :]).astype(np.float32),
        "r128s": (128.0 * (p // 16) + (p % 16)).astype(np.float32).reshape(128, 1),
        "iota_rows": ((np.arange(128)[None, :] // 16) * 2048.0
                      + (np.arange(128)[None, :] % 16) * 128.0
                      + p[:, None] * 1.0).astype(np.float32),
        "ones_w": np.ones((128, S), ml_dtypes.bfloat16),
    }


def host_inputs(inputs):
    consts = host_constants()
    x = np.asarray(inputs["L0_attn_x"], np.float32)
    bgk = np.asarray(inputs["L0_attn_b_gk"], np.float32)
    bgk_t = np.zeros((128, NMEM * H), np.float32)
    for m in range(NMEM):
        for h in range(H):
            bgk_t[:, m * H + h] = -bgk[m, h * 128:(h + 1) * 128]
    bgks = np.asarray(inputs["L0_attn_b_gks"], np.float32)
    bgks_t = np.zeros((128, H), np.float32)
    for h in range(H):
        bgks_t[:, h] = -bgks[h * 128:(h + 1) * 128]
    normw = np.asarray(inputs["L0_attn_norm_w"], np.float32)
    b1 = np.asarray(inputs["L0_ffn_b1"], np.float32)
    b2 = np.asarray(inputs["L0_ffn_b2"], np.float32)
    W1 = np.asarray(inputs["L0_ffn_W1"], np.float32)
    W2 = np.asarray(inputs["L0_ffn_W2"], np.float32)
    base = {
        "w_gate": np.asarray(inputs["L0_attn_w_gate"], np.float32),
        "w_q": np.asarray(inputs["L0_attn_w_q"], np.float32),
        "w_k": np.asarray(inputs["L0_attn_w_k"], np.float32),
        "w_v": np.asarray(inputs["L0_attn_w_v"], np.float32),
        "w_gk1": np.asarray(inputs["L0_attn_w_gk1"], np.float32),
        "w_gk2": np.asarray(inputs["L0_attn_w_gk2"], np.float32),
        "b_gk_t": bgk_t,
        "w_ks": np.asarray(inputs["L0_attn_w_ks"], np.float32),
        "w_vs": np.asarray(inputs["L0_attn_w_vs"], np.float32),
        "w_gk1s": np.asarray(inputs["L0_attn_w_gk1s"], np.float32),
        "w_gk2s": np.asarray(inputs["L0_attn_w_gk2s"], np.float32),
        "b_gks_t": bgks_t,
        "w_g": np.asarray(inputs["L0_attn_w_g"], np.float32),
        "w_o": np.asarray(inputs["L0_attn_w_o"], np.float32),
        "normw_bc": np.tile(np.tile(normw, H)[None, :], (128, 1)).astype(np.float32),
        "ffn_Ws": np.asarray(inputs["L0_ffn_Ws"], np.float32),
        "ffn_bs_bc": np.tile(np.asarray(inputs["L0_ffn_bs"], np.float32)[None, :], (128, 1)),
    }
    base.update(consts)
    in_maps = []
    for c in range(8):
        m = dict(base)
        m["x"] = np.ascontiguousarray(x[c])
        m["ffn_W1"] = np.ascontiguousarray(W1[c])
        m["ffn_b1c"] = np.ascontiguousarray(b1[c].reshape(FF // 128, 128).T)
        m["ffn_W2"] = np.ascontiguousarray(W2[c])
        m["ffn_b2_bc"] = np.tile(b2[c][None, :], (128, 1)).astype(np.float32)
        m["coreid"] = np.full((1, 1), float(c), np.float32)
        in_maps.append(m)
    return in_maps


class MK:
    def __init__(self, upto=99, debug=(), gla_order=(4, 0, 1, 2, 3), p12mode="full"):
        self.p12mode = p12mode
        self.gla_order = list(gla_order)
        self.upto = upto
        self.debug = set(debug)
        self.dbg_t = {}
        self.nc = bass.Bass("TRN2", target_bir_lowering=False, debug=False,
                            detect_race_conditions=True)
        self._sems = {}
        self.cnt = {}

    def sem(self, name):
        if name not in self._sems:
            self._sems[name] = self.nc.alloc_semaphore(name)
            self.cnt[name] = 0
        return self._sems[name]

    def inc(self, instr, name, k=1):
        h = self.sem(name)
        instr.then_inc(h, k)
        self.cnt[name] += k
        return self.cnt[name]

    def incw(self, eng, instr, name, k=1):
        n = self.inc(instr, name, k)
        eng.wait_ge(self.sem(name), n)
        return n

    def w(self, eng, name, val=None):
        h = self.sem(name)
        v = self.cnt[name] if val is None else val
        if v > 0:
            eng.wait_ge(h, v)

    def din(self, name, shape, dtype=F32):
        return self.nc.dram_tensor(name, list(shape), dtype, kind="ExternalInput").ap()

    def dout(self, name, shape, dtype=F32):
        return self.nc.dram_tensor(name, list(shape), dtype, kind="ExternalOutput").ap()

    def dint(self, name, shape, dtype=F32):
        kind = "ExternalOutput" if "omout" in self.debug else "Internal"
        return self.nc.dram_tensor(name, list(shape), dtype, kind=kind).ap()

    def sb(self, name, shape, dtype=F32):
        return self.nc.alloc_sbuf_tensor(name, list(shape), dtype).ap()

    def sbctx(self, stack, name, shape, dtype=F32):
        t = stack.enter_context(self.nc.sbuf_tensor(name, list(shape), dtype))
        return t.ap() if hasattr(t, "ap") else t[:]

    def dbg(self, name):
        return name in self.debug

    def dbg_out(self, name, shape, dtype=F32):
        self.dbg_t[name] = self.dout("dbg_" + name, shape, dtype)
        return self.dbg_t[name]

    def cc1(self, key):
        return self.t_cc[:, CC[key]:CC[key] + 1]

    def dump_tile(self, tile_ap, name, shape, dtype=F32):
        if not self.dbg(name):
            return
        dst = self.dbg_out(name, shape, dtype)
        blk = self.block

        @blk.sync
        def _(q):
            self.w(q, "v")
            self.w(q, "s")
            self.inc(q.dma_start(out=dst[:, :] if len(shape) == 2 else dst[:, :, :],
                                 in_=tile_ap), "d_dbg", 16)

    def declare(self):
        self.x_in = self.din("x", (S, D))
        self.w_gate_in = self.din("w_gate", (D, NMEM))
        self.w_q_in = self.din("w_q", (D, DK))
        self.w_k_in = self.din("w_k", (NMEM, D, DK))
        self.w_v_in = self.din("w_v", (NMEM, D, DV))
        self.w_gk1_in = self.din("w_gk1", (NMEM, D, LOW))
        self.w_gk2_in = self.din("w_gk2", (NMEM, LOW, DK))
        self.bgk_in = self.din("b_gk_t", (128, NMEM * H))
        self.w_ks_in = self.din("w_ks", (D, DK))
        self.w_vs_in = self.din("w_vs", (D, DV))
        self.w_gk1s_in = self.din("w_gk1s", (D, LOW))
        self.w_gk2s_in = self.din("w_gk2s", (LOW, DK))
        self.bgks_in = self.din("b_gks_t", (128, H))
        self.w_g_in = self.din("w_g", (D, DV))
        self.w_o_in = self.din("w_o", (DV, D))
        self.normw_in = self.din("normw_bc", (128, DV))
        self.Ws_in = self.din("ffn_Ws", (D, NE))
        self.bs_in = self.din("ffn_bs_bc", (128, NE))
        self.W1_in = self.din("ffn_W1", (D, FF))
        self.b1_in = self.din("ffn_b1c", (128, FF // 128))
        self.W2_in = self.din("ffn_W2", (FF, D))
        self.b2_in = self.din("ffn_b2_bc", (128, D))
        self.cid_in = self.din("coreid", (1, 1))
        self.cc_in = self.din("cc", (128, NCC))
        self.identf_in = self.din("ident_f", (128, 128))
        self.identb_in = self.din("ident_b", (128, 128), BF16)
        self.triu_in = self.din("triu_inc", (128, 128), BF16)
        self.upst_in = self.din("upper_strict", (128, 128))
        self.onesm_in = self.din("ones_mat", (128, 128))
        self.iota4_in = self.din("iota4_bc", (128, 4))
        self.iotatg_in = self.din("iota_tg", (128, 128))
        self.iotacm_in = self.din("iota_cm", (128, 128))
        self.r128s_in = self.din("r128s", (128, 1))
        self.iotarows_in = self.din("iota_rows", (128, 128))
        self.onesw_in = self.din("ones_w", (128, S), BF16)

        self.out = self.dout("out", (S, D))

        self.tx_hbm = self.dint("tx_hbm", (NMEM * CAP + 8, D), BF16)
        self.om_hbm = self.dint("om_hbm", (NMEM * CAP + 8, D), BF16)
        self.sh_hbm = self.dint("sh_hbm", (S, D), BF16)
        self.w1bf_hbm = self.dint("w1bf_hbm", (FF // 128 * D, 128), BF16)
        self.yn_loc = self.dint("yn_loc", (S, D), BF16)
        self.yn_all = self.dint("yn_all", (NTOK, D), BF16)
        self.pT_loc = self.dint("pT_loc", (NE, S))
        self.pT_all = self.dint("pT_all", (NE * NE, S))
        self.h9_hbm = self.dint("h9_hbm", (128, 256), BF16)
        self.dense_hbm = self.dint("dense_hbm", (NTOK, D), BF16)
        self.dense_rs = self.dint("dense_rs", (S, D), BF16)

        self.t_cc = self.sb("t_cc", (128, NCC))
        self.t_identf = self.sb("t_identf", (128, 128))
        self.t_identb = self.sb("t_identb", (128, 128), BF16)
        self.t_triu = self.sb("t_triu", (128, 128), BF16)
        self.t_upst = self.sb("t_upst", (128, 128))
        self.t_ones = self.sb("t_ones", (128, 128))
        self.t_iota4 = self.sb("t_iota4", (128, 4))
        self.t_iotatg = self.sb("t_iotatg", (128, 128))
        self.t_iotacm = self.sb("t_iotacm", (128, 128))
        self.t_r128s = self.sb("t_r128s", (128, 1))
        self.t_iotarows = self.sb("t_iotarows", (128, 128))
        self.t_onesw = self.sb("t_onesw", (128, S), BF16)
        self.t_normw = self.sb("t_normw", (128, DV))
        self.t_bsbc = self.sb("t_bsbc", (128, NE))
        self.t_b2bc = self.sb("t_b2bc", (128, D))
        self.t_b1c = self.sb("t_b1c", (128, FF // 128))
        self.t_bgk = self.sb("t_bgk", (128, NMEM * H))
        self.t_bgks = self.sb("t_bgks", (128, H))
        self.t_cid = self.sb("t_cid", (128, 1))
        self.t_cid1 = self.sb("t_cid1", (1, 1))
        self.t_w1s = self.sb("t_w1s", (128, 2048), BF16)
        self.rw = self.sb("rw", (128, NT, 2))
        self.disp_idx = self.sb("disp_idx", (128, 2 * NT))
        self.wrap0 = self.sb("wrap0", (128, S // 16), I16)
        self.wrap1 = self.sb("wrap1", (128, S // 16), I16)
        self.wrapC = self.sb("wrapC", (128, KEXP // 16), I16)
        self.wq = self.sb("wq", (128, 8, DK), BF16)
        self.wgate = self.sb("wgate", (128, 8, NMEM), BF16)
        self.wWs = self.sb("wWs", (128, 8, NE), BF16)
        self.t_zbf = self.sb("t_zbf", (128, 1024), BF16)
        self.ln_a = self.sb("ln_a", (128, 8))
        self.snk = self.sb("snk", (128, 1024))
        self.snk_v = self.sb("snk_v", (128, 1024))
        self.wrA = self.sb("wrA", (128, 128))
        self.sc1 = self.sb("sc1", (128, 1024))
        self.sc3 = self.sb("sc3", (128, 128))
        self.sc4 = self.sb("sc4", (128, 128))
        self.sc5 = self.sb("sc5", (128, 128))
        self.sc6 = self.sb("sc6", (128, 128))
        self.scu8 = self.sb("scu8", (128, 128), U8)
        self.sccol = self.sb("sccol", (128, 16))
        self.sci32 = self.sb("sci32", (1, 8), I32)
        self.tid = self.sb("tid", (128, 16))
        self.cnt8 = self.sb("cnt8", (128, 8))
        self.rwd = self.sb("rwd", (128, 2 * NT))
        self.tval = self.sb("tval", (128, 16))

        self.nc.const_aps.register(F32, 0.0, self.cc1("zero"))
        self.nc.const_aps.register(F32, 1.0, self.cc1("one"))

        self.pA = self.sb_ps("pA", F32)
        self.pO = self.sb_ps("pO", F32)
        self.pD = self.sb_ps("pD", F32)
        self.pT1 = self.sb_ps("pT1", BF16)
        self.pT2 = self.sb_ps("pT2", BF16)
        self.pW = [self.sb_ps("pW0", F32), self.sb_ps("pW1", F32)]
        self.pM = self.sb_ps("pM", F32)

    def sb_ps(self, name, dtype):
        n = 512 if dtype == F32 else 1024
        return self.nc.alloc_psum_tensor(name, [128, n], dtype).ap()

    @staticmethod
    def bcast_part(ap_1xN, P=128):
        import dataclasses
        st, n = ap_1xN.ap[-1]
        return dataclasses.replace(ap_1xN, ap=[[0, P], [st, n]])

    @staticmethod
    def bcast_free(ap_Px1, N):
        import dataclasses
        pst, pn = ap_Px1.ap[0]
        return dataclasses.replace(ap_Px1, ap=[[pst, pn], [0, N]])

    @staticmethod
    def bcast_mid(ap_PxN, M):
        import dataclasses
        return dataclasses.replace(
            ap_PxN, ap=[list(ap_PxN.ap[0]), [0, M], list(ap_PxN.ap[-1])])

    def phase_barrier(self):
        blk = self.block

        @blk.gpsimd
        def _(g):
            for name in list(self._sems):
                if name.startswith("d_"):
                    self.w(g, name)
            self.w(g, "t")
            self.w(g, "v")
            self.w(g, "s")
            self.inc(g.nop(), "g")

        for mk_sec in (blk.vector, blk.scalar, blk.tensor, blk.sync):
            @mk_sec
            def _(e):
                self.w(e, "g")

    def wrap_build(self, n, nat_ap, out_tile, replicate=2):
        m = n // 128
        blk = self.block

        @blk.tensor
        def _(t):
            self.w(t, "v")
            self.w(t, "s")
            self.inc(t.transpose(self.pM[0:m, 0:128], nat_ap, self.t_identf), "t")

        @blk.vector
        def _(v):
            self.w(v, "t")
            self.inc(v.tensor_copy(self.wrA[0:m, 0:128], self.pM[0:m, 0:128]), "v")

        for g8 in range(8):
            @blk.tensor
            def _(t, g8=g8):
                self.w(t, "v")
                self.inc(t.transpose(self.pM[0:16, 0:m], self.wrA[0:m, 16 * g8:16 * g8 + 16],
                                     self.t_identf[0:m, 0:m]), "t")

            @blk.vector
            def _(v, g8=g8):
                self.w(v, "t")
                dst = out_tile[0:16, 0:8 * m].rearrange("r (c g) -> r c g", g=8)[:, :, g8]
                self.inc(v.tensor_copy(dst, self.pM[0:16, 0:m]), "v")

        @blk.gpsimd
        def _(g):
            self.w(g, "v")
            for r in range(1, replicate):
                self.inc(g.dma_start(out=out_tile[16 * r:16 * r + 16, 0:8 * m],
                                     in_=out_tile[0:16, 0:8 * m]), "d_wr", 16)
            self.w(g, "d_wr")

    def ln_tile(self, s, v, src_f32, dst_bf16):
        raise RuntimeError("inline in phases")

    def p1_ln(self, stack):
        blk = self.block
        self.xn = self.sbctx(stack, "xn", (128, NT, D), BF16)
        xb = [self.sbctx(stack, f"p1x{i}", (128, D)) for i in range(2)]

        @blk.sync
        def _(q):
            self.w(q, "d_c")
            for j in range(2):
                self.inc(q.dma_start(out=xb[j][:],
                                     in_=self.x_in.rearrange("(c p) d -> p c d", p=128)[:, j, :]),
                         f"d_x{j % 2}", 16)

        for j in range(NT):
            xt = xb[j % 2]

            @blk.vector
            def _(v, j=j, xt=xt):
                self.w(v, f"d_x{j % 2}")
                self.incw(v, v.tensor_scalar(out=self.snk_v[:, 0:D], in0=xt[:], scalar1=1.0,
                                         scalar2=0.0, op0=OP.mult, op1=OP.add,
                                         accum_out=self.ln_a[:, 0:1]), "v")

            @blk.scalar
            def _(s, j=j, xt=xt):
                self.w(s, f"d_x{j % 2}")
                self.inc(s.activation(out=self.snk[:, 0:D], in_=xt[:], func=AF.Square,
                                      bias=self.cc1("zero"), scale=self.cc1("one"),
                                      accum_out=self.ln_a[:, 1:2]), "s")

            @blk.vector
            def _(v, j=j):
                self.w(v, "s")
                self.w(v, "v")
                la = self.ln_a
                self.incw(v, v.tensor_scalar(out=la[:, 2:3], in0=la[:, 0:1], scalar1=1.0 / D,
                                         scalar2=None, op0=OP.mult), "v")
                self.incw(v, v.tensor_scalar(out=la[:, 3:4], in0=la[:, 1:2], scalar1=1.0 / D,
                                         scalar2=None, op0=OP.mult), "v")
                self.incw(v, v.tensor_tensor(out=la[:, 4:5], in0=la[:, 2:3], in1=la[:, 2:3],
                                         op=OP.mult), "v")
                self.incw(v, v.tensor_tensor(out=la[:, 5:6], in0=la[:, 3:4], in1=la[:, 4:5],
                                         op=OP.subtract), "v")

            @blk.scalar
            def _(s, j=j):
                self.w(s, "v")
                self.inc(s.activation(out=self.ln_a[:, 6:7], in_=self.ln_a[:, 5:6],
                                      func=AF.Ln, bias=self.cc1("eps"),
                                      scale=self.cc1("one")), "s")
                self.w(s, "s")
                self.inc(s.activation(out=self.ln_a[:, 6:7], in_=self.ln_a[:, 6:7],
                                      func=AF.Exp, bias=self.cc1("zero"),
                                      scale=self.cc1("half")), "s")

            @blk.vector
            def _(v, j=j, xt=xt):
                self.w(v, "s")
                self.incw(v, v.reciprocal(self.ln_a[:, 7:8], self.ln_a[:, 6:7]), "v")
                self.w(v, "v")
                self.incw(v, v.tensor_scalar(out=self.xn[:, j, :], in0=xt[:],
                                         scalar1=self.ln_a[:, 2:3], scalar2=self.ln_a[:, 7:8],
                                         op0=OP.subtract, op1=OP.mult), "v")

            if j + 2 < NT:
                @blk.sync
                def _(q, j=j, xt=xt):
                    self.w(q, "v")
                    self.inc(q.dma_start(
                        out=xt[:],
                        in_=self.x_in.rearrange("(c p) d -> p c d", p=128)[:, j + 2, :]),
                        f"d_x{j % 2}", 16)

            @blk.tensor
            def _(t, j=j):
                self.w(t, "v")
                for dc in range(8):
                    self.inc(t.transpose(self.pT1[:, dc * 128:(dc + 1) * 128],
                                         self.xn[:, j, dc * 128:(dc + 1) * 128],
                                         self.t_identb), "t")

            @blk.vector
            def _(v, j=j):
                self.w(v, "t")
                for dc in range(8):
                    self.incw(v, v.tensor_copy(self.xnT[:, dc, j * 128:(j + 1) * 128],
                                           self.pT1[:, dc * 128:(dc + 1) * 128]), "v")

        if self.dbg("xnT"):
            dstT = self.dbg_out("xnT", (128, 8, S))

            @blk.gpsimd
            def _(g):
                self.w(g, "v")
                self.inc(g.dma_start(out=dstT[:, :, :], in_=self.xnT[:, :, :]), "d_dbgg", 16)

        if self.dbg("xn"):
            dst = self.dbg_out("xn", (128, NT, D))

            @blk.gpsimd
            def _(g):
                self.w(g, "v")
                self.inc(g.dma_start(out=dst[:, :, :], in_=self.xn[:, :, :]), "d_dbgg", 16)

    def p2_gate(self, stack):
        blk = self.block
        gtile = self.sbctx(stack, "gtile", (128, 4))
        gdbg = self.sbctx(stack, "gdbg", (128, NT * 4)) if self.dbg("gate") else None
        ftile = self.sbctx(stack, "ftile", (128, NT, 4))
        stile = self.sbctx(stack, "stile", (128, NT, 4))
        maskT = self.sbctx(stack, "maskT", (4, S))
        slotT = self.sbctx(stack, "slotT", (4, S))
        col = self.sccol

        for j in range(NT):
            @blk.tensor
            def _(t, j=j):
                self.w(t, "v")
                self.w(t, "d_w", 32)
                for dc in range(8):
                    self.inc(t.matmul(self.pW[0][:, 0:4], self.xnT[:, dc, j * 128:(j + 1) * 128],
                                      self.wgate[:, dc, :], start=(dc == 0), stop=(dc == 7)), "t")

            @blk.vector
            def _(v, j=j):
                self.w(v, "t")
                G = gtile
                self.incw(v, v.tensor_copy(G[:], self.pW[0][:, 0:4]), "v")
                if gdbg is not None:
                    self.incw(v, v.tensor_copy(gdbg[:, 4 * j:4 * j + 4], self.pW[0][:, 0:4]), "v")
                self.incw(v, v.tensor_reduce(out=col[:, 0:1], in_=G[:], axis=AX.X, op=OP.max), "v")
                self.w(v, "v")
                e1 = self.sc3[:, 0:4]
                self.incw(v, v.tensor_scalar(out=e1, in0=G[:], scalar1=col[:, 0:1],
                                         scalar2=None, op0=OP.is_equal), "v")
                c1 = self.sc4[:, 0:4]
                self.incw(v, v.tensor_tensor_scan(out=c1, data0=self.t_onesw[:, 0:4], data1=e1,
                                              initial=0.0, op0=OP.mult, op1=OP.add), "v")
                t1 = self.sc5[:, 0:4]
                self.incw(v, v.tensor_scalar(out=t1, in0=c1, scalar1=1.0, scalar2=None,
                                         op0=OP.is_equal), "v")
                f = ftile[:, j, :]
                self.incw(v, v.tensor_tensor(out=f, in0=e1, in1=t1, op=OP.mult), "v")
                Gm = self.sc6[:, 0:4]
                self.incw(v, v.scalar_tensor_tensor(out=Gm, in0=f, scalar=-1e30, in1=G[:],
                                                op0=OP.mult, op1=OP.add), "v")
                self.incw(v, v.tensor_reduce(out=col[:, 1:2], in_=Gm, axis=AX.X, op=OP.max), "v")
                self.w(v, "v")
                self.incw(v, v.tensor_scalar(out=e1, in0=Gm, scalar1=col[:, 1:2], scalar2=None,
                                         op0=OP.is_equal), "v")
                self.incw(v, v.tensor_tensor_scan(out=c1, data0=self.t_onesw[:, 0:4], data1=e1,
                                              initial=0.0, op0=OP.mult, op1=OP.add), "v")
                self.incw(v, v.tensor_scalar(out=t1, in0=c1, scalar1=1.0, scalar2=None,
                                         op0=OP.is_equal), "v")
                sflag = stile[:, j, :]
                self.incw(v, v.tensor_tensor(out=sflag, in0=e1, in1=t1, op=OP.mult), "v")
                self.incw(v, v.tensor_tensor(out=col[:, 2:3], in0=col[:, 0:1], in1=col[:, 1:2],
                                         op=OP.subtract), "v")
                self.incw(v, v.tensor_tensor(out=self.sc3[:, 0:4], in0=f, in1=sflag, op=OP.add), "v")

            @blk.vector
            def _(v, j=j):
                self.incw(v, v.tensor_copy(self.rwd[:, j:j + 1], col[:, 2:3]), "v")

            @blk.tensor
            def _(t, j=j):
                self.w(t, "v")
                self.inc(t.transpose(self.pM[0:4, 0:128], self.sc3[:, 0:4], self.t_identf), "t")

            @blk.vector
            def _(v, j=j):
                self.w(v, "t")
                self.incw(v, v.tensor_copy(maskT[:, j * 128:(j + 1) * 128], self.pM[0:4, 0:128]), "v")

        @blk.scalar
        def _(s):
            self.w(s, "v")
            self.inc(s.activation(out=self.rwd[:, NT:2 * NT], in_=self.rwd[:, 0:NT],
                                  func=AF.Exp, bias=self.cc1("zero"),
                                  scale=self.cc1("negone")), "s")

        @blk.vector
        def _(v):
            self.w(v, "s")
            self.incw(v, v.tensor_scalar(out=self.rwd[:, NT:2 * NT], in0=self.rwd[:, NT:2 * NT],
                                     scalar1=1.0, scalar2=None, op0=OP.add), "v")
            self.incw(v, v.reciprocal(self.rwd[:, NT:2 * NT], self.rwd[:, NT:2 * NT]), "v")
            for j in range(NT):
                self.incw(v, v.tensor_copy(self.rw[:, j, 0:1], self.rwd[:, NT + j:NT + j + 1]), "v")
                self.incw(v, v.tensor_scalar(out=self.rw[:, j, 1:2], in0=self.rw[:, j, 0:1],
                                         scalar1=-1.0, scalar2=1.0, op0=OP.mult, op1=OP.add), "v")

        @blk.vector
        def _(v):
            self.incw(v, v.tensor_tensor_scan(out=slotT[:], data0=self.t_onesw[0:4, :],
                                          data1=maskT[:], initial=0.0,
                                          op0=OP.mult, op1=OP.add), "v")
            self.incw(v, v.tensor_scalar(out=col[0:4, 8:9], in0=slotT[:, S - 1:S], scalar1=-1.0,
                                     scalar2=float(CAP - 1), op0=OP.mult, op1=OP.add), "v")
            self.w(v, "v")
            self.incw(v, v.tensor_scalar(out=slotT[:], in0=slotT[:], scalar1=col[0:4, 8:9],
                                     scalar2=None, op0=OP.add), "v")

        for j in range(NT):
            @blk.tensor
            def _(t, j=j):
                self.w(t, "v")
                self.inc(t.transpose(self.pM[0:128, 0:4], slotT[:, j * 128:(j + 1) * 128],
                                     self.t_identf[0:4, 0:4]), "t")

            @blk.vector
            def _(v, j=j):
                self.w(v, "t")
                st4 = self.sc4[:, 0:4]
                self.incw(v, v.tensor_copy(st4, self.pM[0:128, 0:4]), "v")
                for c, ftl in ((0, ftile), (1, stile)):
                    fj = ftl[:, j, :]
                    self.incw(v, v.tensor_tensor(out=self.sc5[:, 0:4], in0=fj, in1=self.t_iota4,
                                             op=OP.mult), "v")
                    self.incw(v, v.tensor_reduce(out=col[:, 3:4], in_=self.sc5[:, 0:4],
                                             axis=AX.X, op=OP.add), "v")
                    self.incw(v, v.tensor_tensor(out=self.sc5[:, 0:4], in0=fj, in1=st4,
                                             op=OP.mult), "v")
                    self.incw(v, v.tensor_reduce(out=col[:, 4:5], in_=self.sc5[:, 0:4],
                                             axis=AX.X, op=OP.add), "v")
                    self.incw(v, v.tensor_scalar(out=col[:, 5:6], in0=col[:, 4:5], scalar1=0.0,
                                             scalar2=None, op0=OP.is_ge), "v")
                    self.incw(v, v.scalar_tensor_tensor(out=col[:, 6:7], in0=col[:, 3:4],
                                                    scalar=float(CAP), in1=col[:, 4:5],
                                                    op0=OP.mult, op1=OP.add), "v")
                    self.incw(v, v.tensor_scalar(out=col[:, 6:7], in0=col[:, 6:7], scalar1=-4096.0,
                                             scalar2=None, op0=OP.add), "v")
                    self.incw(v, v.tensor_tensor(out=col[:, 6:7], in0=col[:, 6:7], in1=col[:, 5:6],
                                             op=OP.mult), "v")
                    self.incw(v, v.tensor_scalar(out=self.disp_idx[:, c * NT + j:c * NT + j + 1],
                                             in0=col[:, 6:7], scalar1=4096.0,
                                             scalar2=None, op0=OP.add), "v")

        if self.dbg("fs"):
            dstf = self.dbg_out("fs", (128, NT, 4))
            dsts = self.dbg_out("ss", (128, NT, 4))
            dstm = self.dbg_out("maskT", (4, S))
            dstsl = self.dbg_out("slotT", (4, S))

            @blk.sync
            def _(q):
                self.w(q, "v")
                self.inc(q.dma_start(out=dstf[:, :, :], in_=ftile[:, :, :]), "d_dbg", 16)
                self.inc(q.dma_start(out=dsts[:, :, :], in_=stile[:, :, :]), "d_dbg", 16)
                self.inc(q.dma_start(out=dstm[:, :], in_=maskT[:, :]), "d_dbg", 16)
                self.inc(q.dma_start(out=dstsl[:, :], in_=slotT[:, :]), "d_dbg", 16)

        if gdbg is not None:
            dstg = self.dbg_out("gate", (128, NT * 4))

            @blk.sync
            def _(q):
                self.w(q, "v")
                self.inc(q.dma_start(out=dstg[:, :], in_=gdbg[:, :]), "d_dbg", 16)

        self.wrap_build(S, self.disp_idx[:, 0:NT], self.wrap0)
        self.wrap_build(S, self.disp_idx[:, NT:2 * NT], self.wrap1)

        if self.dbg("disp"):
            dst = self.dbg_out("disp", (128, 2 * NT))

            @blk.sync
            def _(q):
                self.w(q, "v")
                self.inc(q.dma_start(out=dst[:, :], in_=self.disp_idx[:, :]), "d_dbg", 16)
        if self.dbg("rw"):
            dst = self.dbg_out("rw", (128, NT, 2))

            @blk.sync
            def _(q):
                self.w(q, "v")
                self.inc(q.dma_start(out=dst[:, :, :], in_=self.rw[:, :, :]), "d_dbg", 16)

        @blk.gpsimd
        def _(g):
            self.w(g, "v")
            self.w(g, "d_z")
            for wt in (self.wrap0, self.wrap1):
                self.inc(g.dma_scatter_add(out_ap=self.tx_hbm[:, :], in_ap=self.xn[:, :, :],
                                           idxs_ap=wt[:, :], num_idxs=S, num_idxs_reg=S,
                                           elem_size=D, single_packet=False), "d_g", 16)
                self.w(g, "d_g")

    def gla_group(self, stack, gi, tiles):
        blk = self.block
        routed = gi < 4
        cap = CAP if routed else S
        nmh = cap // 1024
        (txT, vsb, u2m, qfh, kfh, lamt, emt, lpt, tmpT, cs_c, eCf, Ssb,
         k2c4, amt4, k2t4, wkb, wvb, wg1b, wg2b) = tiles

        @blk.gpsimd
        def _(g):
            self.w(g, "t")
            self.w(g, "v")
            if routed:
                self.inc(g.dma_start(out=wkb[:],
                                     in_=self.w_k_in[gi].rearrange("(c p) o -> p c o", p=128)),
                         "d_wm", 16)
                self.inc(g.dma_start(out=wvb[:],
                                     in_=self.w_v_in[gi].rearrange("(c p) o -> p c o", p=128)),
                         "d_wm", 16)
                self.inc(g.dma_start(out=wg1b[:],
                                     in_=self.w_gk1_in[gi].rearrange("(c p) o -> p c o", p=128)),
                         "d_wm", 16)
                self.inc(g.dma_start(out=wg2b[0:16, :], in_=self.w_gk2_in[gi][:, :]),
                         "d_wm", 16)
            else:
                self.inc(g.dma_start(out=wkb[:],
                                     in_=self.w_ks_in.rearrange("(c p) o -> p c o", p=128)),
                         "d_wm", 16)
                self.inc(g.dma_start(out=wvb[:],
                                     in_=self.w_vs_in.rearrange("(c p) o -> p c o", p=128)),
                         "d_wm", 16)
                self.inc(g.dma_start(out=wg1b[:],
                                     in_=self.w_gk1s_in.rearrange("(c p) o -> p c o", p=128)),
                         "d_wm", 16)
                self.inc(g.dma_start(out=wg2b[0:16, :], in_=self.w_gk2s_in[:, :]),
                         "d_wm", 16)

        wm_sem, wm_tok = "d_wm", self.cnt["d_wm"]
        bcol = (lambda h: self.t_bgk[:, gi * H + h:gi * H + h + 1]) if routed else                (lambda h: self.t_bgks[:, h:h + 1])

        if routed:
            @blk.sync
            def _(q):
                self.w(q, "d_g")
                self.w(q, "v")
                self.w(q, "d_om")
                self.w(q, "d_sh")
                self.inc(q.dma_start(
                    out=u2m[:, :, :],
                    in_=self.tx_hbm[gi * CAP:(gi + 1) * CAP, :].rearrange(
                        "(c p) d -> p c d", p=128)), "d_tx", 16)

            for tb in range(8):
                @blk.tensor
                def _(t, tb=tb):
                    self.w(t, "d_tx")
                    self.w(t, "v")
                    for dc in range(8):
                        self.inc(t.transpose(self.pT1[:, dc * 128:(dc + 1) * 128],
                                             u2m[:, tb, dc * 128:(dc + 1) * 128],
                                             self.t_identb), "t")

                @blk.vector
                def _(v, tb=tb):
                    self.w(v, "t")
                    for dc in range(8):
                        self.inc(v.tensor_copy(txT[:, dc, tb * 128:(tb + 1) * 128],
                                               self.pT1[:, dc * 128:(dc + 1) * 128]), "v")

        @blk.vector
        def _(v):
            for h in range(H):
                self.inc(v.memset(Ssb[h][:], 0.0), "v")

        for mh in range(nmh):
            def srcT(dc, sl):
                if routed:
                    return txT[:, dc, sl]
                base = mh * 1024
                return self.xnT[:, dc, base + sl.start: base + sl.stop]

            for tb in range(8):
                @blk.tensor
                def _(t, tb=tb, mh=mh):
                    self.w(t, "v")
                    self.w(t, "s")
                    self.w(t, wm_sem, wm_tok)
                    for dvh in range(2):
                        pb = self.pW[dvh]
                        for dc in range(8):
                            self.inc(t.matmul(pb[:, 0:512],
                                              srcT(dc, slice(tb * 128, (tb + 1) * 128)),
                                              wvb[:, dc, dvh * 512:(dvh + 1) * 512],
                                              start=(dc == 0), stop=(dc == 7)), "t")

                @blk.vector
                def _(v, tb=tb):
                    self.w(v, "t")
                    self.inc(v.tensor_copy(vsb[:, tb, 0:512], self.pW[0][:, 0:512]), "v")

                @blk.scalar
                def _(s, tb=tb):
                    self.w(s, "t")
                    self.inc(s.activation(out=vsb[:, tb, 512:1024], in_=self.pW[1][:, 0:512],
                                          func=AF.Copy, bias=0.0, scale=1.0), "s")

            for h in range(H):
                for hf in range(2):
                    @blk.tensor
                    def _(t, h=h, hf=hf, mh=mh):
                        self.w(t, "v")
                        self.w(t, "s")
                        for dc in range(8):
                            self.inc(t.matmul(self.pW[0][0:16, 0:512],
                                              wg1b[:, dc, :],
                                              srcT(dc, slice(hf * 512, (hf + 1) * 512)),
                                              start=(dc == 0), stop=(dc == 7)), "t")

                    @blk.scalar
                    def _(s, h=h, hf=hf):
                        self.w(s, "t")
                        self.inc(s.activation(out=tmpT[0:16, hf * 512:(hf + 1) * 512],
                                              in_=self.pW[0][0:16, 0:512], func=AF.Copy, bias=0.0, scale=1.0), "s")

                    @blk.tensor
                    def _(t, h=h, hf=hf):
                        self.w(t, "s")
                        self.inc(t.matmul(self.pW[1][:, 0:512],
                                          wg2b[0:16, h * 128:(h + 1) * 128],
                                          tmpT[0:16, hf * 512:(hf + 1) * 512],
                                          start=True, stop=True), "t")

                    @blk.scalar
                    def _(s, h=h, hf=hf):
                        self.w(s, "t")
                        self.inc(s.activation(out=self.sc1[:, 0:512], in_=self.pW[1][:, 0:512],
                                              func=AF.Exp, bias=bcol(h),
                                              scale=self.cc1("negone")), "s")
                        self.w(s, "s")
                        self.inc(s.activation(out=lpt[:, hf * 512:(hf + 1) * 512],
                                              in_=self.sc1[:, 0:512], func=AF.Ln,
                                              bias=self.cc1("one"), scale=self.cc1("one")), "s")

                @blk.vector
                def _(v, h=h):
                    self.w(v, "s")
                    for j in range(8):
                        self.inc(v.tensor_tensor_scan(
                            out=cs_c[:, j * 128:(j + 1) * 128],
                            data0=self.t_onesw[:, 0:128],
                            data1=lpt[:, j * 128:(j + 1) * 128],
                            initial=0.0, op0=OP.mult, op1=OP.add), "v")

                @blk.scalar
                def _(s, h=h):
                    self.w(s, "v")
                    self.inc(s.activation(out=lamt[:], in_=cs_c[:, 0:1024], func=AF.Exp,
                                          bias=self.cc1("lnqs"), scale=self.cc1("neg16")), "s")
                    self.inc(s.activation(out=emt[:], in_=cs_c[:, 0:1024], func=AF.Exp,
                                          bias=self.cc1("zero"), scale=self.cc1("inv16")), "s")
                    for j in range(8):
                        self.inc(s.activation(out=eCf[:, h * 8 + j:h * 8 + j + 1],
                                              in_=cs_c[:, j * 128 + 127:j * 128 + 128],
                                              func=AF.Exp, bias=self.cc1("zero"),
                                              scale=self.cc1("neg16")), "s")

                for hf in range(2):
                    @blk.tensor
                    def _(t, h=h, hf=hf, mh=mh):
                        self.w(t, "s")
                        self.w(t, "v")
                        for dc in range(8):
                            self.inc(t.matmul(self.pW[0][:, 0:512],
                                              self.wq[:, dc, h * 128:(h + 1) * 128],
                                              srcT(dc, slice(hf * 512, (hf + 1) * 512)),
                                              start=(dc == 0), stop=(dc == 7)), "t")
                        for dc in range(8):
                            self.inc(t.matmul(self.pW[1][:, 0:512],
                                              wkb[:, dc, h * 128:(h + 1) * 128],
                                              srcT(dc, slice(hf * 512, (hf + 1) * 512)),
                                              start=(dc == 0), stop=(dc == 7)), "t")

                    @blk.vector
                    def _(v, h=h, hf=hf):
                        self.w(v, "t")
                        sl = slice(hf * 512, (hf + 1) * 512)
                        self.inc(v.tensor_tensor(out=qfh[h][:, sl], in0=self.pW[0][:, 0:512],
                                                 in1=lamt[:, sl], op=OP.mult), "v")
                        self.inc(v.tensor_tensor(out=kfh[h][:, sl], in0=self.pW[1][:, 0:512],
                                                 in1=emt[:, sl], op=OP.mult), "v")

            for jl in range(8):
                jb = slice(jl * 128, (jl + 1) * 128)
                jg = mh * 8 + jl

                @blk.tensor
                def _(t, jl=jl, jb=jb):
                    self.w(t, "v")
                    for h in range(H):
                        self.inc(t.matmul(self.pA[:, h * 128:(h + 1) * 128],
                                          kfh[h][:, jb], qfh[h][:, jb],
                                          start=True, stop=True), "t")

                @blk.vector
                def _(v, jl=jl, jb=jb):
                    self.w(v, "t")
                    for h in range(H):
                        self.inc(v.tensor_tensor(out=amt4[:, h * 128:(h + 1) * 128],
                                                 in0=self.pA[:, h * 128:(h + 1) * 128],
                                                 in1=self.t_triu, op=OP.mult), "v")
                        self.inc(v.tensor_scalar(out=k2c4[:, h * 128:(h + 1) * 128],
                                                 in0=kfh[h][:, jb],
                                                 scalar1=eCf[:, h * 8 + jl:h * 8 + jl + 1],
                                                 scalar2=None, op0=OP.mult), "v")

                @blk.tensor
                def _(t, jl=jl, jb=jb, jg=jg, mh=mh):
                    self.w(t, "v")
                    for h in range(H):
                        self.inc(t.transpose(self.pT1[:, h * 128:(h + 1) * 128],
                                             k2c4[:, h * 128:(h + 1) * 128], self.t_identb), "t")
                    for h in range(H):
                        pOb = self.pO if h < 2 else self.pD
                        po = pOb[:, (h % 2) * 256:(h % 2) * 256 + 256]
                        first = (jg == 0)
                        self.inc(t.matmul(po, amt4[:, h * 128:(h + 1) * 128],
                                          vsb[:, jl, h * 256:(h + 1) * 256],
                                          start=True, stop=first), "t")
                        if not first:
                            self.inc(t.matmul(po, qfh[h][:, jb], Ssb[h][:],
                                              start=False, stop=True), "t")

                @blk.vector
                def _(v, jl=jl):
                    self.w(v, "t")
                    for h in range(H):
                        self.inc(v.tensor_copy(k2t4[:, h * 128:(h + 1) * 128],
                                               self.pT1[:, h * 128:(h + 1) * 128]), "v")

                @blk.tensor
                def _(t, jl=jl):
                    self.w(t, "v")
                    for h in range(H):
                        pDb = self.pW[0] if h < 2 else self.pW[1]
                        self.inc(t.matmul(pDb[:, (h % 2) * 256:(h % 2) * 256 + 256],
                                          k2t4[:, h * 128:(h + 1) * 128],
                                          vsb[:, jl, h * 256:(h + 1) * 256],
                                          start=True, stop=True), "t")

                @blk.vector
                def _(v, jl=jl, jg=jg):
                    self.w(v, "t")
                    if jl == 0:
                        self.w(v, "d_om")
                        self.w(v, "d_sh")
                    for h in range(H):
                        pOb = self.pO if h < 2 else self.pD
                        po = pOb[:, (h % 2) * 256:(h % 2) * 256 + 256]
                        self.inc(v.tensor_copy(u2m[:, jl, h * 256:(h + 1) * 256], po), "v")
                    for h in range(H):
                        pDb = self.pW[0] if h < 2 else self.pW[1]
                        self.inc(v.scalar_tensor_tensor(
                            out=Ssb[h][:], in0=Ssb[h][:],
                            scalar=eCf[:, h * 8 + jl:h * 8 + jl + 1],
                            in1=pDb[:, (h % 2) * 256:(h % 2) * 256 + 256],
                            op0=OP.mult, op1=OP.add), "v")

            @blk.sync
            def _(q, mh=mh):
                self.w(q, "v")
                if routed:
                    dst = self.om_hbm[gi * CAP:(gi + 1) * CAP, :]
                    self.inc(q.dma_start(out=dst.rearrange("(c p) d -> p c d", p=128),
                                         in_=u2m[:, :, :]), "d_om", 16)
                else:
                    dst = self.sh_hbm[mh * 1024:(mh + 1) * 1024, :]
                    self.inc(q.dma_start(out=dst.rearrange("(c p) d -> p c d", p=128),
                                         in_=u2m[:, :, :]), "d_sh", 16)

    def p_gla(self, stack):
        blk = self.block
        txT = self.sbctx(stack, "txT", (128, 8, CAP), BF16)
        vsb = self.sbctx(stack, "vsb", (128, 8, DV), BF16)
        u2m = self.sbctx(stack, "u2m", (128, 8, D), BF16)
        qfh = [self.sbctx(stack, f"qfh{h}", (128, 1024), BF16) for h in range(H)]
        kfh = [self.sbctx(stack, f"kfh{h}", (128, 1024), BF16) for h in range(H)]
        emt = self.sbctx(stack, "emt", (128, 1024), BF16)
        lpt = self.sbctx(stack, "lpt", (128, 1024), BF16)
        lamt = lpt
        tmpT = self.sbctx(stack, "tmpT", (16, 1024), BF16)
        cs_c = self.sbctx(stack, "cs_c", (128, 1024))
        eCf = self.sbctx(stack, "eCf", (128, H * 8))
        Ssb = [self.sbctx(stack, f"Ssb{h}", (128, HDV), BF16) for h in range(H)]
        k2c4 = self.sbctx(stack, "k2c4", (128, 512), BF16)
        amt4 = self.sbctx(stack, "amt4", (128, 512), BF16)
        k2t4 = self.sbctx(stack, "k2t4", (128, 512), BF16)
        wkb = self.sbctx(stack, "wkb", (128, 8, DK), BF16)
        wvb = self.sbctx(stack, "wvb", (128, 8, DV), BF16)
        wg1b = self.sbctx(stack, "wg1b", (128, 8, LOW), BF16)
        wg2b = self.sbctx(stack, "wg2b", (16, DK), BF16)
        tiles = (txT, vsb, u2m, qfh, kfh, lamt, emt, lpt, tmpT, cs_c, eCf, Ssb,
                 k2c4, amt4, k2t4, wkb, wvb, wg1b, wg2b)

        @blk.vector
        def _(v):
            self.w(v, "d_g")

        @blk.scalar
        def _(s):
            self.w(s, "d_g")

        for oi, gi in enumerate(self.gla_order):
            self.gla_group(stack, gi, tiles)
            if oi == 0 and self.upto >= 10:
                self.p0b_precast_w1()

        if self.dbg("om"):
            pass

    def p6_reco(self, stack):
        blk = self.block
        self.sc2 = self.sbctx(stack, "sc2", (128, 1024))
        gat0 = self.sbctx(stack, "gat0", (128, 8, DV), BF16)
        gat1 = self.sbctx(stack, "gat1", (128, 8, DV), BF16)
        shb = self.sbctx(stack, "shb", (128, 8, DV), BF16)
        wgt = self.sbctx(stack, "wgt", (128, 8, DV), BF16)
        sg = self.sbctx(stack, "sg", (128, DV))

        @blk.gpsimd
        def _(g):
            self.w(g, "d_wm")
            self.w(g, "t")
            self.w(g, "v")
            self.w(g, "s")
            self.inc(g.dma_start(out=wgt[:],
                                 in_=self.w_g_in.rearrange("(c p) o -> p c o", p=128)),
                     "d_wg", 16)

        wg_tok = self.cnt["d_wg"]

        for hf in range(2):
            @blk.gpsimd
            def _(g, hf=hf):
                self.w(g, "d_om")
                self.w(g, "d_z")
                self.w(g, "s")
                self.w(g, "v")
                for c, (wt, gat) in enumerate(((self.wrap0, gat0), (self.wrap1, gat1))):
                    self.inc(g.dma_gather(out_ap=gat[:, :, :], in_ap=self.om_hbm[:, :],
                                          idxs_ap=wt[:, 64 * hf:64 * hf + 64],
                                          num_idxs=1024, num_idxs_reg=1024,
                                          elem_size=D, single_packet=False), "d_g", 16)
                self.w(g, "d_g")

            @blk.sync
            def _(q, hf=hf):
                self.w(q, "d_sh")
                self.w(q, "v")
                self.inc(q.dma_start(
                    out=shb[:, :, :],
                    in_=self.sh_hbm[hf * 1024:(hf + 1) * 1024, :].rearrange(
                        "(c p) d -> p c d", p=128)), "d_tx", 16)

            for jg in range(8):
                j = hf * 8 + jg

                @blk.scalar
                def _(s, j=j, jg=jg):
                    self.w(s, "d_g")
                    self.w(s, "v")
                    self.inc(s.activation(out=self.sc1[:, 0:DV], in_=gat0[:, jg, :],
                                          func=AF.Copy, bias=0.0,
                                          scale=self.rw[:, j, 0:1]), "s")

                @blk.vector
                def _(v, j=j, jg=jg):
                    self.w(v, "s")
                    self.w(v, "d_tx")
                    self.incw(v, v.scalar_tensor_tensor(out=self.sc2[:, 0:DV], in0=gat1[:, jg, :],
                                                    scalar=self.rw[:, j, 1:2],
                                                    in1=self.sc1[:, 0:DV],
                                                    op0=OP.mult, op1=OP.add), "v")
                    self.incw(v, v.tensor_tensor(out=self.sc1[:, 0:DV], in0=self.sc2[:, 0:DV],
                                             in1=shb[:, jg, :], op=OP.add), "v")

                @blk.scalar
                def _(s, j=j):
                    self.w(s, "v")
                    for h in range(H):
                        self.inc(s.activation(out=self.snk[:, h * HDV:(h + 1) * HDV],
                                              in_=self.sc1[:, h * HDV:(h + 1) * HDV],
                                              func=AF.Square, bias=self.cc1("zero"),
                                              scale=self.cc1("one"),
                                              accum_out=self.sccol[:, 8 + h:9 + h]), "s")

                @blk.vector
                def _(v, j=j):
                    self.w(v, "s")
                    self.incw(v, v.tensor_scalar(out=self.sccol[:, 12:16],
                                             in0=self.sccol[:, 8:12], scalar1=1.0 / HDV,
                                             scalar2=EPS, op0=OP.mult, op1=OP.add), "v")

                @blk.scalar
                def _(s, j=j):
                    self.w(s, "v")
                    self.inc(s.activation(out=self.sccol[:, 8:12], in_=self.sccol[:, 12:16],
                                          func=AF.Ln, bias=self.cc1("zero"),
                                          scale=self.cc1("one")), "s")
                    self.w(s, "s")
                    self.inc(s.activation(out=self.sccol[:, 8:12], in_=self.sccol[:, 8:12],
                                          func=AF.Exp, bias=self.cc1("zero"),
                                          scale=self.cc1("half")), "s")

                @blk.vector
                def _(v, j=j):
                    self.w(v, "s")
                    self.incw(v, v.reciprocal(self.sccol[:, 12:16], self.sccol[:, 8:12]), "v")
                    self.w(v, "v")
                    for h in range(H):
                        sl = slice(h * HDV, (h + 1) * HDV)
                        self.incw(v, v.scalar_tensor_tensor(out=self.sc2[:, sl],
                                                        in0=self.sc1[:, sl],
                                                        scalar=self.sccol[:, 12 + h:13 + h],
                                                        in1=self.t_normw[:, sl],
                                                        op0=OP.mult, op1=OP.mult), "v")

                @blk.tensor
                def _(t, j=j):
                    self.w(t, "v")
                    self.w(t, "d_wg", wg_tok)
                    for dvh in range(2):
                        for dc in range(8):
                            self.inc(t.matmul(self.pW[dvh][:, 0:512],
                                              self.xnT[:, dc, j * 128:(j + 1) * 128],
                                              wgt[:, dc, dvh * 512:(dvh + 1) * 512],
                                              start=(dc == 0), stop=(dc == 7)), "t")

                @blk.scalar
                def _(s, j=j):
                    self.w(s, "t")
                    for dvh in range(2):
                        self.inc(s.activation(out=sg[:, dvh * 512:(dvh + 1) * 512],
                                              in_=self.pW[dvh][:, 0:512], func=AF.Exp,
                                              bias=self.cc1("zero"),
                                              scale=self.cc1("negone")), "s")
                        self.inc(s.activation(out=self.snk[:, dvh * 512:(dvh + 1) * 512],
                                              in_=self.pW[dvh][:, 0:512], func=AF.Copy,
                                              bias=0.0, scale=1.0), "s")

                @blk.vector
                def _(v, j=j):
                    self.w(v, "s")
                    self.incw(v, v.tensor_scalar(out=sg[:], in0=sg[:], scalar1=1.0,
                                             scalar2=None, op0=OP.add), "v")
                    self.incw(v, v.reciprocal(sg[:], sg[:]), "v")
                    self.incw(v, v.tensor_tensor(out=sg[:], in0=sg[:], in1=self.snk[:, 0:DV],
                                             op=OP.mult), "v")

                @blk.vector
                def _(v, j=j):
                    self.w(v, "s")
                    self.incw(v, v.tensor_tensor(out=self.og[:, j, :], in0=self.sc2[:, 0:DV],
                                             in1=sg[:], op=OP.mult), "v")

    def p7_out(self, stack):
        blk = self.block
        wot = self.sbctx(stack, "wot", (128, 8, D), BF16)
        ynT = self.sbctx(stack, "ynT", (128, 8, S), BF16)
        ogT = self.sbctx(stack, "ogT", (128, 8, 128), BF16)
        xb = [self.sbctx(stack, f"p7x{i}", (128, D)) for i in range(2)]
        yb = [self.sbctx(stack, f"p7y{i}", (128, D)) for i in range(2)]
        ynb = [self.sbctx(stack, f"p7n{i}", (128, D), BF16) for i in range(2)]
        pTl = self.sbctx(stack, "pTl", (NE, S))
        zr = self.sbctx(stack, "zr", (128, NE))
        er = self.sbctx(stack, "er", (128, NE))

        @blk.gpsimd
        def _(g):
            self.w(g, "d_wg")
            self.w(g, "t")
            self.w(g, "v")
            self.w(g, "s")
            self.inc(g.dma_start(out=wot[:],
                                 in_=self.w_o_in.rearrange("(c p) o -> p c o", p=128)),
                     "d_wo", 16)
            self.inc(g.dma_start(out=self.wWs[:],
                                 in_=self.Ws_in.rearrange("(c p) o -> p c o", p=128)),
                     "d_wo", 16)

        wo_tok = self.cnt["d_wo"]

        @blk.sync
        def _(q):
            self.w(q, "t")
            self.w(q, "v")
            self.w(q, "s")
            for j in range(2):
                self.inc(q.dma_start(out=xb[j][:],
                                     in_=self.x_in.rearrange("(c p) d -> p c d", p=128)[:, j, :]),
                         f"d_x{j % 2}", 16)

        for j in range(NT):
            xt = xb[j % 2]
            yt = yb[j % 2]
            ynt = ynb[j % 2]

            @blk.tensor
            def _(t, j=j):
                self.w(t, "v")
                self.w(t, "d_wo", wo_tok)
                for dc in range(8):
                    self.inc(t.transpose(self.pT1[:, dc * 128:(dc + 1) * 128],
                                         self.og[:, j, dc * 128:(dc + 1) * 128],
                                         self.t_identb), "t")

            @blk.vector
            def _(v, j=j):
                self.w(v, "t")
                for dc in range(8):
                    self.incw(v, v.tensor_copy(ogT[:, dc, :],
                                           self.pT1[:, dc * 128:(dc + 1) * 128]), "v")

            @blk.tensor
            def _(t, j=j):
                self.w(t, "v")
                for half in range(2):
                    for dc in range(8):
                        self.inc(t.matmul(self.pW[half][:, 0:512],
                                          ogT[:, dc, :],
                                          wot[:, dc, half * 512:(half + 1) * 512],
                                          start=(dc == 0), stop=(dc == 7)), "t")

            @blk.vector
            def _(v, j=j, xt=xt, yt=yt):
                self.w(v, "t")
                self.w(v, f"d_x{j % 2}")
                self.w(v, f"d_y{j % 2}")
                for half in range(2):
                    self.incw(v, v.tensor_tensor(out=yt[:, half * 512:(half + 1) * 512],
                                             in0=self.pW[half][:, 0:512],
                                             in1=xt[:, half * 512:(half + 1) * 512],
                                             op=OP.add), "v")
                self.incw(v, v.tensor_scalar(out=self.snk_v[:, 0:D], in0=yt[:], scalar1=1.0,
                                         scalar2=0.0, op0=OP.mult, op1=OP.add,
                                         accum_out=self.ln_a[:, 0:1]), "v")

            @blk.sync
            def _(q, j=j, yt=yt, xt=xt):
                self.w(q, "v")
                self.inc(q.dma_start(
                    out=self.out.rearrange("(c p) d -> p c d", p=128)[:, j, :],
                    in_=yt[:]), f"d_y{j % 2}", 16)
                if j + 2 < NT:
                    self.inc(q.dma_start(
                        out=xt[:],
                        in_=self.x_in.rearrange("(c p) d -> p c d", p=128)[:, j + 2, :]),
                        f"d_x{j % 2}", 16)

            @blk.scalar
            def _(s, j=j, yt=yt):
                self.w(s, "v")
                self.inc(s.activation(out=self.snk[:, 0:D], in_=yt[:], func=AF.Square,
                                      bias=self.cc1("zero"), scale=self.cc1("one"),
                                      accum_out=self.ln_a[:, 1:2]), "s")

            @blk.vector
            def _(v, j=j):
                self.w(v, "s")
                self.w(v, "v")
                la = self.ln_a
                self.incw(v, v.tensor_scalar(out=la[:, 2:3], in0=la[:, 0:1], scalar1=1.0 / D,
                                         scalar2=None, op0=OP.mult), "v")
                self.incw(v, v.tensor_scalar(out=la[:, 3:4], in0=la[:, 1:2], scalar1=1.0 / D,
                                         scalar2=None, op0=OP.mult), "v")
                self.incw(v, v.tensor_tensor(out=la[:, 4:5], in0=la[:, 2:3], in1=la[:, 2:3],
                                         op=OP.mult), "v")
                self.incw(v, v.tensor_tensor(out=la[:, 5:6], in0=la[:, 3:4], in1=la[:, 4:5],
                                         op=OP.subtract), "v")

            @blk.scalar
            def _(s, j=j):
                self.w(s, "v")
                self.inc(s.activation(out=self.ln_a[:, 6:7], in_=self.ln_a[:, 5:6],
                                      func=AF.Ln, bias=self.cc1("eps"),
                                      scale=self.cc1("one")), "s")
                self.w(s, "s")
                self.inc(s.activation(out=self.ln_a[:, 6:7], in_=self.ln_a[:, 6:7],
                                      func=AF.Exp, bias=self.cc1("zero"),
                                      scale=self.cc1("half")), "s")

            @blk.vector
            def _(v, j=j, yt=yt, ynt=ynt):
                self.w(v, "s")
                self.w(v, f"d_yn{j % 2}")
                self.incw(v, v.reciprocal(self.ln_a[:, 7:8], self.ln_a[:, 6:7]), "v")
                self.w(v, "v")
                self.incw(v, v.tensor_scalar(out=ynt[:], in0=yt[:],
                                         scalar1=self.ln_a[:, 2:3], scalar2=self.ln_a[:, 7:8],
                                         op0=OP.subtract, op1=OP.mult), "v")

            @blk.sync
            def _(q, j=j, ynt=ynt):
                self.w(q, "v")
                self.inc(q.dma_start(
                    out=self.yn_loc.rearrange("(c p) d -> p c d", p=128)[:, j, :],
                    in_=ynt[:]), f"d_yn{j % 2}", 16)

            @blk.tensor
            def _(t, j=j, ynt=ynt):
                self.w(t, "v")
                for dc in range(8):
                    self.inc(t.transpose(self.pT2[:, dc * 128:(dc + 1) * 128],
                                         ynt[:, dc * 128:(dc + 1) * 128],
                                         self.t_identb), "t")

            @blk.vector
            def _(v, j=j):
                self.w(v, "t")
                for dc in range(8):
                    self.incw(v, v.tensor_copy(ynT[:, dc, j * 128:(j + 1) * 128],
                                           self.pT2[:, dc * 128:(dc + 1) * 128]), "v")

            @blk.tensor
            def _(t, j=j):
                self.w(t, "v")
                for dc in range(8):
                    self.inc(t.matmul(self.pW[0][:, 0:8],
                                      ynT[:, dc, j * 128:(j + 1) * 128],
                                      self.wWs[:, dc, :], start=(dc == 0), stop=(dc == 7)), "t")

            @blk.vector
            def _(v, j=j):
                self.w(v, "t")
                self.incw(v, v.tensor_tensor(out=zr[:], in0=self.pW[0][:, 0:8],
                                         in1=self.t_bsbc, op=OP.add), "v")
                self.incw(v, v.tensor_reduce(out=self.sccol[:, 0:1], in_=zr[:], axis=AX.X,
                                         op=OP.max), "v")
                self.incw(v, v.tensor_scalar(out=self.sccol[:, 1:2], in0=self.sccol[:, 0:1],
                                         scalar1=-1.0, scalar2=None, op0=OP.mult), "v")

            @blk.scalar
            def _(s, j=j):
                self.w(s, "v")
                self.inc(s.activation(out=er[:], in_=zr[:], func=AF.Exp,
                                      bias=self.sccol[:, 1:2], scale=self.cc1("one"),
                                      accum_out=self.sccol[:, 2:3]), "s")

            @blk.vector
            def _(v, j=j):
                self.w(v, "s")
                self.incw(v, v.reciprocal(self.sccol[:, 3:4], self.sccol[:, 2:3]), "v")
                self.w(v, "v")
                self.incw(v, v.tensor_scalar(out=er[:], in0=er[:], scalar1=self.sccol[:, 3:4],
                                         scalar2=None, op0=OP.mult), "v")

            @blk.tensor
            def _(t, j=j):
                self.w(t, "v")
                self.inc(t.transpose(self.pM[0:8, 0:128], er[:], self.t_identf), "t")

            @blk.vector
            def _(v, j=j):
                self.w(v, "t")
                self.incw(v, v.tensor_copy(pTl[:, j * 128:(j + 1) * 128], self.pM[0:8, 0:128]), "v")

        @blk.sync
        def _(q):
            self.w(q, "v")
            self.inc(q.dma_start(out=self.pT_loc[:, :], in_=pTl[:]), "d_pt", 16)

        if self.dbg("y"):
            pass

    def p8_collect(self):
        blk = self.block

        @blk.gpsimd
        def _(g):
            self.w(g, "d_pt")
            self.inc(g.collective_compute(
                "AllGather", OP.bypass, replica_groups=[list(range(8))],
                ins=[self.pT_loc[:, :]], outs=[self.pT_all[:, :]]), "coll")
            self.w(g, "d_yn0")
            self.w(g, "d_yn1")
            self.inc(g.collective_compute(
                "AllGather", OP.bypass, replica_groups=[list(range(8))],
                ins=[self.yn_loc[:, :]], outs=[self.yn_all[:, :]]), "coll2")

    def p9_route(self, stack):
        blk = self.block
        pc = self.sbctx(stack, "pc", (128, 1, 128))
        self.wrapB = self.sbctx(stack, "wrapB", (128, NTOK // 16), I16)

        @blk.vector
        def _(v):
            self.incw(v, v.memset(self.wrapB[:], 0), "v")
        lo = self.sccol[:, 4:5]
        hi = self.sccol[:, 5:6]
        cntc = self.sccol[:, 6:7]
        gec = self.sccol[:, 7:8]
        idx8 = self.sbctx(stack, "idx8", (128, 8), I16)
        gtf = self.sbctx(stack, "gtf", (128, 128))
        tie = self.sbctx(stack, "tie", (128, 128))
        csr = self.sbctx(stack, "csr", (128, 128))
        hstage = self.sbctx(stack, "hstage", (128, 256), BF16)
        hgat = self.sbctx(stack, "hgat", (128, 16, 256), BF16)
        offrow = self.sbctx(stack, "offrow", (1, 128))
        orbc = self.sbctx(stack, "orbc", (128, 128))
        pqt = self.sbctx(stack, "pqt", (128, 16))
        kqt = self.sbctx(stack, "kqt", (128, 32))

        idxn = self.sbctx(stack, "idxn", (128, 1))

        @blk.vector
        def _(v):
            self.incw(v, v.memset(idx8[:], 0), "v")

        @blk.vector
        def _(v):
            self.w(v, "coll")
            self.incw(v, v.tensor_scalar(out=self.sccol[:, 0:1], in0=self.t_cid, scalar1=16.0,
                                     scalar2=None, op0=OP.mult), "v")
            self.incw(v, v.tensor_tensor(out=idxn[:], in0=self.t_r128s,
                                     in1=self.sccol[:, 0:1], op=OP.add), "v")

        self.wrap_build(128, idxn[:, :], idx8)

        @blk.gpsimd
        def _(g):
            self.w(g, "v")
            self.inc(g.dma_gather(
                out_ap=pc[:, :, :],
                in_ap=self.pT_all.rearrange("e (c j) -> (e c) j", j=128),
                idxs_ap=idx8[:, :], num_idxs=128, num_idxs_reg=128,
                elem_size=128), "d_g", 16)

        @blk.vector
        def _(v):
            self.w(v, "d_g")
            self.incw(v, v.memset(lo, 0.0), "v")
            self.incw(v, v.memset(hi, 1.0), "v")

        for it in range(NBIS):
            @blk.vector
            def _(v, it=it):
                self.w(v, "t")
                self.incw(v, v.tensor_tensor(out=self.sccol[:, 0:1], in0=lo, in1=hi, op=OP.add), "v")
                self.incw(v, v.tensor_scalar(out=self.sccol[:, 0:1], in0=self.sccol[:, 0:1],
                                         scalar1=0.5, scalar2=None, op0=OP.mult), "v")
                self.w(v, "v")
                self.incw(v, v.tensor_scalar(out=self.scu8[:, 0:128], in0=pc[:, 0, :],
                                         scalar1=self.sccol[:, 0:1], scalar2=0.0,
                                         op0=OP.is_gt, op1=OP.add,
                                         accum_out=cntc), "v")

            @blk.tensor
            def _(t, it=it):
                self.w(t, "v")
                self.inc(t.matmul(self.pM[:, 0:1], self.t_ones, cntc,
                                  start=True, stop=True), "t")

            @blk.vector
            def _(v, it=it):
                self.w(v, "t")
                self.incw(v, v.tensor_copy(cntc, self.pM[:, 0:1]), "v")
                self.incw(v, v.tensor_scalar(out=gec, in0=cntc, scalar1=float(KEXP),
                                         scalar2=None, op0=OP.is_ge), "v")
                self.incw(v, v.tensor_tensor(out=self.sccol[:, 2:3], in0=self.sccol[:, 0:1],
                                         in1=lo, op=OP.subtract), "v")
                self.w(v, "v")
                self.incw(v, v.scalar_tensor_tensor(out=lo, in0=self.sccol[:, 2:3],
                                                scalar=gec, in1=lo,
                                                op0=OP.mult, op1=OP.add), "v")
                self.incw(v, v.tensor_scalar(out=self.sccol[:, 3:4], in0=gec, scalar1=-1.0,
                                         scalar2=1.0, op0=OP.mult, op1=OP.add), "v")
                self.incw(v, v.tensor_tensor(out=self.sccol[:, 2:3], in0=self.sccol[:, 0:1],
                                         in1=hi, op=OP.subtract), "v")
                self.w(v, "v")
                self.incw(v, v.scalar_tensor_tensor(out=hi, in0=self.sccol[:, 2:3],
                                                scalar=self.sccol[:, 3:4], in1=hi,
                                                op0=OP.mult, op1=OP.add), "v")

        @blk.vector
        def _(v):
            self.w(v, "v")
            self.incw(v, v.tensor_scalar(out=gtf[:], in0=pc[:, 0, :], scalar1=hi,
                                     scalar2=0.0, op0=OP.is_gt, op1=OP.add,
                                     accum_out=cntc), "v")

        @blk.tensor
        def _(t):
            self.w(t, "v")
            self.inc(t.matmul(self.pM[:, 0:1], self.t_ones, cntc, start=True, stop=True), "t")

        @blk.vector
        def _(v):
            self.w(v, "t")
            self.incw(v, v.tensor_copy(cntc, self.pM[:, 0:1]), "v")
            self.incw(v, v.tensor_scalar(out=self.sccol[:, 2:3], in0=cntc, scalar1=-1.0,
                                     scalar2=float(KEXP), op0=OP.mult, op1=OP.add), "v")
            self.w(v, "v")
            self.incw(v, v.tensor_scalar(out=tie[:], in0=pc[:, 0, :], scalar1=lo,
                                     scalar2=None, op0=OP.is_gt), "v")
            self.incw(v, v.tensor_tensor(out=tie[:], in0=tie[:], in1=gtf[:], op=OP.subtract), "v")
            self.incw(v, v.tensor_tensor_scan(out=csr[:], data0=self.t_onesw[:, 0:128],
                                          data1=tie[:], initial=0.0,
                                          op0=OP.mult, op1=OP.add), "v")
            self.incw(v, v.tensor_copy(self.sccol[:, 0:1], csr[:, 127:128]), "v")

        @blk.tensor
        def _(t):
            self.w(t, "v")
            self.inc(t.matmul(self.pM[:, 0:1], self.t_upst, self.sccol[:, 0:1],
                              start=True, stop=True), "t")

        @blk.vector
        def _(v):
            self.incw(v, v.tensor_tensor_scan(out=csr[:], data0=self.t_onesw[:, 0:128],
                                          data1=gtf[:], initial=0.0,
                                          op0=OP.mult, op1=OP.add), "v")
            self.incw(v, v.tensor_copy(self.sccol[:, 0:1], csr[:, 127:128]), "v")

        @blk.tensor
        def _(t):
            self.w(t, "v")
            self.inc(t.matmul(self.pM[:, 0:1], self.t_upst, self.sccol[:, 0:1],
                              start=True, stop=True), "t")

        @blk.vector
        def _(v):
            self.w(v, "t")
            self.incw(v, v.tensor_copy(self.sccol[:, 1:2], self.pM[:, 0:1]), "v")
            self.incw(v, v.tensor_scalar(out=self.sccol[:, 2:3], in0=self.sccol[:, 0:1],
                                     scalar1=0.0, scalar2=None, op0=OP.is_equal), "v")
            self.incw(v, v.scalar_tensor_tensor(out=self.sccol[:, 2:3], in0=self.sccol[:, 2:3],
                                            scalar=1e9, in1=self.sccol[:, 1:2],
                                            op0=OP.mult, op1=OP.add), "v")
            self.incw(v, v.tensor_tensor(out=tie[:], in0=pc[:, 0, :], in1=gtf[:],
                                     op=OP.mult), "v")
            self.incw(v, v.tensor_copy(hstage[:, 0:128], csr[:]), "v")
            self.incw(v, v.tensor_copy(hstage[:, 128:256], tie[:]), "v")

        @blk.tensor
        def _(t):
            self.w(t, "v")
            self.inc(t.transpose(self.pM[0:1, 0:128], self.sccol[:, 2:3], self.t_identf), "t")

        @blk.vector
        def _(v):
            self.w(v, "t")
            self.incw(v, v.tensor_copy(offrow[0:1, 0:128], self.pM[0:1, 0:128]), "v")

        @blk.gpsimd
        def _(g):
            self.w(g, "v")
            g.partition_broadcast(orbc[:, :], offrow[0:1, 0:128])
            self.inc(g.nop(), "g")

        @blk.vector
        def _(v):
            self.w(v, "g")
            for qb in range(16):
                qv = self.t_iotacm[:, qb:qb + 1]
                orB = orbc[:, :]
                qvB = self.bcast_free(qv, 128)
                self.incw(v, v.tensor_tensor(out=csr[:], in0=orB, in1=qvB, op=OP.is_le), "v")
                self.incw(v, v.tensor_reduce(out=pqt[:, qb:qb + 1], in_=csr[:], axis=AX.X,
                                         op=OP.add), "v")
                self.incw(v, v.tensor_tensor(out=csr[:], in0=csr[:], in1=orB, op=OP.mult), "v")
                self.incw(v, v.tensor_reduce(out=kqt[:, qb:qb + 1], in_=csr[:], axis=AX.X,
                                         op=OP.max), "v")
            self.incw(v, v.tensor_scalar(out=pqt[:, 0:16], in0=pqt[:, 0:16], scalar1=-1.0,
                                     scalar2=None, op0=OP.add), "v")
            self.incw(v, v.tensor_tensor(out=kqt[:, 0:16], in0=self.t_iotacm[:, 0:16],
                                     in1=kqt[:, 0:16], op=OP.subtract), "v")
            self.incw(v, v.tensor_scalar(out=kqt[:, 16:32], in0=kqt[:, 0:16], scalar1=1.0,
                                     scalar2=None, op0=OP.add), "v")

        @blk.sync
        def _(q):
            self.w(q, "v")
            self.inc(q.dma_start(out=self.h9_hbm[:, :], in_=hstage[:, 0:256]), "d_tx", 16)

        self.wrap_build(KEXP, pqt[:, 0:16], self.wrapC)

        @blk.gpsimd
        def _(g):
            self.w(g, "v")
            self.w(g, "d_tx")
            self.inc(g.dma_gather(out_ap=hgat[:, :, :], in_ap=self.h9_hbm[:, :],
                                  idxs_ap=self.wrapC[:, :], num_idxs=KEXP,
                                  num_idxs_reg=KEXP, elem_size=256,
                                  single_packet=False), "d_g", 16)

        @blk.vector
        def _(v):
            self.w(v, "d_g")
            for qb in range(16):
                self.incw(v, v.tensor_scalar(out=csr[:], in0=hgat[:, qb, 0:128],
                                         scalar1=kqt[:, qb:qb + 1], scalar2=0.0,
                                         op0=OP.is_le, op1=OP.add,
                                         accum_out=self.sccol[:, 3:4]), "v")
                self.incw(v, v.scalar_tensor_tensor(out=self.tid[:, qb:qb + 1],
                                                in0=pqt[:, qb:qb + 1], scalar=128.0,
                                                in1=self.sccol[:, 3:4],
                                                op0=OP.mult, op1=OP.add), "v")
                self.incw(v, v.tensor_scalar(out=csr[:], in0=hgat[:, qb, 0:128],
                                         scalar1=kqt[:, 16 + qb:17 + qb], scalar2=None,
                                         op0=OP.is_equal), "v")
                self.incw(v, v.tensor_tensor_reduce(out=gtf[:], in0=csr[:],
                                                in1=hgat[:, qb, 128:256], scale=1.0,
                                                scalar=0.0, op0=OP.mult, op1=OP.add,
                                                accum_out=self.tval[:, qb:qb + 1]), "v")

        self.wrap_build(KEXP, self.tid[:, :], self.wrapC)

        if self.dbg("omout") or self.dbg("dbg9"):
            dst = self.dbg_out("tid", (128, 16))
            dst2 = self.dbg_out("tval", (128, 16))

            @blk.sync
            def _(q):
                self.w(q, "v")
                self.inc(q.dma_start(out=dst[:, :], in_=self.tid[:, :]), "d_dbg", 16)
                self.inc(q.dma_start(out=dst2[:, :], in_=self.tval[:, :]), "d_dbg", 16)

    def p10_ffn(self, stack):
        blk = self.block
        w2t = self.sbctx(stack, "w2t", (128, 32, D), BF16)
        hT = self.sbctx(stack, "hT", (128, 32, TBF), BF16)
        xg = [self.sbctx(stack, f"xg{i}", (128, 4, D), BF16) for i in range(2)]
        xinT = self.sbctx(stack, "xinT", (128, 8, TBF), BF16)
        w1t = [self.sbctx(stack, f"w1t{i}", (128, 8, 128), BF16) for i in range(2)]
        ob = [self.sbctx(stack, f"ob{i}", (128, D), BF16) for i in range(2)]
        pOD = [self.pO, self.pD]

        tok_xg = {}
        tok_tr = {}
        tok_ptv = {}
        tok_w1mm = {}
        tok_w1 = {}
        tok_gelu = {}
        tok_w2mm = {}
        tok_obe = {}

        @blk.gpsimd
        def _(g):
            self.w(g, "v")
            self.w(g, "coll2")
            tok_xg[0] = self.inc(g.dma_gather(
                out_ap=xg[0][:, :, :], in_ap=self.yn_all[:, :],
                idxs_ap=self.wrapC[:, 0:32], num_idxs=TBF, num_idxs_reg=TBF,
                elem_size=D, single_packet=False), "d_g0", 16)
            self.inc(g.dma_start(out=w2t[:],
                                 in_=self.W2_in.rearrange("(c p) o -> p c o", p=128)),
                     "d_w2", 16)

        w2_tok = self.cnt["d_w2"]

        for tb in range(NTB):
            for sb4 in range(4):
                pt = self.pT1 if sb4 % 2 == 0 else self.pT2

                @blk.tensor
                def _(t, tb=tb, sb4=sb4, pt=pt):
                    self.w(t, f"d_g{tb % 2}", tok_xg[tb])
                    self.w(t, "v", tok_ptv.get((tb, sb4 - 2),
                                               tok_ptv.get((tb - 1, sb4 + 2), 0)))
                    for dc in range(8):
                        self.inc(t.transpose(pt[:, dc * 128:(dc + 1) * 128],
                                             xg[tb % 2][:, sb4, dc * 128:(dc + 1) * 128],
                                             self.t_identb), "t")
                    tok_tr[tb] = self.cnt["t"]

                @blk.vector
                def _(v, tb=tb, sb4=sb4, pt=pt):
                    self.w(v, "t", tok_tr[tb])
                    if tb > 0 and sb4 == 0:
                        self.w(v, "t", tok_w1mm[(tb - 1) * 32 + 31])
                    for dc in range(8):
                        self.inc(v.tensor_copy(xinT[:, dc, sb4 * 128:(sb4 + 1) * 128],
                                               pt[:, dc * 128:(dc + 1) * 128]), "v")
                    tok_ptv[(tb, sb4)] = self.cnt["v"]

            if tb + 1 < NTB:
                @blk.gpsimd
                def _(g, tb=tb):
                    if tb >= 1:
                        self.w(g, "t", tok_tr[tb - 1])
                    tok_xg[tb + 1] = self.inc(g.dma_gather(
                        out_ap=xg[(tb + 1) % 2][:, :, :], in_ap=self.yn_all[:, :],
                        idxs_ap=self.wrapC[:, 32 * (tb + 1):32 * (tb + 1) + 32],
                        num_idxs=TBF, num_idxs_reg=TBF, elem_size=D,
                        single_packet=False), f"d_g{(tb + 1) % 2}", 16)

            if tb == 0:
                @blk.gpsimd
                def _(g):
                    self.w(g, "d_w1b0")
                    self.w(g, "d_w1b1")
                    tok_w1[0] = self.inc(g.dma_start(
                        out=w1t[0][:, :, :],
                        in_=self.w1bf_hbm[0:D, :].rearrange("(c p) k -> p c k", p=128)),
                        "d_w1_0", 16)

            for fb in range(32):
                fg = tb * 32 + fb
                if fg + 1 < NTB * 32:
                    @blk.gpsimd
                    def _(g, tb=tb, fb=fb, fg=fg):
                        if fg - 1 >= 0:
                            self.w(g, "t", tok_w1mm[fg - 1])
                        tok_w1[fg + 1] = self.inc(g.dma_start(
                            out=w1t[(fg + 1) % 2][:, :, :],
                            in_=self.w1bf_hbm[((fg + 1) % 32) * D:((fg + 1) % 32 + 1) * D,
                                              :].rearrange("(c p) k -> p c k", p=128)),
                            f"d_w1_{(fg + 1) % 2}", 16)

                @blk.tensor
                def _(t, tb=tb, fb=fb, fg=fg):
                    self.w(t, f"d_w1_{fg % 2}", tok_w1[fg])
                    self.w(t, "v", tok_ptv[(tb, 3)])
                    self.w(t, "s", tok_gelu.get((tb, fb - 2),
                                                tok_gelu.get((tb - 1, fb + 30), 0)))
                    for dc in range(8):
                        self.inc(t.matmul(self.pW[fb % 2][:, 0:TBF],
                                          w1t[fg % 2][:, dc, :],
                                          xinT[:, dc, :],
                                          start=(dc == 0), stop=(dc == 7)), "t")
                    tok_w1mm[fg] = self.cnt["t"]

                @blk.scalar
                def _(s, tb=tb, fb=fb, fg=fg):
                    self.w(s, "t", tok_w1mm[fg])
                    if tb > 0 and fb == 0:
                        self.w(s, "t", tok_w2mm[(tb - 1, 3)])
                    self.inc(s.activation(out=hT[:, fb, :], in_=self.pW[fb % 2][:, 0:TBF],
                                          func=AF.Gelu_apprx_tanh,
                                          bias=self.t_b1c[:, fb:fb + 1],
                                          scale=self.cc1("one")), "s")
                    tok_gelu[(tb, fb)] = self.cnt["s"]

            for obi in range(TBF // 128):
                qi = tb * 4 + obi

                @blk.tensor
                def _(t, tb=tb, obi=obi):
                    self.w(t, "s", tok_gelu[(tb, 31)])
                    if tb == 0 and obi == 0:
                        self.w(t, "d_w2", w2_tok)
                    prev = tok_obe.get((tb, obi - 1), tok_obe.get((tb - 1, 3), 0))
                    self.w(t, "v", prev)
                    for half in range(2):
                        for fb in range(32):
                            self.inc(t.matmul(pOD[half][:, 0:512],
                                              hT[:, fb, obi * 128:(obi + 1) * 128],
                                              w2t[:, fb, half * 512:(half + 1) * 512],
                                              start=(fb == 0), stop=(fb == 31)), "t")
                    tok_w2mm[(tb, obi)] = self.cnt["t"]

                @blk.vector
                def _(v, tb=tb, obi=obi, qi=qi):
                    self.w(v, "t", tok_w2mm[(tb, obi)])
                    self.w(v, f"d_ob{qi % 2}")
                    for half in range(2):
                        sl = slice(half * 512, (half + 1) * 512)
                        self.inc(v.tensor_tensor(out=self.sc1[:, sl],
                                                 in0=pOD[half][:, 0:512],
                                                 in1=self.t_b2bc[:, sl], op=OP.add), "v")
                    self.inc(v.tensor_scalar(out=ob[qi % 2][:], in0=self.sc1[:, 0:D],
                                             scalar1=self.tval[:, qi:qi + 1],
                                             scalar2=None, op0=OP.mult), "v")
                    tok_obe[(tb, obi)] = self.cnt["v"]

                @blk.gpsimd
                def _(g, tb=tb, obi=obi, qi=qi):
                    self.w(g, "v", tok_obe[(tb, obi)])
                    if qi == 0:
                        self.w(g, "d_z2")
                    self.inc(g.dma_scatter_add(
                        out_ap=self.dense_hbm[:, :],
                        in_ap=ob[qi % 2].rearrange("p (a d) -> p a d", a=1),
                        idxs_ap=self.wrapC[:, 8 * qi:8 * qi + 8], num_idxs=128,
                        num_idxs_reg=128, elem_size=D, single_packet=False),
                        f"d_ob{qi % 2}", 16)

    def p11_collect(self):
        blk = self.block

        @blk.gpsimd
        def _(g):
            self.w(g, "d_ob0")
            self.w(g, "d_ob1")
            self.inc(g.collective_compute(
                "ReduceScatter", OP.add, replica_groups=[list(range(8))],
                ins=[self.dense_hbm[:, :]], outs=[self.dense_rs[:, :]]), "coll3")

    def p12_combine(self, stack):
        blk = self.block
        ab = [self.sbctx(stack, f"adb{i}", (128, D), BF16) for i in range(2)]
        yb = [self.sbctx(stack, f"ydb{i}", (128, D)) for i in range(2)]
        rb = [self.sbctx(stack, f"rdb{i}", (128, D)) for i in range(2)]
        tok_ld = {}
        tok_add = {}
        tok_st = {}

        @blk.sync
        def _(q):
            self.w(q, "coll3")
            self.w(q, "d_y0")
            self.w(q, "d_y1")

        for j in range(NT):
            @blk.sync
            def _(q, j=j):
                if j >= 2:
                    self.w(q, "v", tok_add[j - 2])
                self.inc(q.dma_start(
                    out=ab[j % 2][:],
                    in_=self.dense_rs.rearrange("(c p) d -> p c d", p=128)[:, j, :]),
                    f"d_x{j % 2}", 16)
                tok_ld[j] = self.inc(q.dma_start(
                    out=yb[j % 2][:],
                    in_=self.out.rearrange("(c p) d -> p c d", p=128)[:, j, :]),
                    f"d_x{j % 2}", 16)

            @blk.vector
            def _(v, j=j):
                self.w(v, f"d_x{j % 2}", tok_ld[j])
                if j >= 2:
                    self.w(v, f"d_cb{j % 2}", tok_st[j - 2])
                self.inc(v.tensor_tensor(out=rb[j % 2][:], in0=yb[j % 2][:],
                                         in1=ab[j % 2][:], op=OP.add), "v")
                tok_add[j] = self.cnt["v"]

            @blk.scalar
            def _(s, j=j):
                self.w(s, "v", tok_add[j])
                tok_st[j] = self.inc(s.dma_start(
                    out=self.out.rearrange("(c p) d -> p c d", p=128)[:, j, :],
                    in_=rb[j % 2][:]), f"d_cb{j % 2}", 16)

    def finale(self):
        blk = self.block

        @blk.gpsimd
        def _(g):
            for name in list(self._sems):
                if (name.startswith("d_") or name.startswith("coll")
                        or name in ("v", "s", "t", "g")):
                    self.w(g, name)

    def build(self):
        from contextlib import ExitStack
        self.declare()
        with ExitStack() as top:
            self.block = top.enter_context(self.nc.Block())
            for nm in ("d_y0", "d_y1", "d_yn0", "d_yn1", "d_ob0", "d_ob1",
                       "d_om", "d_sh", "d_g", "d_wm", "coll"):
                self.sem(nm)
            self.sem("t")
            self.sem("v")
            self.sem("s")
            self.sem("g")
            self.p0_setup()
            if self.upto >= 1:
                with ExitStack() as stA:
                    self.xnT = self.sbctx(stA, "xnT", (128, 8, S), BF16)
                    self.p1_ln_outer(stA)
                    if self.upto >= 6:
                        with ExitStack() as st67:
                            self.og = self.sbctx(st67, "og", (128, NT, DV), BF16)
                            with ExitStack() as st:
                                self.p6_reco(st)
                            if self.upto >= 7:
                                with ExitStack() as st:
                                    self.p7_out(st)
            if self.upto >= 8:
                self.p8_collect()
            if self.upto >= 9:
                self.phase_barrier()
                with ExitStack() as st:
                    self.p9_route(st)
            if self.upto >= 10:
                self.phase_barrier()
                with ExitStack() as st:
                    self.p10_ffn(st)
            if self.upto >= 11:
                self.p11_collect()
            if self.upto >= 12:
                self.phase_barrier()
                with ExitStack() as st:
                    self.p12_combine(st)
            self.finale()
        lower_extended_insts(self.nc)
        return self.nc

    def p1_ln_outer(self, st_outer):
        from contextlib import ExitStack
        with ExitStack() as st:
            self.p1_ln(st)
            if self.upto >= 2:
                self.p2_gate(st)
        if self.upto >= 4:
            with ExitStack() as st:
                self.p_gla(st)

    def p0_setup(self):
        blk = self.block

        @blk.sync
        def _(q):
            for dst, src in [
                (self.t_cc, self.cc_in), (self.t_identf, self.identf_in),
                (self.t_identb, self.identb_in), (self.t_triu, self.triu_in),
                (self.t_upst, self.upst_in), (self.t_ones, self.onesm_in),
                (self.t_iota4, self.iota4_in), (self.t_iotatg, self.iotatg_in),
                (self.t_iotacm, self.iotacm_in), (self.t_r128s, self.r128s_in),
                (self.t_iotarows, self.iotarows_in), (self.t_onesw, self.onesw_in),
                (self.t_normw, self.normw_in), (self.t_bsbc, self.bs_in),
                (self.t_b2bc, self.b2_in), (self.t_b1c, self.b1_in),
                (self.t_bgk, self.bgk_in), (self.t_bgks, self.bgks_in),
                (self.t_cid1, self.cid_in),
            ]:
                self.inc(q.dma_start(out=dst[:], in_=src[:, :]), "d_c", 16)

        @blk.vector
        def _(v):
            self.w(v, "d_c")
            self.inc(v.memset(self.t_zbf[:], 0.0), "v")
            for wt in (self.wrap0, self.wrap1, self.wrapC):
                self.inc(v.memset(wt[:], 0), "v")

        @blk.scalar
        def _(s):
            self.w(s, "d_c")

        @blk.tensor
        def _(t):
            self.w(t, "d_c")

        @blk.gpsimd
        def _(g):
            g.load_library(MLP_LIB)
            self.inc(g.dma_start(out=self.wgate[:],
                                 in_=self.w_gate_in.rearrange("(c p) o -> p c o", p=128)),
                     "d_w", 16)
            self.inc(g.dma_start(out=self.wq[:],
                                 in_=self.w_q_in.rearrange("(c p) o -> p c o", p=128)),
                     "d_w", 16)
            self.w(g, "d_c")
            g.partition_broadcast(self.t_cid[:], self.t_cid1[0:1, 0:1])
            self.inc(g.nop(), "g")
            self.w(g, "v")
            self.inc(g.dma_start(
                out=self.tx_hbm[0:NMEM * CAP, :].rearrange("(c p) d -> p c d", p=128),
                in_=self.bcast_mid(self.t_zbf[:], NMEM * CAP // 128)), "d_z", 16)
            self.inc(g.dma_start(out=self.tx_hbm[NMEM * CAP:NMEM * CAP + 8, :],
                                 in_=self.t_zbf[0:8, :]), "d_z", 16)
            self.inc(g.dma_start(out=self.om_hbm[NMEM * CAP:NMEM * CAP + 8, :],
                                 in_=self.t_zbf[0:8, :]), "d_z", 16)

        @blk.scalar
        def _(s):
            self.w(s, "v")
            for i in range(8):
                out3 = self.dense_hbm[i * 2048:(i + 1) * 2048, :].rearrange(
                    "(c p) d -> p c d", p=128)
                self.inc(s.dma_start(out=out3,
                                     in_=self.bcast_mid(self.t_zbf[:], 16)),
                         "d_z2", 16)


    def p0b_precast_w1(self):
        blk = self.block

        @blk.gpsimd
        def _(g):
            atok = {0: 0, 1: 0}
            btok = {0: 0, 1: 0}
            i = 0
            for dc in range(8):
                for ch in range(4):
                    b = i % 2
                    half = self.t_w1s[:, b * 1024:(b + 1) * 1024]
                    if btok[b]:
                        self.w(g, f"d_w1b{b}", btok[b])
                    atok[b] = self.inc(g.dma_start(
                        out=half,
                        in_=self.W1_in[dc * 128:(dc + 1) * 128,
                                       ch * 1024:(ch + 1) * 1024]), f"d_w1a{b}", 16)
                    self.w(g, f"d_w1a{b}", atok[b])
                    btok[b] = self.inc(g.dma_start(
                        out=self.w1bf_hbm.rearrange("(f c p) k -> c p f k", c=8, p=128)[
                            dc][:, ch * 8:(ch + 1) * 8, :],
                        in_=half.rearrange("p (f k) -> p f k", k=128)), f"d_w1b{b}", 16)
                    i += 1
                    self.w(g, "d_w1b")


_NC = None


def kernel(**inputs):
    global _NC
    in_maps = host_inputs(inputs)
    if _NC is None:
        _NC = MK(upto=12).build()
    res = run_bass_kernel_spmd(_NC, in_maps, core_ids=list(range(8)))
    out = np.stack([np.asarray(r["out"], np.float32) for r in res.results], 0)
    return out.reshape(8, 2048, 1024)
